# version 03
# baseline (speedup 1.0000x reference)
import numpy as np
import contextlib
import sys
sys.path.insert(0, '/opt/trn_rl_repo')
import concourse.bass as bass
import concourse.mybir as mybir
from concourse.bass_core_rust import IndirectOffsetOnAxis

F32 = mybir.dt.float32
BF16 = mybir.dt.bfloat16
I32 = mybir.dt.int32
AF = mybir.ActivationFunctionType
ALU = mybir.AluOpType
AX = mybir.AxisListType

ENGINES = ("sync", "tensor", "vector", "scalar", "gpsimd")

D = 1024
T = 4096
NCH = 32
KC = 8
NE = 8
CAP1 = 1024
CAPREF = 2560
H = 16
DH = 64
NSC = 8
EXPC = 18.0


class P:
    def __init__(self, nc):
        self.nc = nc
        self.q = {e: [] for e in ENGINES}
        self.sems = {}
        self.cnt = {}
        self.virt = {}
        self.lastw = {}
        self._ctx = contextlib.ExitStack()

    def sem(self, name):
        if name not in self.sems:
            self.sems[name] = self._ctx.enter_context(self.nc.semaphore(name))
            self.cnt[name] = 0
        return self.sems[name]

    CHAINED = ("vector", "scalar", "gpsimd")

    def op(self, eng, fn, sem=None, n=1, _dma=False):
        inc = 16 if _dma else 1
        if eng in self.CHAINED:
            chname = "chain_" + eng + ("d" if (_dma and eng == "gpsimd") else "")
            other = None
            if eng == "gpsimd":
                other = "chain_gpsimdd" if not _dma else "chain_gpsimd"
            assert not (_dma and eng in ("vector", "scalar")), "no DMAs on vector/scalar"
            ch = self.sem(chname)
            cv = self.cnt[chname]
            self.cnt[chname] = cv + inc
            if sem is not None:
                reg = self.virt.setdefault(sem, {"ch": chname, "marks": []})
                assert reg["ch"] == chname, f"sem {sem} has mixed producers"
                units = (reg["marks"][-1][0] if reg["marks"] else 0) + n
                reg["marks"].append((units, self.cnt[chname]))
            if other is not None and self.cnt.get(other, 0) > self.lastw.get((eng, other), 0):
                oc = self.sem(other)
                ov = self.cnt[other]
                self.lastw[(eng, other)] = ov
                self.q[eng].append(lambda e, s=oc, v=ov: e.wait_ge(s, v))
            def run(e, fn=fn, ch=ch, cv=cv, inc=inc):
                inst = fn(e)
                if cv > 0:
                    inst = inst._wait_ge(ch, cv)
                return inst.then_inc(ch, inc)
            self.q[eng].append(run)
            return None
        if sem is not None:
            assert sem not in self.virt, f"sem {sem} already virtual"
            s = self.sem(sem)
            self.cnt[sem] += n
            self.q[eng].append(lambda e, fn=fn, s=s, n=n: fn(e).then_inc(s, n))
            return self.cnt[sem]
        self.q[eng].append(lambda e, fn=fn: fn(e))
        return None

    def dma(self, eng, fn, sem=None):
        return self.op(eng, fn, sem, 16, _dma=True)

    def base(self, sem):
        if sem in self.virt:
            marks = self.virt[sem]["marks"]
            return marks[-1][0] if marks else 0
        return self.cnt.get(sem, 0)

    def wait(self, eng, sem, val=None):
        if sem in self.virt:
            reg = self.virt[sem]
            marks = reg["marks"]
            if not marks:
                return
            if val is None:
                tgt = marks[-1][1]
            else:
                tgt = None
                for units, chaincum in marks:
                    if units >= val:
                        tgt = chaincum
                        break
                if tgt is None:
                    tgt = marks[-1][1]
            chname = reg["ch"]
            producer_eng = chname.replace("chain_", "").rstrip("d") if chname.endswith("dd") is False else ""
            producer_eng = chname[6:-1] if chname.endswith("d") and chname != "chain_gpsimd" else chname[6:]
            if chname == "chain_gpsimdd":
                producer_eng = "gpsimd"
            ch = self.sem(chname)
            if eng == producer_eng and chname in ("chain_vector", "chain_scalar", "chain_gpsimd"):
                return
            self.q[eng].append(lambda e, s=ch, v=tgt: e.wait_ge(s, v))
            return
        s = self.sem(sem)
        v = self.cnt[sem] if val is None else val
        if v > 0:
            self.q[eng].append(lambda e, s=s, v=v: e.wait_ge(s, v))

    def set_barrier_tiles(self, tiles):
        self._btiles = tiles

    def barrier(self, engines=ENGINES, name="bar"):
        s = self.sem(name)
        tgt = self.cnt[name] + len(engines)
        self.cnt[name] = tgt
        for eng in engines:
            t = self._btiles.get(eng)
            if eng in ("sync", "tensor") or t is None:
                self.q[eng].append(lambda e, s=s: e.nop().then_inc(s, 1))
            else:
                if eng == "scalar":
                    self.op(eng, lambda e, t=t: e.activation(t, t, AF.Identity))
                else:
                    self.op(eng, lambda e, t=t: e.tensor_copy(t, t))
                chname = "chain_" + eng
                ch = self.sem(chname)
                cv = self.cnt[chname]
                self.q[eng].append(lambda e, ch=ch, cv=cv: e.wait_ge(ch, cv))
                self.q[eng].append(lambda e, s=s: e.sem_inc(s, 1))
        for eng in engines:
            self.q[eng].append(lambda e, s=s, v=tgt: e.wait_ge(s, v))

    def build(self):
        with self.nc.Block() as block:
            for eng in ENGINES:
                qe = self.q[eng]
                def body(handle, qe=qe):
                    for fn in qe:
                        fn(handle)
                getattr(block, eng)(body)


class MK:
    def __init__(self, stages, out_half=False, emit_counts=False):
        self.stages = stages
        self.out_half = out_half
        self.emit_counts = emit_counts
        nc = bass.Bass("TRN2", target_bir_lowering=False, debug=False, dynamic_dma_scratch_size=8192)
        self.nc = nc
        self.p = P(nc)
        ctx = self.p._ctx
        self.ctx = ctx
        sbuf = lambda *a: ctx.enter_context(nc.sbuf_tensor(*a))
        psum = lambda *a: ctx.enter_context(nc.psum_tensor(*a))

        ap = {}
        def din(nm, shape, dt=F32):
            ap[nm] = nc.dram_tensor(nm, shape, dt, kind="ExternalInput").ap()
        self.ap = ap

        st = set(stages)
        self.need_l0ffn = bool(st & {"l0ffn_a", "l0ffn_b"})
        self.need_l0attn = "l0attn" in st
        self.need_l1attn = "l1attn" in st
        self.need_l1ffn = bool(st & {"l1ffn_a", "l1ffn_b"})
        self.need_gate = self.need_l1ffn or emit_counts

        din("x0", [T, D])
        if self.need_l0ffn:
            din("l0r_wT", [D, NE]); din("l0r_b", [1, NE])
            din("l0e_wT", [NE, D, D]); din("l0e_b", [NE, D])
        if self.need_l0attn:
            din("l0a_wq", [D, D]); din("l0a_wk", [D, D])
            din("l0a_wv", [D, D]); din("l0a_wo", [D, D])
        if self.need_l1attn:
            din("l1a_wqkvT", [D, 3 * D]); din("l1a_wprojT", [D, D])
        if self.need_gate:
            din("l1f_wg", [D, NE])
        if self.need_l1ffn:
            din("l1f_w1", [NE, D, 4 * D]); din("l1f_b1T", [NE, 128, 32])
            din("l1f_w2", [NE, 4 * D, D]); din("l1f_b2", [NE, D])
            din("priors", [4, NE]); din("selprev", [4, 1])
        self.outs = {}
        rows = T // 2 if out_half else T
        self.outs["xout"] = nc.dram_tensor("xout", [rows, D], F32, kind="ExternalOutput").ap()
        if emit_counts:
            self.outs["counts"] = nc.dram_tensor("counts", [1, NE], F32, kind="ExternalOutput").ap()
        self.halfsel = None
        if out_half:
            din("halfsel", [1, 1], F32)

        dram = lambda nm, shape, dt: nc.dram_tensor(nm, shape, dt, kind="Internal").ap()
        self.xhat_d = dram("xhat_d", [T, D], BF16)
        if self.need_l0ffn:
            self.l0e_bf = dram("l0e_bf", [NE, D, D], BF16)
        if self.need_l0attn or self.need_l1attn:
            self.qfm_d = dram("qfm_d", [D, T], BF16)
            self.kfm_d = dram("kfm_d", [D, T], BF16)
            self.vtm_d = dram("vtm_d", [T, D], BF16)
        if self.need_l0attn:
            self.ktm_d = dram("ktm_d", [T, D], BF16)
            self.l0a_bf = dram("l0a_bf", [4, D, D], BF16)
        if self.need_l1attn:
            self.l1aq_bf = dram("l1aq_bf", [D, 3 * D], BF16)
            self.l1ap_bf = dram("l1ap_bf", [D, D], BF16)
        if self.need_l1ffn:
            self.w1_bf = dram("w1_bf", [NE, D, 4 * D], BF16)
            self.w2_bf = dram("w2_bf", [NE, 4 * D, D], BF16)
            self.b2_bf = dram("b2_bf", [NE, D], BF16)
            self.idx_d = dram("idx_d", [NE * CAP1, 1], I32)
            self.eo_d = dram("eo_d", [NE * CAP1, D], BF16)
            self.ddisp_d = dram("ddisp_d", [CAP1, D], BF16)

        it = nc.inline_tensor
        self.c_ident = it(np.eye(128, dtype=np.float32), "c_ident")
        self.c_ones_row = it(np.ones((1, 128), dtype=np.float32), "c_ones_row")
        self.c_ones_col = it(np.ones((128, 1), dtype=np.float32), "c_ones_col")
        self.c_ustrict = it(np.triu(np.ones((128, 128), dtype=np.float32), 1), "c_ustrict")
        self.c_iota8 = it(np.tile(np.arange(NE, dtype=np.float32), (128, 1)), "c_iota8")
        self.c_iotam = it(np.tile(np.arange(NE, dtype=np.float32) - 999.0, (128, 1)), "c_iotam")
        self.c_tokid = it(np.arange(T, dtype=np.int32).reshape(NCH, 128).T.copy(), "c_tokid")
        self.c_senti = it(np.full((NE * CAP1, 1), 1 << 29, dtype=np.int32), "c_senti")

        self.XT = sbuf("XT", [128, NCH, D], F32)
        self.FM = sbuf("FM", [128, KC, T], BF16)
        import os as _os
        _wb = int(_os.environ.get("DEVK_WB", "512"))
        self.wbuf = sbuf("wbuf", [128, KC, _wb], BF16)
        self._xtmb_ded = sbuf("xtmb_d", [128, 2, D], BF16)
        self.lncol = sbuf("lncol", [128, 6, NCH], F32)
        self.gaux = sbuf("gaux", [128, 4, NCH], F32)
        self.rw8 = sbuf("rw8", [128, NCH, NE], F32)
        self.rwT1 = sbuf("rwT1", [8, 128], BF16)
        self.rwb1 = sbuf("rwb1", [128, NE], BF16)
        self.bexp_b = sbuf("bexp_b", [NE, D], BF16)
        self.oh_f = sbuf("oh_f", [128, NE], F32)
        self.scr8 = sbuf("scr8", [128, NE], F32)
        self.gsc = sbuf("gsc", [128, NCH], F32)
        self.slot_f = sbuf("slot_f", [128, NCH], F32)
        self.slot_i = sbuf("slot_i", [128, NCH], I32)
        self.slot2_i = sbuf("slot2_i", [128, NCH], I32)
        self.idxc64 = sbuf("idxc64", [128, 64], I32)
        self.idx1 = sbuf("idx1", [128, 1], I32)
        self.poscol = sbuf("poscol", [128, 4], F32)
        self.posi = sbuf("posi", [128, 1], I32)
        self.carry = sbuf("carry", [1, NE], F32)
        self.offrow = sbuf("offrow", [1, NE], F32)
        self.offb = sbuf("offb", [128, NE], F32)
        self.selprev_s = sbuf("selprev_s", [4, 1], F32)
        self.priors_s = sbuf("priors_s", [4, NE], F32)
        self.epscol = sbuf("epscol", [128, 1], F32)
        self.negC = sbuf("negC", [128, 1], F32)
        self.onesr_b = sbuf("onesr_b", [1, 128], BF16)
        self.onesc_b = sbuf("onesc_b", [128, 1], BF16)
        self.onesr_f = sbuf("onesr_f", [1, 128], F32)
        self.onesc_f = sbuf("onesc_f", [128, 1], F32)
        self.ident_b = sbuf("ident_b", [128, 128], BF16)
        self.ustr_b = sbuf("ustr_b", [128, 128], BF16)
        self.iota8_f = sbuf("iota8_f", [128, NE], F32)
        self.big8_f = sbuf("big8_f", [128, NE], F32)
        self.iotam_f = sbuf("iotam_f", [128, NE], F32)
        self.tokid_s = sbuf("tokid_s", [128, NCH], I32)
        self.smallb = sbuf("smallb", [128, KC, NE], BF16)
        self.wgb = sbuf("wgb", [128, KC, NE], BF16)
        self.l0rb_b = sbuf("l0rb_b", [1, NE], BF16)
        self.b1c = sbuf("b1c", [128, 32], F32)
        self.b2sm = sbuf("b2sm", [1, 512], BF16)
        self.idxc = sbuf("idxc", [128, 8], I32)
        self.cnt_s = sbuf("cnt_s", [1, NE], F32)
        self.btile = sbuf("btile", [128, 4], F32)
        self.p.set_barrier_tiles({
            "vector": self.btile[0:1, 0:1],
            "scalar": self.btile[0:1, 1:2],
            "gpsimd": self.btile[0:1, 2:3],
        })

        self.PS = [psum(f"ps{i}", [128, 512], F32) for i in range(4)]
        self.PSB = psum("psb", [128, 1024], F32)
        self.PT = psum("pt", [128, 128], BF16)
        self.PSM = psum("psm", [128, 16], F32)

        if self._xtmb_ded is not None:
            self.xtmb = self._xtmb_ded[:]
        else:
            self.xtmb = self.wbuf[:].rearrange("p k n -> p (k n)")[:, 0:2048].rearrange(
                "p (b n) -> p b n", b=2)
        self.gl = self.PSB[:, 0:256].rearrange("p (n e) -> p n e", e=NE)
        fmf = self.FM[:].rearrange("p k t -> p (k t)")
        self.fmf = fmf
        def fv(off, n, spec=None, **kw):
            v = fmf[:, off:off + n]
            if spec:
                v = v.rearrange(spec, **kw)
            return v
        self.v_qf = fv(0, 4096, "p (k t) -> p k t", t=512)
        self.v_kf = fv(4096, 8192, "p (b k t) -> p b k t", b=2, t=512)
        self.v_vt = fv(12288, 8192, "p (b a o) -> p b a o", b=2, o=D)
        self.v_kt = fv(20480, 4096, "p (a o) -> p a o", o=D)
        self.v_af = fv(24576, 4096, "p (k t) -> p k t", t=512)
        self.v_at = fv(28672, 1024, "p (b t) -> p b t", t=512)
        self.v_ysb = fv(29696, 512)
        self.v_rrb = fv(30208, 512)
        self.statef = fv(30720, 2048).bitcast(F32).rearrange("p (k d) -> p k d", d=DH)
        self.stateb = fv(29696, 512).rearrange("p (k d) -> p k d", d=DH)
        self.fv = fv
        self.v_vt65 = fv(12288, 2080, "p (b a h w) -> p b a h w", b=2, a=4, h=4)
        self.v_dfm = fv(0, 8192, "p (k t) -> p k t", t=1024)
        self.v_hfm = fv(8192, 16384, "p (k t) -> p k t", t=512)
        self.v_gath = fv(24576, 2048, "p (b t) -> p b t", t=1024)
        self.v_eob = fv(26624, 1024, "p (b t) -> p b t", t=512)

    def mm(self, out, lhsT, rhs, start=True, stop=True, sem=None):
        return self.p.op("tensor", lambda e: e.matmul(out, lhsT, rhs, start=start, stop=stop), sem)

    def emit_prep(self):
        p = self.p
        def _mkregs(e):
            self.bregs = {"tok": e.to_reg(T - 1), "slot": e.to_reg(NE * CAP1 - 1),
                          "half": e.to_reg(T // 2 - 1)}
        p.q["gpsimd"].append(_mkregs)
        p.dma("sync", lambda e: e.dma_start(out=self.onesr_f[:], in_=self.c_ones_row.ap()), "ld0")
        p.dma("sync", lambda e: e.dma_start(out=self.onesc_f[:], in_=self.c_ones_col.ap()), "ld0")
        p.dma("gpsimd", lambda e: e.dma_start(out=self.ustr_b[:], in_=self.c_ustrict.ap()), "ld0g")
        p.dma("sync", lambda e: e.dma_start(out=self.iota8_f[:], in_=self.c_iota8.ap()), "ld0")
        p.dma("sync", lambda e: e.dma_start(out=self.iotam_f[:], in_=self.c_iotam.ap()), "ld0")
        p.dma("sync", lambda e: e.dma_start(out=self.tokid_s[:], in_=self.c_tokid.ap()), "ld0")
        p.dma("gpsimd", lambda e: e.dma_start(out=self.onesr_b[:], in_=self.c_ones_row.ap()), "ld0g")
        p.dma("gpsimd", lambda e: e.dma_start(out=self.onesc_b[:], in_=self.c_ones_col.ap()), "ld0g")
        p.dma("gpsimd", lambda e: e.dma_start(out=self.ident_b[:], in_=self.c_ident.ap()), "ld0g")
        p.op("vector", lambda e: e.memset(self.btile[:, 0:1], 0.0))
        p.op("scalar", lambda e: e.memzero(self.btile[:, 1:2]))
        p.op("gpsimd", lambda e: e.memset(self.btile[:, 2:3], 0.0))
        p.op("vector", lambda e: e.memset(self.epscol[:], 1e-5))
        p.op("vector", lambda e: e.memset(self.negC[:], -EXPC))
        p.op("vector", lambda e: e.memset(self.big8_f[:], 999.0))
        for c in range(NCH):
            p.dma("sync", lambda e, c=c: e.dma_start(
                out=self.XT[:, c, :], in_=self.ap["x0"][c * 128:(c + 1) * 128, :]), "ldx")
        if self.need_l0ffn:
            p.dma("gpsimd", lambda e: e.dma_start(
                out=self.smallb[:],
                in_=self.ap["l0r_wT"].rearrange("(k p) e -> p k e", p=128)), "wsm")
            p.dma("gpsimd", lambda e: e.dma_start(out=self.l0rb_b[:], in_=self.ap["l0r_b"]), "wsm")
            p.dma("gpsimd", lambda e: e.dma_start(out=self.bexp_b[:], in_=self.ap["l0e_b"]), "wsm")
            for i in range(NE):
                p.dma("gpsimd", lambda e, i=i: e.dma_start(
                    out=self.l0e_bf[i], in_=self.ap["l0e_wT"][i]), "wl0e")
        if self.need_gate:
            p.dma("gpsimd", lambda e: e.dma_start(
                out=self.wgb[:],
                in_=self.ap["l1f_wg"].rearrange("(k p) e -> p k e", p=128)), "wsm2")
        if self.need_l0attn:
            for i, nm in enumerate(("l0a_wq", "l0a_wk", "l0a_wv", "l0a_wo")):
                p.dma("gpsimd", lambda e, i=i, nm=nm: e.dma_start(
                    out=self.l0a_bf[i], in_=self.ap[nm]), "wl0a")
        if self.need_l1attn:
            p.dma("gpsimd", lambda e: e.dma_start(out=self.l1aq_bf, in_=self.ap["l1a_wqkvT"]), "wl1a")
            p.dma("gpsimd", lambda e: e.dma_start(out=self.l1ap_bf, in_=self.ap["l1a_wprojT"]), "wl1a")
        if self.need_l1ffn:
            for i in range(NE):
                p.dma("gpsimd", lambda e, i=i: e.dma_start(
                    out=self.w1_bf[i], in_=self.ap["l1f_w1"][i]), "ww1")
                p.dma("gpsimd", lambda e, i=i: e.dma_start(
                    out=self.w2_bf[i], in_=self.ap["l1f_w2"][i]), "ww2")
            p.dma("gpsimd", lambda e: e.dma_start(out=self.b2_bf, in_=self.ap["l1f_b2"]), "wsm2")
            p.dma("sync", lambda e: e.dma_start(out=self.priors_s[:], in_=self.ap["priors"]), "ld0")
            p.dma("sync", lambda e: e.dma_start(out=self.selprev_s[:], in_=self.ap["selprev"]), "ld0")
        p.wait("sync", "ldx")
        p.wait("sync", "ld0")
        p.wait("gpsimd", "ld0g")
        p.barrier(name="prep")

    def emit_ln(self, tag):
        p = self.p
        blt = [p.base("lt0"), p.base("lt1")]
        for c in range(NCH):
            p.op("vector", lambda e, c=c: e.reduce_sum(
                self.lncol[:, 0, c:c + 1], self.XT[:, c, :], axis=AX.X), "lv")
            p.op("scalar", lambda e, c=c: e.activation(
                self.PSB[:, :], self.XT[:, c, :], AF.Square,
                accum_out=self.lncol[:, 1, c:c + 1]), "ls")
        p.wait("vector", "ls")
        p.op("vector", lambda e: e.tensor_scalar(
            self.lncol[:, 2, :], self.lncol[:, 0, :], 1.0 / D, None, ALU.mult))
        p.op("vector", lambda e: e.tensor_tensor(
            self.lncol[:, 3, :], self.lncol[:, 2, :], self.lncol[:, 2, :], ALU.mult))
        p.op("vector", lambda e: e.scalar_tensor_tensor(
            self.lncol[:, 4, :], self.lncol[:, 1, :], 1.0 / D, self.lncol[:, 3, :],
            ALU.mult, ALU.subtract), "lv")
        p.wait("scalar", "lv")
        p.op("scalar", lambda e: e.activation(
            self.lncol[:, 5, :], self.lncol[:, 4, :], AF.Sqrt, bias=self.epscol[:]), "ls")
        p.wait("vector", "ls")
        p.op("vector", lambda e: e.reciprocal(self.lncol[:, 4, :], self.lncol[:, 5, :]))
        for c in range(NCH):
            if c >= 2:
                p.wait("vector", f"lt{c % 2}", blt[c % 2] + 16 * (c // 2))
            p.op("vector", lambda e, c=c: e.tensor_scalar(
                self.xtmb[:, c % 2, :], self.XT[:, c, :],
                self.lncol[:, 2, c:c + 1], self.lncol[:, 4, c:c + 1],
                ALU.subtract, ALU.mult), "ln")
            p.wait("sync", "ln")
            p.dma("sync", lambda e, c=c: e.dma_start(
                out=self.xhat_d[c * 128:(c + 1) * 128, :], in_=self.xtmb[:, c % 2, :]),
                f"lt{c % 2}")
        p.wait("sync", "lt0")
        p.wait("sync", "lt1")
        for k in range(KC):
            p.dma("sync", lambda e, k=k: e.dma_start(
                out=self.FM[:, k, :], in_=self.xhat_d[:, k * 128:(k + 1) * 128],
                transpose=True), "lf")
        p.wait("sync", "lf")
        p.barrier(name="b")

    def emit_l0ffn(self, tag):
        p = self.p
        PS, PSM = self.PS, self.PSM
        bge = p.base("ge"); bbbe = p.base("bbe"); bew = p.base("ew")
        p.wait("tensor", "wsm", 16 * 2)
        for c in range(NCH):
            if c >= 1:
                p.wait("tensor", "ge", bge + c)
            for k in range(KC):
                self.mm(PSM[:, 0:8], self.FM[:, k, c * 128:(c + 1) * 128],
                        self.smallb[:, k, :], start=(k == 0), stop=False)
            self.mm(PSM[:, 0:8], self.onesr_b[:], self.l0rb_b[:], start=False, stop=True,
                    sem="gm")
            p.wait("scalar", "gm")
            p.op("scalar", lambda e, c=c: e.activation(
                self.gl[:, c, :], PSM[:, 0:8], AF.Exp), "ge")
        p.wait("vector", "ge")
        p.op("vector", lambda e: e.tensor_reduce(
            self.gaux[:, 0, :].rearrange("p (n o) -> p n o", o=1), self.gl, AX.X, ALU.add))
        p.op("vector", lambda e: e.reciprocal(self.gaux[:, 1, :], self.gaux[:, 0, :]))
        for c in range(NCH):
            p.op("vector", lambda e, c=c: e.max(self.scr8[:], self.gl[:, c, :]))
            p.op("vector", lambda e, c=c: e.tensor_scalar(
                self.oh_f[:], self.gl[:, c, :], self.scr8[:, 1:2], None, ALU.is_ge))
            p.op("vector", lambda e, c=c: e.tensor_scalar(
                self.rw8[:, c, :], self.gl[:, c, :], self.gaux[:, 1, c:c + 1], 0.5,
                ALU.mult, ALU.mult))
            p.op("vector", lambda e, c=c: e.tensor_tensor(
                self.rw8[:, c, :], self.rw8[:, c, :], self.oh_f[:], ALU.mult))
        p.wait("tensor", "wsm")
        for c in range(NCH):
            p.op("vector", lambda e, c=c: e.tensor_copy(
                self.rwb1[:], self.rw8[:, c, :]), "gw")
            p.wait("tensor", "gw")
            p.op("tensor", lambda e, c=c: e.transpose(
                self.PT[0:8, :], self.rwb1[:], self.ident_b[:]), "bt")
            p.wait("vector", "bt")
            p.op("vector", lambda e: e.tensor_copy(self.rwT1[:], self.PT[0:8, :]), "bte")
            p.wait("tensor", "bte")
            for s in range(2):
                slot = c * 2 + s
                bank = PS[slot % 2]
                if slot >= 2:
                    p.wait("tensor", "bbe", bbbe + slot - 1)
                self.mm(bank[:, :], self.rwT1[:], self.bexp_b[:, s * 512:(s + 1) * 512],
                        sem="bbm")
                p.wait("vector", "bbm")
                p.op("vector", lambda e, c=c, s=s, bank=bank: e.tensor_tensor(
                    self.XT[:, c, s * 512:(s + 1) * 512], self.XT[:, c, s * 512:(s + 1) * 512],
                    bank[:, :], ALU.add), "bbe")
        p.wait("sync", "wl0e")
        for i in range(NE):
            for half in range(2):
                slot = i * 2 + half
                if slot >= 1:
                    p.wait("sync", "ew", bew + slot * NCH)
                p.dma("sync", lambda e, i=i, h=half: e.dma_start(
                    out=self.wbuf[:],
                    in_=self.l0e_bf[i, :, h * 512:(h + 1) * 512].rearrange(
                        "(k p) n -> p k n", p=128)), "el")
                p.wait("tensor", "el")
                for c in range(NCH):
                    pb = PS[2 + (c % 2)]
                    if c >= 2:
                        p.wait("tensor", "ew", bew + slot * NCH + c - 1)
                    for k in range(KC):
                        self.mm(pb[:, :], self.FM[:, k, c * 128:(c + 1) * 128],
                                self.wbuf[:, k, :], start=(k == 0), stop=(k == KC - 1),
                                sem=("em" if k == KC - 1 else None))
                    p.wait("vector", "em")
                    p.op("vector", lambda e, c=c, h=half, pb=pb, i=i: e.scalar_tensor_tensor(
                        self.XT[:, c, h * 512:(h + 1) * 512], pb[:, :],
                        self.rw8[:, c, i:i + 1],
                        self.XT[:, c, h * 512:(h + 1) * 512],
                        ALU.mult, ALU.add), "ew")
        p.wait("vector", "ew")
        p.barrier(name="bf")

    def emit_finish(self):
        p = self.p
        if self.out_half:
            raise NotImplementedError
        for c in range(NCH):
            p.dma("sync", lambda e, c=c: e.dma_start(
                out=self.outs["xout"][c * 128:(c + 1) * 128, :], in_=self.XT[:, c, :]), "fst")
        p.wait("sync", "fst")

    def emit_dump_dram(self, src_ap):
        p = self.p
        for c in range(NCH):
            p.dma("sync", lambda e, c=c: e.dma_start(
                out=self.xtmb[:, 0, :], in_=src_ap[c * 128:(c + 1) * 128, :]), "dmp1")
            p.wait("vector", "dmp1")
            p.op("vector", lambda e, c=c: e.tensor_copy(self.XT[:, c, :], self.xtmb[:, 0, :]), "dmp2")
            p.wait("sync", "dmp2")
            p.dma("sync", lambda e, c=c: e.dma_start(
                out=self.outs["xout"][c * 128:(c + 1) * 128, :], in_=self.XT[:, c, :]), "fst")
            p.wait("sync", "fst")
        p.wait("sync", "fst")

    def emit_dump_sbuf_f32(self, get_slice, nch=NCH):
        p = self.p
        for c in range(nch):
            p.dma("sync", lambda e, c=c: e.dma_start(
                out=self.outs["xout"][c * 128:(c + 1) * 128, :], in_=get_slice(c)), "fst")
        p.wait("sync", "fst")

    def emit_all(self):
        import os
        dump = os.environ.get("DEVK_DUMP", "")
        self.emit_prep()
        for s in self.stages:
            self.emit_ln(s)
            if dump == "xhat":
                self.emit_dump_dram(self.xhat_d)
                self.p.build()
                return self.nc
            if dump == "fm":
                xv = self.outs["xout"].rearrange("(k a p) d -> (k) (a p d)", k=KC, a=4)
                for k in range(KC):
                    self.p.dma("gpsimd", lambda e, k=k: e.dma_start(
                        out=xv[k:k + 1, :].rearrange("a (p t) -> (a p) t", p=128),
                        in_=self.FM[:, k, :]), "fst")
                self.p.wait("gpsimd", "fst")
                self.p.build()
                return self.nc
            if s in ("l0ffn_a", "l0ffn_b"):
                self.emit_l0ffn(s)
            elif s == "l0attn":
                self.emit_l0attn(s)
            elif s == "l1attn":
                self.emit_l1attn(s)
            elif s in ("l1ffn_a", "l1ffn_b"):
                self.emit_l1ffn(s)
            elif s == "l1gate":
                self.emit_l1ffn(s, full=False)
        self.emit_finish()
        self.p.build()
        return self.nc


def host_inputs(d, batch):
    m = {"x0": np.ascontiguousarray(np.asarray(d["L0_attn_X"])[batch])}
    m["l0r_wT"] = np.ascontiguousarray(np.asarray(d["L0_ffn_W_router"]).T)
    m["l0r_b"] = np.ascontiguousarray(np.asarray(d["L0_ffn_b_router"])[None, :])
    m["l0e_wT"] = np.ascontiguousarray(np.asarray(d["L0_ffn_W_experts"]).transpose(0, 2, 1))
    m["l0e_b"] = np.ascontiguousarray(np.asarray(d["L0_ffn_b_experts"]))
    m["l0a_wq"] = np.ascontiguousarray(np.asarray(d["L0_attn_Wq"]))
    m["l0a_wk"] = np.ascontiguousarray(np.asarray(d["L0_attn_Wk"]))
    m["l0a_wv"] = np.ascontiguousarray(np.asarray(d["L0_attn_Wv"]))
    m["l0a_wo"] = np.ascontiguousarray(np.asarray(d["L0_attn_Wo"]))
    m["l1a_wqkvT"] = np.ascontiguousarray(np.asarray(d["L1_attn_W_qkv"]).T)
    m["l1a_wprojT"] = np.ascontiguousarray(np.asarray(d["L1_attn_W_proj"]).T)
    m["l1f_wg"] = np.ascontiguousarray(np.asarray(d["L1_ffn_wg"]))
    m["l1f_w1"] = np.ascontiguousarray(np.asarray(d["L1_ffn_w1"]))
    m["l1f_b1T"] = np.ascontiguousarray(
        np.asarray(d["L1_ffn_b1"]).reshape(8, 32, 128).transpose(0, 2, 1))
    m["l1f_w2"] = np.ascontiguousarray(np.asarray(d["L1_ffn_w2"]))
    m["l1f_b2"] = np.ascontiguousarray(np.asarray(d["L1_ffn_b2"]))
    return m


def _proj_fm_out(self, tag, wsrc_d, col0, dst_d):
    p = self.p
    PS = self.PS
    bpae = p.base("pae"); bpas = [p.base("pas0"), p.base("pas1")]
    for oc in range(8):
        p.wait("sync", "pam")
        p.dma("sync", lambda e, oc=oc: e.dma_start(
            out=self.wbuf[:, :, 0:128],
            in_=wsrc_d[:, col0 + oc * 128:col0 + (oc + 1) * 128].rearrange(
                "(k p) n -> p k n", p=128)), "pal")
        p.wait("tensor", "pal")
        for ts in range(8):
            slot = oc * 8 + ts
            bank = PS[slot % 2]
            if slot >= 2:
                p.wait("tensor", "pae", bpae + slot - 1)
            for k in range(KC):
                self.mm(bank[:, :], self.wbuf[:, k, 0:128],
                        self.FM[:, k, ts * 512:(ts + 1) * 512],
                        start=(k == 0), stop=(k == KC - 1),
                        sem=("pam" if k == KC - 1 else None))
            p.wait("scalar", "pam")
            if slot >= 2:
                p.wait("scalar", f"pas{slot % 2}", bpas[slot % 2] + 16 * (slot // 2))
            p.op("scalar", lambda e, slot=slot, bank=bank: e.activation(
                self.xtmb[:, slot % 2, 0:512], bank[:, :], AF.Identity), "pae")
            p.wait("sync", "pae")
            p.dma("sync", lambda e, oc=oc, ts=ts, slot=slot: e.dma_start(
                out=dst_d[oc * 128:(oc + 1) * 128, ts * 512:(ts + 1) * 512],
                in_=self.xtmb[:, slot % 2, 0:512]), f"pas{slot % 2}")
    p.wait("sync", "pam")
    p.wait("sync", "pas0")
    p.wait("sync", "pas1")


def _proj_tm_out(self, tag, wsrc_d, col0, width, dst_d):
    p = self.p
    PS = self.PS
    bpte = p.base("pte"); bpts = [p.base("pts0"), p.base("pts1")]
    for half in range(width // 512):
        p.wait("sync", "ptm")
        p.dma("sync", lambda e, half=half: e.dma_start(
            out=self.wbuf[:],
            in_=wsrc_d[:, col0 + half * 512:col0 + (half + 1) * 512].rearrange(
                "(k p) n -> p k n", p=128)), "ptl")
        p.wait("tensor", "ptl")
        for c in range(NCH):
            slot = half * NCH + c
            bank = PS[2 + slot % 2]
            if slot >= 2:
                p.wait("tensor", "pte", bpte + slot - 1)
            for k in range(KC):
                self.mm(bank[:, :], self.FM[:, k, c * 128:(c + 1) * 128],
                        self.wbuf[:, k, :], start=(k == 0), stop=(k == KC - 1),
                        sem=("ptm" if k == KC - 1 else None))
            p.wait("scalar", "ptm")
            if slot >= 2:
                p.wait("scalar", f"pts{slot % 2}", bpts[slot % 2] + 16 * (slot // 2))
            p.op("scalar", lambda e, slot=slot, bank=bank: e.activation(
                self.xtmb[:, slot % 2, 512:1024], bank[:, :], AF.Identity), "pte")
            p.wait("sync", "pte")
            p.dma("sync", lambda e, c=c, half=half, slot=slot: e.dma_start(
                out=dst_d[c * 128:(c + 1) * 128, half * 512:(half + 1) * 512],
                in_=self.xtmb[:, slot % 2, 512:1024]), f"pts{slot % 2}")
    p.wait("sync", "ptm")
    p.wait("sync", "pts0")
    p.wait("sync", "pts1")


def emit_l0attn(self, tag):
    p = self.p
    PS, PSB = self.PS, self.PSB
    bse = p.base("se"); bye = p.base("ye"); bsup = p.base("sup")
    bavm = p.base("avm"); bpje = p.base("pje")
    p.wait("sync", "wl0a")
    _proj_fm_out(self, "q" + tag, self.l0a_bf[0], 0, self.qfm_d)
    _proj_fm_out(self, "k" + tag, self.l0a_bf[1], 0, self.kfm_d)
    _proj_tm_out(self, "v" + tag, self.l0a_bf[2], 0, D, self.vtm_d)
    _proj_tm_out(self, "kt" + tag, self.l0a_bf[1], 0, D, self.ktm_d)
    p.barrier(name="ba")
    p.op("vector", lambda e: e.memset(self.statef, 0.0))
    p.op("vector", lambda e: e.memset(self.stateb, 0.0), "stz")
    p.wait("tensor", "stz")
    g_se = g_ye = g_sup = g_av = g_pje = 0
    for sc in range(NSC):
        if sc >= 1:
            p.wait("sync", "pjm")
            p.wait("tensor", "pje")
        p.dma("sync", lambda e, sc=sc: e.dma_start(
            out=self.v_qf, in_=self.qfm_d[:, sc * 512:(sc + 1) * 512].rearrange(
                "(k p) t -> p k t", p=128)), "bl")
        p.dma("sync", lambda e, sc=sc: e.dma_start(
            out=self.v_kf[:, 0], in_=self.kfm_d[:, sc * 512:(sc + 1) * 512].rearrange(
                "(k p) t -> p k t", p=128)), "bl")
        p.dma("sync", lambda e, sc=sc: e.dma_start(
            out=self.v_vt[:, 0], in_=self.vtm_d[sc * 512:(sc + 1) * 512, :].rearrange(
                "(a p) o -> p a o", p=128)), "bl")
        p.dma("sync", lambda e, sc=sc: e.dma_start(
            out=self.v_kt, in_=self.ktm_d[sc * 512:(sc + 1) * 512, :].rearrange(
                "(a p) o -> p a o", p=128)), "bl")
        p.wait("tensor", "bl")
        p.wait("tensor", "sbu")
        for h in range(H):
            hb = (h % 2) * 64
            hk = h // 2
            gh = sc * H + h
            psy = PS[h % 2]
            if gh >= 2:
                p.wait("tensor", "ye", bye + gh - 1)
            self.mm(psy[0:64, :], self.stateb[hb:hb + 64, hk, :],
                    self.v_qf[hb:hb + 64, hk, :], start=True, stop=False)
            for kc in range(4):
                gs = g_se
                g_se += 1
                pss = PS[2 + gs % 2]
                if gs >= 2:
                    p.wait("tensor", "se", bse + gs - 1)
                self.mm(pss[:, :],
                        self.v_kf[hb:hb + 64, 0, hk, kc * 128:(kc + 1) * 128],
                        self.v_qf[hb:hb + 64, hk, :], start=True, stop=True,
                        sem="sm")
                p.wait("scalar", "sm")
                if g_av >= 2:
                    p.wait("scalar", "avm", bavm + g_av - 1)
                p.op("scalar", lambda e, gs=gs, pss=pss: e.activation(
                    self.v_at[:, gs % 2, :], pss[:, :], AF.Identity), "se")
                p.wait("gpsimd", "se")
                p.op("gpsimd", lambda e, kc=kc, gs=gs: e.affine_select(
                    out=self.v_at[:, gs % 2, :], in_=self.v_at[:, gs % 2, :],
                    pattern=[[1, 512]], compare_op=ALU.is_ge, fill=0.0,
                    base=-kc * 128, channel_multiplier=-1), "msk")
                p.wait("tensor", "msk")
                g_av += 1
                self.mm(psy[0:64, :], self.v_vt[:, 0, kc, h * 64:(h + 1) * 64],
                        self.v_at[:, gs % 2, :], start=False, stop=(kc == 3),
                        sem="avm")
            p.wait("scalar", "avm")
            p.op("scalar", lambda e, hb=hb, hk=hk, psy=psy: e.activation(
                self.v_af[hb:hb + 64, hk, :], psy[0:64, :], AF.Identity), "ye")
            if gh >= 1:
                p.wait("tensor", "sup", bsup + gh)
            for kc in range(4):
                self.mm(PSB[0:64, 0:64], self.v_kt[:, kc, h * 64:(h + 1) * 64],
                        self.v_vt[:, 0, kc, h * 64:(h + 1) * 64],
                        start=(kc == 0), stop=(kc == 3),
                        sem=("sum" if kc == 3 else None))
            p.wait("vector", "sum")
            p.op("vector", lambda e, hb=hb, hk=hk: e.tensor_tensor(
                self.statef[hb:hb + 64, hk, :], self.statef[hb:hb + 64, hk, :],
                PSB[0:64, 0:64], ALU.add), "sup")
            p.op("vector", lambda e, hb=hb, hk=hk: e.tensor_copy(
                self.stateb[hb:hb + 64, hk, :], self.statef[hb:hb + 64, hk, :]),
                "sbu")
        p.wait("tensor", "ye")
        p.wait("tensor", "se")
        for s in range(2):
            p.wait("sync", "pjm")
            p.dma("sync", lambda e, s=s: e.dma_start(
                out=self.wbuf[:],
                in_=self.l0a_bf[3][:, s * 512:(s + 1) * 512].rearrange(
                    "(k p) n -> p k n", p=128)), "pjl")
            p.wait("tensor", "pjl")
            for tc in range(4):
                gp = g_pje
                g_pje += 1
                bank = PS[2 + gp % 2]
                if gp >= 2:
                    p.wait("tensor", "pje", bpje + gp - 1)
                for k in range(KC):
                    self.mm(bank[:, :], self.v_af[:, k, tc * 128:(tc + 1) * 128],
                            self.wbuf[:, k, :], start=(k == 0), stop=(k == KC - 1),
                            sem=("pjm" if k == KC - 1 else None))
                p.wait("vector", "pjm")
                p.op("vector", lambda e, sc=sc, tc=tc, s=s, bank=bank: e.tensor_tensor(
                    self.XT[:, sc * 4 + tc, s * 512:(s + 1) * 512],
                    self.XT[:, sc * 4 + tc, s * 512:(s + 1) * 512],
                    bank[:, :], ALU.add), "pje")
    p.wait("vector", "pje")
    p.barrier(name="bb")


MK.emit_l0attn = emit_l0attn


def emit_l1attn(self, tag):
    p = self.p
    PS, PSB = self.PS, self.PSB
    bse = p.base("se"); bavs = p.base("avs"); bpje = p.base("pje")
    p.wait("sync", "wl1a")
    _proj_fm_out(self, "q" + tag, self.l1aq_bf, 0, self.qfm_d)
    _proj_fm_out(self, "k" + tag, self.l1aq_bf, D, self.kfm_d)
    _proj_tm_out(self, "v" + tag, self.l1aq_bf, 2 * D, D, self.vtm_d)
    p.barrier(name="ba")
    p.op("vector", lambda e: e.memset(self.v_vt65[:, :, :, :, 64:65], 1.0), "vo")
    p.wait("tensor", "vo")
    rr32 = self.statef[:].rearrange("p a b -> p (a b)")[0:1, 0:512]
    g_se = g_pje = g_kl = 0
    for sc in range(NSC):
        if sc >= 1:
            p.wait("sync", "pjm")
            p.wait("tensor", "pje")
        p.dma("sync", lambda e, sc=sc: e.dma_start(
            out=self.v_qf, in_=self.qfm_d[:, sc * 512:(sc + 1) * 512].rearrange(
                "(k p) t -> p k t", p=128)), "bl")
        p.wait("tensor", "bl")
        for g in range(4):
            p.wait("tensor", "yse")
            for ksc in range(sc + 1):
                par = g_kl % 2
                g_kl += 1
                if g_kl >= 3:
                    p.wait("sync", "sm", None)
                p.dma("sync", lambda e, g=g, ksc=ksc, par=par: e.dma_start(
                    out=self.v_kf[:, par, 0:2, :],
                    in_=self.kfm_d[256 * g:256 * (g + 1),
                                    ksc * 512:(ksc + 1) * 512].rearrange(
                        "(k p) t -> p k t", p=128)), f"kl{par}")
            
                for kc4 in range(4):
                    p.dma("sync", lambda e, g=g, ksc=ksc, par=par, kc4=kc4: e.dma_start(
                        out=self.v_vt65[:, par, kc4, :, 0:64],
                        in_=self.vtm_d[ksc * 512 + kc4 * 128:ksc * 512 + (kc4 + 1) * 128,
                                        256 * g:256 * (g + 1)].rearrange(
                            "p (h w) -> p h w", w=64)), f"vl{par}")
                p.wait("tensor", f"kl{par}")
                p.wait("tensor", f"vl{par}")
                for hi in range(4):
                    h = 4 * g + hi
                    hb = (h % 2) * 64
                    hkk = hi // 2
                    psy = PS[hi]
                    for kc in range(4):
                        gs = g_se
                        g_se += 1
                        pss = PSB[:, (gs % 2) * 512:(gs % 2 + 1) * 512]
                        if gs >= 2:
                            p.wait("tensor", "se", bse + gs - 1)
                        self.mm(pss, self.v_kf[hb:hb + 64, par, hkk,
                                                kc * 128:(kc + 1) * 128],
                                self.v_qf[hb:hb + 64, h // 2, :],
                                start=True, stop=True, sem="sm")
                        p.wait("scalar", "sm")
                        if g_se >= 3:
                            p.wait("scalar", "avs", bavs + g_se - 2)
                        p.op("scalar", lambda e, gs=gs, pss=pss: e.activation(
                            self.v_at[:, gs % 2, :], pss, AF.Exp,
                            bias=self.negC[:], scale=0.125), "se")
                        if ksc == sc:
                            p.wait("gpsimd", "se")
                            p.op("gpsimd", lambda e, kc=kc, gs=gs: e.affine_select(
                                out=self.v_at[:, gs % 2, :], in_=self.v_at[:, gs % 2, :],
                                pattern=[[1, 512]], compare_op=ALU.is_ge, fill=0.0,
                                base=-kc * 128, channel_multiplier=-1), "msk")
                            p.wait("tensor", "msk")
                        else:
                            p.wait("tensor", "se")
                        first = (ksc == 0 and kc == 0)
                        self.mm(psy[0:65, :],
                                self.v_vt65[:, par, kc, hi, :],
                                self.v_at[:, gs % 2, :], start=first,
                                stop=(ksc == sc and kc == 3), sem="avs")
            for hi in range(4):
                h = 4 * g + hi
                hb = (h % 2) * 64
                psy = PS[hi]
                p.wait("vector", "avs")
                p.op("vector", lambda e, psy=psy, rr32=rr32: e.reciprocal(
                    rr32, psy[64:65, :]))
                p.op("vector", lambda e, rr32=rr32: e.tensor_copy(
                    self.v_rrb[0:1, :], rr32), "rrb")
                p.wait("tensor", "rrb")
                p.wait("tensor", "se")
                self.mm(PSB[:, 0:512], self.onesr_b[:], self.v_rrb[0:1, 0:512],
                        start=True, stop=True, sem="bcm")
                p.wait("scalar", "bcm")
                p.op("scalar", lambda e, psy=psy: e.activation(
                    self.v_ysb[0:64, :], psy[0:64, :], AF.Identity), "yse")
                p.wait("vector", "yse")
                p.op("vector", lambda e, hb=hb, h=h: e.tensor_tensor(
                    self.v_af[hb:hb + 64, h // 2, :], self.v_ysb[0:64, :],
                    PSB[0:64, 0:512], ALU.mult), "afw")
                p.wait("tensor", "afw")
        p.wait("tensor", "afw")
        for s in range(2):
            p.wait("sync", "pjm")
            p.dma("sync", lambda e, s=s: e.dma_start(
                out=self.wbuf[:],
                in_=self.l1ap_bf[:, s * 512:(s + 1) * 512].rearrange(
                    "(k p) n -> p k n", p=128)), "pjl")
            p.wait("tensor", "pjl")
            for tc in range(4):
                gp = g_pje
                g_pje += 1
                bank = PS[2 + gp % 2]
                if gp >= 2:
                    p.wait("tensor", "pje", bpje + gp - 1)
                for k in range(KC):
                    self.mm(bank[:, :], self.v_af[:, k, tc * 128:(tc + 1) * 128],
                            self.wbuf[:, k, :], start=(k == 0), stop=(k == KC - 1),
                            sem=("pjm" if k == KC - 1 else None))
                p.wait("vector", "pjm")
                p.op("vector", lambda e, sc=sc, tc=tc, s=s, bank=bank: e.tensor_tensor(
                    self.XT[:, sc * 4 + tc, s * 512:(s + 1) * 512],
                    self.XT[:, sc * 4 + tc, s * 512:(s + 1) * 512],
                    bank[:, :], ALU.add), "pje")
    p.wait("vector", "pje")
    p.barrier(name="bb")


MK.emit_l1attn = emit_l1attn


def emit_l1ffn(self, tag, full=True):
    p = self.p
    PS, PSB, PSM = self.PS, self.PSB, self.PSM
    gl = self.gl
    bge = p.base("ge"); bhe = p.base("he"); bcb = p.base("cb")
    p.wait("tensor", "wsm2")
    for c in range(NCH):
        if c >= 1:
            p.wait("tensor", "ge", bge + c)
        for k in range(KC):
            self.mm(PSM[:, 0:8], self.FM[:, k, c * 128:(c + 1) * 128],
                    self.wgb[:, k, :], start=(k == 0), stop=(k == KC - 1),
                    sem=("gm" if k == KC - 1 else None))
        p.wait("scalar", "gm")
        p.op("scalar", lambda e, c=c: e.activation(
            gl[:, c, :], PSM[:, 0:8], AF.Exp), "ge")
    p.wait("vector", "ge")
    p.op("vector", lambda e: e.tensor_reduce(
        self.gaux[:, 0, :].rearrange("p (n o) -> p n o", o=1), gl, AX.X, ALU.add))
    p.op("vector", lambda e: e.reciprocal(self.gaux[:, 1, :], self.gaux[:, 0, :]))
    idxf = self.rw8[:, :, 0]
    m1f = self.rw8[:, :, 1]
    for c in range(NCH):
        p.op("vector", lambda e, c=c: e.max(self.scr8[:], gl[:, c, :]))
        p.op("vector", lambda e, c=c: e.tensor_copy(m1f[:, c:c + 1], self.scr8[:, 0:1]))
        p.op("vector", lambda e, c=c: e.tensor_scalar(
            self.oh_f[:], gl[:, c, :], self.scr8[:, 0:1], None, ALU.is_ge))
        p.op("vector", lambda e: e.tensor_tensor(
            self.scr8[:], self.oh_f[:], self.iotam_f[:], ALU.mult))
        p.op("vector", lambda e: e.tensor_scalar(
            self.scr8[:], self.scr8[:], 999.0, None, ALU.add))
        p.op("vector", lambda e, c=c: e.tensor_reduce(
            idxf[:, c:c + 1].rearrange("p (n o) -> p n o", o=1),
            self.scr8[:].rearrange("p (n e) -> p n e", n=1), AX.X, ALU.min))
    p.op("vector", lambda e: e.tensor_tensor(
        self.gaux[:, 3, :], m1f[:], self.gaux[:, 1, :], ALU.mult), "gg")
    p.wait("tensor", "gg")
    if full:
        self.mm(PSM[0:1, 0:8], self.selprev_s[:], self.priors_s[:], sem="of")
        p.wait("vector", "of")
        p.op("vector", lambda e: e.tensor_copy(self.offrow[:], PSM[0:1, 0:8]))
        p.op("vector", lambda e: e.tensor_copy(self.carry[:], PSM[0:1, 0:8]), "cw")
        p.wait("tensor", "cw")
        self.mm(PSM[:, 8:16], self.onesr_f[:], self.offrow[:], sem="ob")
        p.wait("vector", "ob")
        p.op("vector", lambda e: e.tensor_copy(self.offb[:], PSM[:, 8:16]), "obs")
    else:
        p.op("vector", lambda e: e.memset(self.offrow[:], 0.0))
        p.op("vector", lambda e: e.memset(self.offb[:], 0.0))
        p.op("vector", lambda e: e.memset(self.carry[:], 0.0), "cw")
        p.wait("tensor", "cw")
    for c in range(NCH):
        p.op("vector", lambda e, c=c: e.tensor_scalar(
            self.oh_f[:], self.iota8_f[:], idxf[:, c:c + 1], None, ALU.is_equal))
        p.op("vector", lambda e: e.tensor_copy(self.scr8b_oh(), self.oh_f[:]), "ohb")
        p.wait("tensor", "ohb")
        p.wait("tensor", "pe", None)
        p.wait("tensor", "cw", None)
        self.mm(PSM[:, 0:8], self.ustr_b[:], self.scr8b_oh(), start=True, stop=False)
        self.mm(PSM[:, 0:8], self.onesr_f[:], self.carry[:], start=False, stop=True,
                sem="pm")
        self.mm(PS[0][0:1, 0:8], self.onesc_f[:], self.oh_f[:], start=True, stop=True,
                sem="cm")
        p.wait("vector", "pm")
        p.op("vector", lambda e: e.tensor_tensor(
            self.scr8[:], PSM[:, 0:8], self.oh_f[:], ALU.mult))
        p.op("vector", lambda e: e.tensor_reduce(
            self.poscol[:, 0:1].rearrange("p (n o) -> p n o", o=1),
            self.scr8[:].rearrange("p (n e) -> p n e", n=1), AX.X, ALU.add), "pe")
        p.op("vector", lambda e, c=c: e.tensor_scalar(
            self.poscol[:, 1:2], self.poscol[:, 0:1], float(CAPREF), None, ALU.is_lt))
        p.op("vector", lambda e, c=c: e.scalar_tensor_tensor(
            self.gsc[:, c:c + 1], self.gaux[:, 3, c:c + 1], 0.5, self.poscol[:, 1:2],
            ALU.mult, ALU.mult))
        p.op("vector", lambda e: e.tensor_tensor(
            self.scr8[:], self.offb[:], self.oh_f[:], ALU.mult))
        p.op("vector", lambda e: e.tensor_reduce(
            self.poscol[:, 2:3].rearrange("p (n o) -> p n o", o=1),
            self.scr8[:].rearrange("p (n e) -> p n e", n=1), AX.X, ALU.add))
        p.op("vector", lambda e: e.tensor_sub(
            self.poscol[:, 3:4], self.poscol[:, 0:1], self.poscol[:, 2:3]))
        p.op("vector", lambda e, c=c: e.scalar_tensor_tensor(
            self.slot_f[:, c:c + 1], idxf[:, c:c + 1], float(CAP1), self.poscol[:, 3:4],
            ALU.mult, ALU.add))
        p.op("vector", lambda e: e.tensor_scalar(
            self.poscol[:, 2:3], self.poscol[:, 3:4], float(CAP1), None, ALU.is_ge))
        p.op("vector", lambda e, c=c: e.scalar_tensor_tensor(
            self.slot_f[:, c:c + 1], self.poscol[:, 2:3], 1e7, self.slot_f[:, c:c + 1],
            ALU.mult, ALU.add))
        p.op("vector", lambda e, c=c: e.tensor_scalar(
            self.poscol[:, 2:3], self.poscol[:, 1:2], -1e7, 1e7, ALU.mult, ALU.add))
        p.op("vector", lambda e, c=c: e.tensor_tensor(
            self.slot_f[:, c:c + 1], self.slot_f[:, c:c + 1], self.poscol[:, 2:3],
            ALU.add))
        p.op("vector", lambda e, c=c: e.tensor_copy(
            self.slot_i[:, c:c + 1], self.slot_f[:, c:c + 1]), "sl")
        p.op("vector", lambda e, c=c: e.tensor_copy(
            self.posi[:, 0:1], self.poscol[:, 3:4]))
        p.op("vector", lambda e: e.tensor_scalar(
            self.posi[:, 0:1], self.posi[:, 0:1], 127, None, ALU.bitwise_and))
        p.op("vector", lambda e, c=c: e.tensor_copy(
            self.slot_f[:, c:c + 1], self.posi[:, 0:1]))
        p.op("vector", lambda e, c=c: e.tensor_sub(
            self.poscol[:, 2:3], self.poscol[:, 3:4], self.slot_f[:, c:c + 1]))
        p.op("vector", lambda e: e.tensor_scalar(
            self.poscol[:, 2:3], self.poscol[:, 2:3], 1.0 / 128.0, None, ALU.mult))
        p.op("vector", lambda e, c=c: e.scalar_tensor_tensor(
            self.poscol[:, 2:3], idxf[:, c:c + 1], 8.0, self.poscol[:, 2:3],
            ALU.mult, ALU.add))
        p.op("vector", lambda e, c=c: e.scalar_tensor_tensor(
            self.slot_f[:, c:c + 1], self.slot_f[:, c:c + 1], 64.0, self.poscol[:, 2:3],
            ALU.mult, ALU.add))
        p.op("vector", lambda e, c=c: e.tensor_scalar(
            self.poscol[:, 2:3], self.poscol[:, 1:2], -1e7, 1e7, ALU.mult, ALU.add))
        p.op("vector", lambda e, c=c: e.tensor_tensor(
            self.slot_f[:, c:c + 1], self.slot_f[:, c:c + 1], self.poscol[:, 2:3],
            ALU.add))
        p.op("vector", lambda e, c=c: e.tensor_scalar(
            self.poscol[:, 2:3], self.poscol[:, 3:4], float(CAP1), None, ALU.is_ge))
        p.op("vector", lambda e, c=c: e.scalar_tensor_tensor(
            self.slot_f[:, c:c + 1], self.poscol[:, 2:3], 1e7, self.slot_f[:, c:c + 1],
            ALU.mult, ALU.add))
        p.op("vector", lambda e, c=c: e.tensor_copy(
            self.slot2_i[:, c:c + 1], self.slot_f[:, c:c + 1]), "sl")
        p.wait("vector", "cm")
        p.op("vector", lambda e: e.tensor_tensor(
            self.carry[:], self.carry[:], PS[0][0:1, 0:8], ALU.add), "cw")
    if self.emit_counts and not full:
        p.op("vector", lambda e: e.tensor_sub(self.cnt_s[:], self.carry[:], self.offrow[:]),
             "cnt")
        p.wait("sync", "cnt")
        p.dma("sync", lambda e: e.dma_start(out=self.outs["counts"], in_=self.cnt_s[:]), "cst")
        p.wait("sync", "cst")
        return
    p.dma("sync", lambda e: e.dma_start(out=self.idx_d, in_=self.c_senti.ap()), "sn")
    p.wait("gpsimd", "sn")
    p.wait("gpsimd", "sl")
    for c in range(NCH):
        p.dma("gpsimd", lambda e, c=c: e.indirect_dma_start(
            out=self.idx_d, out_offset=IndirectOffsetOnAxis(self.slot2_i[:, c:c + 1], 0),
            in_=self.tokid_s[:, c:c + 1], in_offset=None,
            bounds_check=self.bregs['slot'], oob_is_err=False), "sc")
    p.wait("sync", "sc")
    p.dma("sync", lambda e: e.dma_start(
        out=self.idxc64[:].rearrange("p (a n) -> p a n", n=1),
        in_=self.idx_d.rearrange("(p a) n -> p a n", p=128)), "ix")
    p.wait("gpsimd", "ix")
    p.wait("tensor", "cw")
    p.op("vector", lambda e: e.memset(self.xtmb[:], 0.0), "xz")
    p.wait("gpsimd", "xz")
    for ei in range(NE):
        for j in range(KC):
            p.wait("gpsimd", f"gst{j % 2}", None)
            p.op("gpsimd", lambda e, ei=ei, j=j: e.tensor_copy(
                self.idx1[:], self.idxc64[:, ei * 8 + j:ei * 8 + j + 1]), "ixc")
            p.dma("gpsimd", lambda e, ei=ei, j=j: e.indirect_dma_start(
                out=self.xtmb[:, j % 2, :], out_offset=None,
                in_=self.xhat_d,
                in_offset=IndirectOffsetOnAxis(self.idx1[:], 0),
                bounds_check=self.bregs['tok'], oob_is_err=False), "gth")
            p.wait("sync", "gth")
            p.dma("sync", lambda e, j=j: e.dma_start(
                out=self.ddisp_d[j * 128:(j + 1) * 128, :],
                in_=self.xtmb[:, j % 2, :]), f"gst{j % 2}")
        p.wait("sync", "gst0")
        p.wait("sync", "gst1")
        p.wait("sync", "w1m")
        for k in range(KC):
            p.dma("sync", lambda e, k=k: e.dma_start(
                out=self.v_dfm[:, k, :],
                in_=self.ddisp_d[:, k * 128:(k + 1) * 128], transpose=True), "dt")
        p.dma("sync", lambda e, ei=ei: e.dma_start(
            out=self.b1c[:], in_=self.ap["l1f_b1T"][ei]), "dt")
        p.wait("tensor", "dt")
        for t2 in range(2):
            g_h = 0
            for fs in range(8):
                p.wait("sync", "w1m")
                p.dma("sync", lambda e, ei=ei, fs=fs: e.dma_start(
                    out=self.wbuf[:],
                    in_=self.w1_bf[ei, :, fs * 512:(fs + 1) * 512].rearrange(
                        "(k p) n -> p k n", p=128)), "w1l")
                p.wait("tensor", "w1l")
                for fc in range(4):
                    slot = fs * 4 + fc
                    bank = PS[slot % 2]
                    if (ei * 2 + t2) * 32 + slot >= 2:
                        p.wait("tensor", "he", bhe + (ei * 2 + t2) * 32 + slot - 1)
                    for k in range(KC):
                        self.mm(bank[:, :], self.wbuf[:, k, fc * 128:(fc + 1) * 128],
                                self.v_dfm[:, k, t2 * 512:(t2 + 1) * 512],
                                start=(k == 0), stop=(k == KC - 1),
                                sem=("w1m" if k == KC - 1 else None))
                    p.wait("scalar", "w1m")
                    p.op("scalar", lambda e, fs=fs, fc=fc, bank=bank: e.activation(
                        self.v_hfm[:, fs * 4 + fc, :], bank[:, :], AF.Relu,
                        bias=self.b1c[:, fs * 4 + fc:fs * 4 + fc + 1]), "he")
            p.wait("tensor", "he")
            p.wait("sync", "w1m")
            for mh in range(2):
                p.wait("sync", "em")
                p.dma("sync", lambda e, ei=ei, mh=mh: e.dma_start(
                    out=self.b2sm[:], in_=self.b2_bf[ei:ei + 1, mh * 512:(mh + 1) * 512]),
                    "b2l")
                for kg in range(4):
                    p.wait("sync", "w2m")
                    p.dma("sync", lambda e, ei=ei, mh=mh, kg=kg: e.dma_start(
                        out=self.wbuf[:],
                        in_=self.w2_bf[ei, kg * 1024:(kg + 1) * 1024,
                                        mh * 512:(mh + 1) * 512].rearrange(
                            "(k p) n -> p k n", p=128)), "w2l")
                    p.wait("tensor", "w2l")
                    if kg == 0:
                        p.wait("tensor", "ee", None)
                        p.wait("tensor", "b2l")
                    for tc in range(4):
                        for k in range(KC):
                            self.mm(PS[tc][:, :],
                                    self.v_hfm[:, kg * 8 + k, tc * 128:(tc + 1) * 128],
                                    self.wbuf[:, k, :],
                                    start=(kg == 0 and k == 0), stop=False,
                                    sem=("w2m" if k == KC - 1 else None))
                    if kg == 3:
                        for tc in range(4):
                            self.mm(PS[tc][:, :], self.onesr_b[:], self.b2sm[:, 0:512],
                                    start=False, stop=True, sem="em")
                p.wait("scalar", "em")
                for tc in range(4):
                    p.op("scalar", lambda e, tc=tc: e.activation(
                        self.v_eob[:, tc % 2, :], PS[tc][:, :], AF.Identity), "ee")
                    p.wait("sync", "ee")
                    p.dma("sync", lambda e, ei=ei, t2=t2, mh=mh, tc=tc: e.dma_start(
                        out=self.eo_d[ei * CAP1 + t2 * 512 + tc * 128:
                                      ei * CAP1 + t2 * 512 + (tc + 1) * 128,
                                      mh * 512:(mh + 1) * 512],
                        in_=self.v_eob[:, tc % 2, :]), f"eos{tc % 2}")
                    p.wait("scalar", f"eos{tc % 2}")
    p.wait("gpsimd", "eos0")
    p.wait("gpsimd", "eos1")
    for c in range(NCH):
        p.op("vector", lambda e, c=c: e.memset(self.v_gath[:, c % 2, :], 0.0), "gz")
        p.wait("gpsimd", "gz")
        if c >= 2:
            p.wait("gpsimd", "cb", bcb + c - 1)
        p.dma("gpsimd", lambda e, c=c: e.indirect_dma_start(
            out=self.v_gath[:, c % 2, :], out_offset=None,
            in_=self.eo_d, in_offset=IndirectOffsetOnAxis(self.slot_i[:, c:c + 1], 0),
            bounds_check=self.bregs['slot'], oob_is_err=False), "gb")
        p.wait("vector", "gb")
        for s in range(2):
            p.op("vector", lambda e, c=c, s=s: e.scalar_tensor_tensor(
                self.XT[:, c, s * 512:(s + 1) * 512],
                self.v_gath[:, c % 2, s * 512:(s + 1) * 512],
                self.gsc[:, c:c + 1],
                self.XT[:, c, s * 512:(s + 1) * 512],
                ALU.mult, ALU.add), "cb" if s == 1 else None)
    p.wait("vector", "cb")
    p.barrier(name="bf")


def scr8b_oh(self):
    if not hasattr(self, "_oh_b"):
        self._oh_b = self.ctx.enter_context(self.nc.sbuf_tensor("oh_b", [128, NE], BF16))
    return self._oh_b[:]


MK.emit_l1ffn = emit_l1ffn
MK.scr8b_oh = scr8b_oh


_PROGRAM_CACHE = {}


def _get_program():
    if "nc" not in _PROGRAM_CACHE:
        mk = MK(["l0ffn_a", "l0attn", "l0ffn_b", "l1ffn_a", "l1attn", "l1ffn_b"])
        nc = mk.emit_all()
        _PROGRAM_CACHE["nc"] = nc
        _PROGRAM_CACHE["mk"] = mk
    return _PROGRAM_CACHE["nc"], _PROGRAM_CACHE["mk"]


def kernel(**inputs):
    from concourse.bass_utils import run_bass_kernel_spmd
    nc, mk = _get_program()
    zpri = np.zeros((4, NE), np.float32)
    zsel = np.zeros((4, 1), np.float32)
    in_maps = []
    for c in range(8):
        m = host_inputs(inputs, c // 2)
        m["priors"] = zpri
        m["selprev"] = zsel
        m = {k: v for k, v in m.items() if k in mk.ap}
        in_maps.append(m)
    res = run_bass_kernel_spmd(nc, in_maps, core_ids=list(range(8)))
    out = np.empty((4, 4096, 1024), dtype=np.float32)
    for c in range(8):
        h = c % 2
        out[c // 2, h * 2048:(h + 1) * 2048, :] = res.results[c]["xout"][h * 2048:(h + 1) * 2048, :]
    return out


# version 04
# speedup vs baseline: 1.3125x; 1.3125x over previous
import numpy as np
import contextlib
import sys
sys.path.insert(0, '/opt/trn_rl_repo')
import concourse.bass as bass
import concourse.mybir as mybir
from concourse.bass_core_rust import IndirectOffsetOnAxis

F32 = mybir.dt.float32
BF16 = mybir.dt.bfloat16
I32 = mybir.dt.int32
AF = mybir.ActivationFunctionType
ALU = mybir.AluOpType
AX = mybir.AxisListType

ENGINES = ("sync", "tensor", "vector", "scalar", "gpsimd")

D = 1024
T = 4096
NCH = 32
KC = 8
NE = 8
CAP1 = 512
KCAP = CAP1 // 128
CAPREF = 2560
H = 16
DH = 64
NSC = 8
EXPC = 18.0


class P:
    def __init__(self, nc):
        self.nc = nc
        self.q = {e: [] for e in ENGINES}
        self.sems = {}
        self.cnt = {}
        self.virt = {}
        self.lastw = {}
        self._ctx = contextlib.ExitStack()

    def sem(self, name):
        if name not in self.sems:
            self.sems[name] = self._ctx.enter_context(self.nc.semaphore(name))
            self.cnt[name] = 0
        return self.sems[name]

    CHAINED = ("vector", "scalar", "gpsimd")

    def op(self, eng, fn, sem=None, n=1, _dma=False):
        inc = 16 if _dma else 1
        if eng in self.CHAINED:
            chname = "chain_" + eng + ("d" if (_dma and eng == "gpsimd") else "")
            other = None
            if eng == "gpsimd":
                other = "chain_gpsimdd" if not _dma else "chain_gpsimd"
            assert not (_dma and eng in ("vector", "scalar")), "no DMAs on vector/scalar"
            ch = self.sem(chname)
            cv = self.cnt[chname]
            self.cnt[chname] = cv + inc
            if sem is not None:
                reg = self.virt.setdefault(sem, {"ch": chname, "marks": []})
                assert reg["ch"] == chname, f"sem {sem} has mixed producers"
                units = (reg["marks"][-1][0] if reg["marks"] else 0) + n
                reg["marks"].append((units, self.cnt[chname]))
            if other is not None and self.cnt.get(other, 0) > self.lastw.get((eng, other), 0):
                oc = self.sem(other)
                ov = self.cnt[other]
                self.lastw[(eng, other)] = ov
                self.q[eng].append(lambda e, s=oc, v=ov: e.wait_ge(s, v))
            def run(e, fn=fn, ch=ch, cv=cv, inc=inc):
                inst = fn(e)
                if cv > 0:
                    inst = inst._wait_ge(ch, cv)
                return inst.then_inc(ch, inc)
            self.q[eng].append(run)
            return None
        if sem is not None:
            assert sem not in self.virt, f"sem {sem} already virtual"
            s = self.sem(sem)
            self.cnt[sem] += n
            self.q[eng].append(lambda e, fn=fn, s=s, n=n: fn(e).then_inc(s, n))
            return self.cnt[sem]
        self.q[eng].append(lambda e, fn=fn: fn(e))
        return None

    def dma(self, eng, fn, sem=None):
        return self.op(eng, fn, sem, 16, _dma=True)

    def base(self, sem):
        if sem in self.virt:
            marks = self.virt[sem]["marks"]
            return marks[-1][0] if marks else 0
        return self.cnt.get(sem, 0)

    def wait(self, eng, sem, val=None):
        if sem in self.virt:
            reg = self.virt[sem]
            marks = reg["marks"]
            if not marks:
                return
            if val is None:
                tgt = marks[-1][1]
            else:
                tgt = None
                for units, chaincum in marks:
                    if units >= val:
                        tgt = chaincum
                        break
                if tgt is None:
                    tgt = marks[-1][1]
            chname = reg["ch"]
            producer_eng = chname.replace("chain_", "").rstrip("d") if chname.endswith("dd") is False else ""
            producer_eng = chname[6:-1] if chname.endswith("d") and chname != "chain_gpsimd" else chname[6:]
            if chname == "chain_gpsimdd":
                producer_eng = "gpsimd"
            ch = self.sem(chname)
            if eng == producer_eng and chname in ("chain_vector", "chain_scalar", "chain_gpsimd"):
                return
            self.q[eng].append(lambda e, s=ch, v=tgt: e.wait_ge(s, v))
            return
        s = self.sem(sem)
        v = self.cnt[sem] if val is None else val
        if v > 0:
            self.q[eng].append(lambda e, s=s, v=v: e.wait_ge(s, v))

    def set_barrier_tiles(self, tiles):
        self._btiles = tiles

    def barrier(self, engines=ENGINES, name="bar"):
        s = self.sem(name)
        tgt = self.cnt[name] + len(engines)
        self.cnt[name] = tgt
        for eng in engines:
            t = self._btiles.get(eng)
            if eng in ("sync", "tensor") or t is None:
                self.q[eng].append(lambda e, s=s: e.nop().then_inc(s, 1))
            else:
                if eng == "scalar":
                    self.op(eng, lambda e, t=t: e.activation(t, t, AF.Identity))
                else:
                    self.op(eng, lambda e, t=t: e.tensor_copy(t, t))
                chname = "chain_" + eng
                ch = self.sem(chname)
                cv = self.cnt[chname]
                self.q[eng].append(lambda e, ch=ch, cv=cv: e.wait_ge(ch, cv))
                self.q[eng].append(lambda e, s=s: e.sem_inc(s, 1))
        for eng in engines:
            self.q[eng].append(lambda e, s=s, v=tgt: e.wait_ge(s, v))

    def build(self):
        with self.nc.Block() as block:
            for eng in ENGINES:
                qe = self.q[eng]
                def body(handle, qe=qe):
                    for fn in qe:
                        fn(handle)
                getattr(block, eng)(body)


class MK:
    def __init__(self, stages, out_half=False, emit_counts=False):
        self.stages = stages
        self.out_half = out_half
        self.emit_counts = emit_counts
        nc = bass.Bass("TRN2", target_bir_lowering=False, debug=False, dynamic_dma_scratch_size=8192)
        self.nc = nc
        self.p = P(nc)
        ctx = self.p._ctx
        self.ctx = ctx
        sbuf = lambda *a: ctx.enter_context(nc.sbuf_tensor(*a))
        psum = lambda *a: ctx.enter_context(nc.psum_tensor(*a))

        ap = {}
        def din(nm, shape, dt=F32):
            ap[nm] = nc.dram_tensor(nm, shape, dt, kind="ExternalInput").ap()
        self.ap = ap

        st = set(stages)
        self.need_l0ffn = bool(st & {"l0ffn_a", "l0ffn_b"})
        self.need_l0attn = "l0attn" in st
        self.need_l1attn = "l1attn" in st
        self.need_l1ffn = bool(st & {"l1ffn_a", "l1ffn_b"})
        self.need_gate = self.need_l1ffn or emit_counts

        din("x0", [T, D])
        if self.need_l0ffn:
            din("l0r_wT", [D, NE]); din("l0r_b", [1, NE])
            din("l0e_wT", [NE, D, D]); din("l0e_b", [NE, D])
        if self.need_l0attn:
            din("l0a_wq", [D, D]); din("l0a_wk", [D, D])
            din("l0a_wv", [D, D]); din("l0a_wo", [D, D])
        if self.need_l1attn:
            din("l1a_wqkvT", [D, 3 * D]); din("l1a_wprojT", [D, D])
        if self.need_gate:
            din("l1f_wg", [D, NE])
        if self.need_l1ffn:
            din("l1f_w1", [NE, D, 4 * D]); din("l1f_b1T", [NE, 128, 32])
            din("l1f_w2", [NE, 4 * D, D]); din("l1f_b2", [NE, D])
            din("priors", [4, NE]); din("selprev", [4, 1])
        self.outs = {}
        rows = T // 2 if out_half else T
        self.outs["xout"] = nc.dram_tensor("xout", [rows, D], F32, kind="ExternalOutput").ap()
        if emit_counts:
            self.outs["counts"] = nc.dram_tensor("counts", [1, NE], F32, kind="ExternalOutput").ap()
        self.halfsel = None
        if out_half:
            din("halfsel", [1, 1], F32)

        dram = lambda nm, shape, dt: nc.dram_tensor(nm, shape, dt, kind="Internal").ap()
        self.xhat_d = dram("xhat_d", [T, D], BF16)
        if self.need_l0ffn:
            self.l0e_bf = dram("l0e_bf", [NE, D, D], BF16)
        if self.need_l0attn or self.need_l1attn:
            self.qfm_d = dram("qfm_d", [D, T], BF16)
            self.kfm_d = dram("kfm_d", [D, T], BF16)
            self.vtm_d = dram("vtm_d", [T, D], BF16)
        if self.need_l0attn:
            self.ktm_d = dram("ktm_d", [T, D], BF16)
            self.l0a_bf = dram("l0a_bf", [4, D, D], BF16)
        if self.need_l1attn:
            self.l1aq_bf = dram("l1aq_bf", [D, 3 * D], BF16)
            self.l1ap_bf = dram("l1ap_bf", [D, D], BF16)
        if self.need_l1ffn:
            self.w1_bf = dram("w1_bf", [NE, D, 4 * D], BF16)
            self.w2_bf = dram("w2_bf", [NE, 4 * D, D], BF16)
            self.b2_bf = dram("b2_bf", [NE, D], BF16)
            self.idx_d = dram("idx_d", [NE * CAP1, 1], I32)
            self.eo_d = dram("eo_d", [NE * CAP1, D], BF16)
            self.ddisp_d = dram("ddisp_d", [CAP1, D], BF16)

        it = nc.inline_tensor
        self.c_ident = it(np.eye(128, dtype=np.float32), "c_ident")
        self.c_ones_row = it(np.ones((1, 128), dtype=np.float32), "c_ones_row")
        self.c_ones_col = it(np.ones((128, 1), dtype=np.float32), "c_ones_col")
        self.c_ustrict = it(np.triu(np.ones((128, 128), dtype=np.float32), 1), "c_ustrict")
        self.c_iota8 = it(np.tile(np.arange(NE, dtype=np.float32), (128, 1)), "c_iota8")
        self.c_iotam = it(np.tile(np.arange(NE, dtype=np.float32) - 999.0, (128, 1)), "c_iotam")
        self.c_tokid = it(np.arange(T, dtype=np.int32).reshape(NCH, 128).T.copy(), "c_tokid")
        self.c_senti = it(np.full((NE * CAP1, 1), 1 << 29, dtype=np.int32), "c_senti")

        self.XT = sbuf("XT", [128, NCH, D], F32)
        self.FM = sbuf("FM", [128, KC, T], BF16)
        import os as _os
        _wb = int(_os.environ.get("DEVK_WB", "512"))
        self.wbuf = sbuf("wbuf", [128, KC, _wb], BF16)
        self._xtmb_ded = sbuf("xtmb_d", [128, 2, D], BF16)
        self.lncol = sbuf("lncol", [128, 6, NCH], F32)
        self.gaux = sbuf("gaux", [128, 4, NCH], F32)
        self.rw8 = sbuf("rw8", [128, NCH, NE], F32)
        self.rwT1 = sbuf("rwT1", [8, 128], BF16)
        self.rwb1 = sbuf("rwb1", [128, NE], BF16)
        self.bexp_b = sbuf("bexp_b", [NE, D], BF16)
        self.oh_f = sbuf("oh_f", [128, NE], F32)
        self.scr8 = sbuf("scr8", [128, NE], F32)
        self.gsc = sbuf("gsc", [128, NCH], F32)
        self.slot_f = sbuf("slot_f", [128, NCH], F32)
        self.slot_i = sbuf("slot_i", [128, NCH], I32)
        self.slot2_i = sbuf("slot2_i", [128, NCH], I32)
        self.idxc64 = sbuf("idxc64", [128, NE * KCAP], I32)
        self.idx1 = sbuf("idx1", [128, 1], I32)
        self.poscol = sbuf("poscol", [128, 4], F32)
        self.posi = sbuf("posi", [128, 1], I32)
        self.carry = sbuf("carry", [1, NE], F32)
        self.offrow = sbuf("offrow", [1, NE], F32)
        self.offb = sbuf("offb", [128, NE], F32)
        self.selprev_s = sbuf("selprev_s", [4, 1], F32)
        self.priors_s = sbuf("priors_s", [4, NE], F32)
        self.epscol = sbuf("epscol", [128, 1], F32)
        self.negC = sbuf("negC", [128, 1], F32)
        self.onesr_b = sbuf("onesr_b", [1, 128], BF16)
        self.onesc_b = sbuf("onesc_b", [128, 1], BF16)
        self.onesr_f = sbuf("onesr_f", [1, 128], F32)
        self.onesc_f = sbuf("onesc_f", [128, 1], F32)
        self.ident_b = sbuf("ident_b", [128, 128], BF16)
        self.ustr_b = sbuf("ustr_b", [128, 128], BF16)
        self.iota8_f = sbuf("iota8_f", [128, NE], F32)
        self.big8_f = sbuf("big8_f", [128, NE], F32)
        self.iotam_f = sbuf("iotam_f", [128, NE], F32)
        self.tokid_s = sbuf("tokid_s", [128, NCH], I32)
        self.smallb = sbuf("smallb", [128, KC, NE], BF16)
        self.wgb = sbuf("wgb", [128, KC, NE], BF16)
        self.l0rb_b = sbuf("l0rb_b", [1, NE], BF16)
        self.b1c = sbuf("b1c", [128, 32], F32)
        self.b2sm = sbuf("b2sm", [1, 512], BF16)
        self.idxc = sbuf("idxc", [128, 8], I32)
        self.cnt_s = sbuf("cnt_s", [1, NE], F32)
        self.btile = sbuf("btile", [128, 4], F32)
        self.p.set_barrier_tiles({
            "vector": self.btile[0:1, 0:1],
            "scalar": self.btile[0:1, 1:2],
            "gpsimd": self.btile[0:1, 2:3],
        })

        self.PS = [psum(f"ps{i}", [128, 512], F32) for i in range(4)]
        self.PSB = psum("psb", [128, 1024], F32)
        self.PT = psum("pt", [128, 128], BF16)
        self.PSM = psum("psm", [128, 16], F32)

        if self._xtmb_ded is not None:
            self.xtmb = self._xtmb_ded[:]
        else:
            self.xtmb = self.wbuf[:].rearrange("p k n -> p (k n)")[:, 0:2048].rearrange(
                "p (b n) -> p b n", b=2)
        self.gl = self.PSB[:, 0:256].rearrange("p (n e) -> p n e", e=NE)
        fmf = self.FM[:].rearrange("p k t -> p (k t)")
        self.fmf = fmf
        def fv(off, n, spec=None, **kw):
            v = fmf[:, off:off + n]
            if spec:
                v = v.rearrange(spec, **kw)
            return v
        self.v_qf = fv(0, 4096, "p (k t) -> p k t", t=512)
        self.v_kf = fv(4096, 8192, "p (b k t) -> p b k t", b=2, t=512)
        self.v_vt = fv(12288, 8192, "p (b a o) -> p b a o", b=2, o=D)
        self.v_kt = fv(20480, 4096, "p (a o) -> p a o", o=D)
        self.v_af = fv(24576, 4096, "p (k t) -> p k t", t=512)
        self.v_at = fv(28672, 1024, "p (b t) -> p b t", t=512)
        self.v_ysb = fv(29696, 512)
        self.v_rrb = fv(30208, 512)
        self.statef = fv(30720, 2048).bitcast(F32).rearrange("p (k d) -> p k d", d=DH)
        self.stateb = fv(29696, 512).rearrange("p (k d) -> p k d", d=DH)
        self.fv = fv
        self.v_vt65 = fv(12288, 2080, "p (b a h w) -> p b a h w", b=2, a=4, h=4)
        self.v_dfm = fv(0, 4096, "p (k t) -> p k t", t=512)
        self.wbufB = fv(4096, 4096, "p (k t) -> p k t", t=512)
        self.v_hfm = fv(8192, 16384, "p (k t) -> p k t", t=512)
        self.v_gath = fv(24576, 2048, "p (b t) -> p b t", t=1024)
        self.v_eob = fv(26624, 1024, "p (b t) -> p b t", t=512)

    def mm(self, out, lhsT, rhs, start=True, stop=True, sem=None):
        return self.p.op("tensor", lambda e: e.matmul(out, lhsT, rhs, start=start, stop=stop), sem)

    def emit_prep(self):
        p = self.p
        def _mkregs(e):
            self.bregs = {"tok": e.to_reg(T - 1), "slot": e.to_reg(NE * CAP1 - 1),
                          "half": e.to_reg(T // 2 - 1)}
        p.q["gpsimd"].append(_mkregs)
        p.dma("sync", lambda e: e.dma_start(out=self.onesr_f[:], in_=self.c_ones_row.ap()), "ld0")
        p.dma("sync", lambda e: e.dma_start(out=self.onesc_f[:], in_=self.c_ones_col.ap()), "ld0")
        p.dma("gpsimd", lambda e: e.dma_start(out=self.ustr_b[:], in_=self.c_ustrict.ap()), "ld0g")
        p.dma("sync", lambda e: e.dma_start(out=self.iota8_f[:], in_=self.c_iota8.ap()), "ld0")
        p.dma("sync", lambda e: e.dma_start(out=self.iotam_f[:], in_=self.c_iotam.ap()), "ld0")
        p.dma("sync", lambda e: e.dma_start(out=self.tokid_s[:], in_=self.c_tokid.ap()), "ld0")
        p.dma("gpsimd", lambda e: e.dma_start(out=self.onesr_b[:], in_=self.c_ones_row.ap()), "ld0g")
        p.dma("gpsimd", lambda e: e.dma_start(out=self.onesc_b[:], in_=self.c_ones_col.ap()), "ld0g")
        p.dma("gpsimd", lambda e: e.dma_start(out=self.ident_b[:], in_=self.c_ident.ap()), "ld0g")
        p.op("vector", lambda e: e.memset(self.btile[:, 0:1], 0.0))
        p.op("scalar", lambda e: e.memzero(self.btile[:, 1:2]))
        p.op("gpsimd", lambda e: e.memset(self.btile[:, 2:3], 0.0))
        p.op("vector", lambda e: e.memset(self.epscol[:], 1e-5))
        p.op("vector", lambda e: e.memset(self.negC[:], -EXPC))
        p.op("vector", lambda e: e.memset(self.big8_f[:], 999.0))
        for c in range(NCH):
            p.dma("sync", lambda e, c=c: e.dma_start(
                out=self.XT[:, c, :], in_=self.ap["x0"][c * 128:(c + 1) * 128, :]), "ldx")
        if self.need_l1ffn:
            p.dma("sync", lambda e: e.dma_start(out=self.priors_s[:], in_=self.ap["priors"]), "ld0")
            p.dma("sync", lambda e: e.dma_start(out=self.selprev_s[:], in_=self.ap["selprev"]), "ld0")
        p.wait("sync", "ldx")
        p.wait("sync", "ld0")
        p.wait("gpsimd", "ld0g")
        p.barrier(name="prep")
        if self.need_l0ffn:
            p.dma("gpsimd", lambda e: e.dma_start(
                out=self.smallb[:],
                in_=self.ap["l0r_wT"].rearrange("(k p) e -> p k e", p=128)), "wsm")
            p.dma("gpsimd", lambda e: e.dma_start(out=self.l0rb_b[:], in_=self.ap["l0r_b"]), "wsm")
            p.dma("gpsimd", lambda e: e.dma_start(out=self.bexp_b[:], in_=self.ap["l0e_b"]), "wsm")
        if self.need_gate:
            p.dma("gpsimd", lambda e: e.dma_start(
                out=self.wgb[:],
                in_=self.ap["l1f_wg"].rearrange("(k p) e -> p k e", p=128)), "wsm2")
        if self.need_l1ffn:
            p.dma("gpsimd", lambda e: e.dma_start(out=self.b2_bf, in_=self.ap["l1f_b2"]), "wsm2")
        if self.need_l0ffn:
            for i in range(NE):
                p.dma("gpsimd", lambda e, i=i: e.dma_start(
                    out=self.l0e_bf[i], in_=self.ap["l0e_wT"][i]), "wl0e")
        if self.need_l0attn:
            for i, nm in enumerate(("l0a_wq", "l0a_wk", "l0a_wv", "l0a_wo")):
                p.dma("gpsimd", lambda e, i=i, nm=nm: e.dma_start(
                    out=self.l0a_bf[i], in_=self.ap[nm]), "wl0a")
        if self.need_l1attn:
            p.dma("gpsimd", lambda e: e.dma_start(out=self.l1aq_bf, in_=self.ap["l1a_wqkvT"]), "wl1a")
            p.dma("gpsimd", lambda e: e.dma_start(out=self.l1ap_bf, in_=self.ap["l1a_wprojT"]), "wl1a")
        if self.need_l1ffn:
            for i in range(NE):
                p.dma("gpsimd", lambda e, i=i: e.dma_start(
                    out=self.w1_bf[i], in_=self.ap["l1f_w1"][i]), "ww1")
                p.dma("gpsimd", lambda e, i=i: e.dma_start(
                    out=self.w2_bf[i], in_=self.ap["l1f_w2"][i]), "ww2")

    def emit_ln(self, tag):
        p = self.p
        blt = [p.base("lt0"), p.base("lt1")]
        for c in range(NCH):
            p.op("vector", lambda e, c=c: e.reduce_sum(
                self.lncol[:, 0, c:c + 1], self.XT[:, c, :], axis=AX.X), "lv")
            p.op("scalar", lambda e, c=c: e.activation(
                self.PSB[:, :], self.XT[:, c, :], AF.Square,
                accum_out=self.lncol[:, 1, c:c + 1]), "ls")
        p.wait("vector", "ls")
        p.op("vector", lambda e: e.tensor_scalar(
            self.lncol[:, 2, :], self.lncol[:, 0, :], 1.0 / D, None, ALU.mult))
        p.op("vector", lambda e: e.tensor_tensor(
            self.lncol[:, 3, :], self.lncol[:, 2, :], self.lncol[:, 2, :], ALU.mult))
        p.op("vector", lambda e: e.scalar_tensor_tensor(
            self.lncol[:, 4, :], self.lncol[:, 1, :], 1.0 / D, self.lncol[:, 3, :],
            ALU.mult, ALU.subtract), "lv")
        p.wait("scalar", "lv")
        p.op("scalar", lambda e: e.activation(
            self.lncol[:, 5, :], self.lncol[:, 4, :], AF.Sqrt, bias=self.epscol[:]), "ls")
        p.wait("vector", "ls")
        p.op("vector", lambda e: e.reciprocal(self.lncol[:, 4, :], self.lncol[:, 5, :]))
        for c in range(NCH):
            if c >= 2:
                p.wait("vector", f"lt{c % 2}", blt[c % 2] + 16 * (c // 2))
            p.op("vector", lambda e, c=c: e.tensor_scalar(
                self.xtmb[:, c % 2, :], self.XT[:, c, :],
                self.lncol[:, 2, c:c + 1], self.lncol[:, 4, c:c + 1],
                ALU.subtract, ALU.mult), "ln")
            p.wait("sync", "ln")
            p.dma("sync", lambda e, c=c: e.dma_start(
                out=self.xhat_d[c * 128:(c + 1) * 128, :], in_=self.xtmb[:, c % 2, :]),
                f"lt{c % 2}")
        p.wait("sync", "lt0")
        p.wait("sync", "lt1")
        for k in range(KC):
            p.dma("sync", lambda e, k=k: e.dma_start(
                out=self.FM[:, k, :], in_=self.xhat_d[:, k * 128:(k + 1) * 128],
                transpose=True), "lf")
        p.wait("sync", "lf")
        p.barrier(name="b")

    def emit_l0ffn(self, tag):
        p = self.p
        PS, PSM = self.PS, self.PSM
        bge = p.base("ge"); bbbe = p.base("bbe"); bew = p.base("ew")
        p.wait("tensor", "wsm", 16 * 2)
        for c in range(NCH):
            if c >= 1:
                p.wait("tensor", "ge", bge + c)
            for k in range(KC):
                self.mm(PSM[:, 0:8], self.FM[:, k, c * 128:(c + 1) * 128],
                        self.smallb[:, k, :], start=(k == 0), stop=False)
            self.mm(PSM[:, 0:8], self.onesr_b[:], self.l0rb_b[:], start=False, stop=True,
                    sem="gm")
            p.wait("scalar", "gm")
            p.op("scalar", lambda e, c=c: e.activation(
                self.gl[:, c, :], PSM[:, 0:8], AF.Exp), "ge")
        p.wait("vector", "ge")
        p.op("vector", lambda e: e.tensor_reduce(
            self.gaux[:, 0, :].rearrange("p (n o) -> p n o", o=1), self.gl, AX.X, ALU.add))
        p.op("vector", lambda e: e.reciprocal(self.gaux[:, 1, :], self.gaux[:, 0, :]))
        for c in range(NCH):
            p.op("vector", lambda e, c=c: e.max(self.scr8[:], self.gl[:, c, :]))
            p.op("vector", lambda e, c=c: e.tensor_scalar(
                self.oh_f[:], self.gl[:, c, :], self.scr8[:, 1:2], None, ALU.is_ge))
            p.op("vector", lambda e, c=c: e.tensor_scalar(
                self.rw8[:, c, :], self.gl[:, c, :], self.gaux[:, 1, c:c + 1], 0.5,
                ALU.mult, ALU.mult))
            p.op("vector", lambda e, c=c: e.tensor_tensor(
                self.rw8[:, c, :], self.rw8[:, c, :], self.oh_f[:], ALU.mult))
        p.wait("tensor", "wsm")
        for c in range(NCH):
            p.op("vector", lambda e, c=c: e.tensor_copy(
                self.rwb1[:], self.rw8[:, c, :]), "gw")
            p.wait("tensor", "gw")
            p.op("tensor", lambda e, c=c: e.transpose(
                self.PT[0:8, :], self.rwb1[:], self.ident_b[:]), "bt")
            p.wait("vector", "bt")
            p.op("vector", lambda e: e.tensor_copy(self.rwT1[:], self.PT[0:8, :]), "bte")
            p.wait("tensor", "bte")
            for s in range(2):
                slot = c * 2 + s
                bank = PS[slot % 2]
                if slot >= 2:
                    p.wait("tensor", "bbe", bbbe + slot - 1)
                self.mm(bank[:, :], self.rwT1[:], self.bexp_b[:, s * 512:(s + 1) * 512],
                        sem="bbm")
                p.wait("vector", "bbm")
                p.op("vector", lambda e, c=c, s=s, bank=bank: e.tensor_tensor(
                    self.XT[:, c, s * 512:(s + 1) * 512], self.XT[:, c, s * 512:(s + 1) * 512],
                    bank[:, :], ALU.add), "bbe")
        p.wait("sync", "wl0e")
        for i in range(NE):
            for half in range(2):
                slot = i * 2 + half
                if slot >= 1:
                    p.wait("sync", "ew", bew + slot * NCH)
                p.dma("sync", lambda e, i=i, h=half: e.dma_start(
                    out=self.wbuf[:],
                    in_=self.l0e_bf[i, :, h * 512:(h + 1) * 512].rearrange(
                        "(k p) n -> p k n", p=128)), "el")
                p.wait("tensor", "el")
                for c in range(NCH):
                    pb = PS[2 + (c % 2)]
                    if c >= 2:
                        p.wait("tensor", "ew", bew + slot * NCH + c - 1)
                    for k in range(KC):
                        self.mm(pb[:, :], self.FM[:, k, c * 128:(c + 1) * 128],
                                self.wbuf[:, k, :], start=(k == 0), stop=(k == KC - 1),
                                sem=("em" if k == KC - 1 else None))
                    p.wait("vector", "em")
                    p.op("vector", lambda e, c=c, h=half, pb=pb, i=i: e.scalar_tensor_tensor(
                        self.XT[:, c, h * 512:(h + 1) * 512], pb[:, :],
                        self.rw8[:, c, i:i + 1],
                        self.XT[:, c, h * 512:(h + 1) * 512],
                        ALU.mult, ALU.add), "ew")
        p.wait("vector", "ew")
        p.barrier(name="bf")

    def emit_finish(self):
        p = self.p
        if self.out_half:
            raise NotImplementedError
        for c in range(NCH):
            p.dma("sync", lambda e, c=c: e.dma_start(
                out=self.outs["xout"][c * 128:(c + 1) * 128, :], in_=self.XT[:, c, :]), "fst")
        p.wait("sync", "fst")

    def emit_dump_dram(self, src_ap):
        p = self.p
        for c in range(NCH):
            p.dma("sync", lambda e, c=c: e.dma_start(
                out=self.xtmb[:, 0, :], in_=src_ap[c * 128:(c + 1) * 128, :]), "dmp1")
            p.wait("vector", "dmp1")
            p.op("vector", lambda e, c=c: e.tensor_copy(self.XT[:, c, :], self.xtmb[:, 0, :]), "dmp2")
            p.wait("sync", "dmp2")
            p.dma("sync", lambda e, c=c: e.dma_start(
                out=self.outs["xout"][c * 128:(c + 1) * 128, :], in_=self.XT[:, c, :]), "fst")
            p.wait("sync", "fst")
        p.wait("sync", "fst")

    def emit_dump_sbuf_f32(self, get_slice, nch=NCH):
        p = self.p
        for c in range(nch):
            p.dma("sync", lambda e, c=c: e.dma_start(
                out=self.outs["xout"][c * 128:(c + 1) * 128, :], in_=get_slice(c)), "fst")
        p.wait("sync", "fst")

    def emit_all(self):
        import os
        dump = os.environ.get("DEVK_DUMP", "")
        self.emit_prep()
        for s in self.stages:
            self.emit_ln(s)
            if dump == "xhat":
                self.emit_dump_dram(self.xhat_d)
                self.p.build()
                return self.nc
            if dump == "fm":
                xv = self.outs["xout"].rearrange("(k a p) d -> (k) (a p d)", k=KC, a=4)
                for k in range(KC):
                    self.p.dma("gpsimd", lambda e, k=k: e.dma_start(
                        out=xv[k:k + 1, :].rearrange("a (p t) -> (a p) t", p=128),
                        in_=self.FM[:, k, :]), "fst")
                self.p.wait("gpsimd", "fst")
                self.p.build()
                return self.nc
            if s in ("l0ffn_a", "l0ffn_b"):
                self.emit_l0ffn(s)
            elif s == "l0attn":
                self.emit_l0attn(s)
            elif s == "l1attn":
                self.emit_l1attn(s)
            elif s in ("l1ffn_a", "l1ffn_b"):
                self.emit_l1ffn(s)
            elif s == "l1gate":
                self.emit_l1ffn(s, full=False)
        self.emit_finish()
        self.p.build()
        return self.nc


def host_inputs(d, batch):
    m = {"x0": np.ascontiguousarray(np.asarray(d["L0_attn_X"])[batch])}
    m["l0r_wT"] = np.ascontiguousarray(np.asarray(d["L0_ffn_W_router"]).T)
    m["l0r_b"] = np.ascontiguousarray(np.asarray(d["L0_ffn_b_router"])[None, :])
    m["l0e_wT"] = np.ascontiguousarray(np.asarray(d["L0_ffn_W_experts"]).transpose(0, 2, 1))
    m["l0e_b"] = np.ascontiguousarray(np.asarray(d["L0_ffn_b_experts"]))
    m["l0a_wq"] = np.ascontiguousarray(np.asarray(d["L0_attn_Wq"]))
    m["l0a_wk"] = np.ascontiguousarray(np.asarray(d["L0_attn_Wk"]))
    m["l0a_wv"] = np.ascontiguousarray(np.asarray(d["L0_attn_Wv"]))
    m["l0a_wo"] = np.ascontiguousarray(np.asarray(d["L0_attn_Wo"]))
    m["l1a_wqkvT"] = np.ascontiguousarray(np.asarray(d["L1_attn_W_qkv"]).T)
    m["l1a_wprojT"] = np.ascontiguousarray(np.asarray(d["L1_attn_W_proj"]).T)
    m["l1f_wg"] = np.ascontiguousarray(np.asarray(d["L1_ffn_wg"]))
    m["l1f_w1"] = np.ascontiguousarray(np.asarray(d["L1_ffn_w1"]))
    m["l1f_b1T"] = np.ascontiguousarray(
        np.asarray(d["L1_ffn_b1"]).reshape(8, 32, 128).transpose(0, 2, 1))
    m["l1f_w2"] = np.ascontiguousarray(np.asarray(d["L1_ffn_w2"]))
    m["l1f_b2"] = np.ascontiguousarray(np.asarray(d["L1_ffn_b2"]))
    return m


def _proj_fm_out(self, tag, wsrc_d, col0, dst_d):
    p = self.p
    PS = self.PS
    bpae = p.base("pae"); bpas = [p.base("pas0"), p.base("pas1")]
    for oc in range(8):
        p.wait("sync", "pam")
        p.dma("sync", lambda e, oc=oc: e.dma_start(
            out=self.wbuf[:, :, 0:128],
            in_=wsrc_d[:, col0 + oc * 128:col0 + (oc + 1) * 128].rearrange(
                "(k p) n -> p k n", p=128)), "pal")
        p.wait("tensor", "pal")
        for ts in range(8):
            slot = oc * 8 + ts
            bank = PS[slot % 2]
            if slot >= 2:
                p.wait("tensor", "pae", bpae + slot - 1)
            for k in range(KC):
                self.mm(bank[:, :], self.wbuf[:, k, 0:128],
                        self.FM[:, k, ts * 512:(ts + 1) * 512],
                        start=(k == 0), stop=(k == KC - 1),
                        sem=("pam" if k == KC - 1 else None))
            p.wait("scalar", "pam")
            if slot >= 2:
                p.wait("scalar", f"pas{slot % 2}", bpas[slot % 2] + 16 * (slot // 2))
            p.op("scalar", lambda e, slot=slot, bank=bank: e.activation(
                self.xtmb[:, slot % 2, 0:512], bank[:, :], AF.Identity), "pae")
            p.wait("sync", "pae")
            p.dma("sync", lambda e, oc=oc, ts=ts, slot=slot: e.dma_start(
                out=dst_d[oc * 128:(oc + 1) * 128, ts * 512:(ts + 1) * 512],
                in_=self.xtmb[:, slot % 2, 0:512]), f"pas{slot % 2}")
    p.wait("sync", "pam")
    p.wait("sync", "pas0")
    p.wait("sync", "pas1")


def _proj_tm_out(self, tag, wsrc_d, col0, width, dst_d):
    p = self.p
    PS = self.PS
    bpte = p.base("pte"); bpts = [p.base("pts0"), p.base("pts1")]
    for half in range(width // 512):
        p.wait("sync", "ptm")
        p.dma("sync", lambda e, half=half: e.dma_start(
            out=self.wbuf[:],
            in_=wsrc_d[:, col0 + half * 512:col0 + (half + 1) * 512].rearrange(
                "(k p) n -> p k n", p=128)), "ptl")
        p.wait("tensor", "ptl")
        for c in range(NCH):
            slot = half * NCH + c
            bank = PS[2 + slot % 2]
            if slot >= 2:
                p.wait("tensor", "pte", bpte + slot - 1)
            for k in range(KC):
                self.mm(bank[:, :], self.FM[:, k, c * 128:(c + 1) * 128],
                        self.wbuf[:, k, :], start=(k == 0), stop=(k == KC - 1),
                        sem=("ptm" if k == KC - 1 else None))
            p.wait("scalar", "ptm")
            if slot >= 2:
                p.wait("scalar", f"pts{slot % 2}", bpts[slot % 2] + 16 * (slot // 2))
            p.op("scalar", lambda e, slot=slot, bank=bank: e.activation(
                self.xtmb[:, slot % 2, 512:1024], bank[:, :], AF.Identity), "pte")
            p.wait("sync", "pte")
            p.dma("sync", lambda e, c=c, half=half, slot=slot: e.dma_start(
                out=dst_d[c * 128:(c + 1) * 128, half * 512:(half + 1) * 512],
                in_=self.xtmb[:, slot % 2, 512:1024]), f"pts{slot % 2}")
    p.wait("sync", "ptm")
    p.wait("sync", "pts0")
    p.wait("sync", "pts1")


def emit_l0attn(self, tag):
    p = self.p
    PS, PSB = self.PS, self.PSB
    bse = p.base("se"); bye = p.base("ye"); bsup = p.base("sup")
    bavm = p.base("avm"); bpje = p.base("pje")
    p.wait("sync", "wl0a")
    _proj_fm_out(self, "q" + tag, self.l0a_bf[0], 0, self.qfm_d)
    _proj_fm_out(self, "k" + tag, self.l0a_bf[1], 0, self.kfm_d)
    _proj_tm_out(self, "v" + tag, self.l0a_bf[2], 0, D, self.vtm_d)
    _proj_tm_out(self, "kt" + tag, self.l0a_bf[1], 0, D, self.ktm_d)
    p.barrier(name="ba")
    p.op("vector", lambda e: e.memset(self.statef, 0.0))
    p.op("vector", lambda e: e.memset(self.stateb, 0.0), "stz")
    p.wait("tensor", "stz")
    g_se = g_ye = g_sup = g_av = g_pje = 0
    for sc in range(NSC):
        if sc >= 1:
            p.wait("sync", "pjm")
            p.wait("tensor", "pje")
        p.dma("sync", lambda e, sc=sc: e.dma_start(
            out=self.v_qf, in_=self.qfm_d[:, sc * 512:(sc + 1) * 512].rearrange(
                "(k p) t -> p k t", p=128)), "bl")
        p.dma("sync", lambda e, sc=sc: e.dma_start(
            out=self.v_kf[:, 0], in_=self.kfm_d[:, sc * 512:(sc + 1) * 512].rearrange(
                "(k p) t -> p k t", p=128)), "bl")
        p.dma("sync", lambda e, sc=sc: e.dma_start(
            out=self.v_vt[:, 0], in_=self.vtm_d[sc * 512:(sc + 1) * 512, :].rearrange(
                "(a p) o -> p a o", p=128)), "bl")
        p.dma("sync", lambda e, sc=sc: e.dma_start(
            out=self.v_kt, in_=self.ktm_d[sc * 512:(sc + 1) * 512, :].rearrange(
                "(a p) o -> p a o", p=128)), "bl")
        p.wait("tensor", "bl")
        p.wait("tensor", "sbu")
        for h in range(H):
            hb = (h % 2) * 64
            hk = h // 2
            gh = sc * H + h
            psy = PS[h % 2]
            if gh >= 2:
                p.wait("tensor", "ye", bye + gh - 1)
            self.mm(psy[0:64, :], self.stateb[hb:hb + 64, hk, :],
                    self.v_qf[hb:hb + 64, hk, :], start=True, stop=False)
            for kc in range(4):
                gs = g_se
                g_se += 1
                pss = PS[2 + gs % 2]
                if gs >= 2:
                    p.wait("tensor", "se", bse + gs - 1)
                self.mm(pss[:, :],
                        self.v_kf[hb:hb + 64, 0, hk, kc * 128:(kc + 1) * 128],
                        self.v_qf[hb:hb + 64, hk, :], start=True, stop=True,
                        sem="sm")
                p.wait("scalar", "sm")
                if g_av >= 2:
                    p.wait("scalar", "avm", bavm + g_av - 1)
                p.op("scalar", lambda e, gs=gs, pss=pss: e.activation(
                    self.v_at[:, gs % 2, :], pss[:, :], AF.Identity), "se")
                p.wait("gpsimd", "se")
                p.op("gpsimd", lambda e, kc=kc, gs=gs: e.affine_select(
                    out=self.v_at[:, gs % 2, :], in_=self.v_at[:, gs % 2, :],
                    pattern=[[1, 512]], compare_op=ALU.is_ge, fill=0.0,
                    base=-kc * 128, channel_multiplier=-1), "msk")
                p.wait("tensor", "msk")
                g_av += 1
                self.mm(psy[0:64, :], self.v_vt[:, 0, kc, h * 64:(h + 1) * 64],
                        self.v_at[:, gs % 2, :], start=False, stop=(kc == 3),
                        sem="avm")
            p.wait("scalar", "avm")
            p.op("scalar", lambda e, hb=hb, hk=hk, psy=psy: e.activation(
                self.v_af[hb:hb + 64, hk, :], psy[0:64, :], AF.Identity), "ye")
            if gh >= 1:
                p.wait("tensor", "sup", bsup + gh)
            for kc in range(4):
                self.mm(PSB[0:64, 0:64], self.v_kt[:, kc, h * 64:(h + 1) * 64],
                        self.v_vt[:, 0, kc, h * 64:(h + 1) * 64],
                        start=(kc == 0), stop=(kc == 3),
                        sem=("sum" if kc == 3 else None))
            p.wait("vector", "sum")
            p.op("vector", lambda e, hb=hb, hk=hk: e.tensor_tensor(
                self.statef[hb:hb + 64, hk, :], self.statef[hb:hb + 64, hk, :],
                PSB[0:64, 0:64], ALU.add), "sup")
            p.op("vector", lambda e, hb=hb, hk=hk: e.tensor_copy(
                self.stateb[hb:hb + 64, hk, :], self.statef[hb:hb + 64, hk, :]),
                "sbu")
        p.wait("tensor", "ye")
        p.wait("tensor", "se")
        for s in range(2):
            p.wait("sync", "pjm")
            p.dma("sync", lambda e, s=s: e.dma_start(
                out=self.wbuf[:],
                in_=self.l0a_bf[3][:, s * 512:(s + 1) * 512].rearrange(
                    "(k p) n -> p k n", p=128)), "pjl")
            p.wait("tensor", "pjl")
            for tc in range(4):
                gp = g_pje
                g_pje += 1
                bank = PS[2 + gp % 2]
                if gp >= 2:
                    p.wait("tensor", "pje", bpje + gp - 1)
                for k in range(KC):
                    self.mm(bank[:, :], self.v_af[:, k, tc * 128:(tc + 1) * 128],
                            self.wbuf[:, k, :], start=(k == 0), stop=(k == KC - 1),
                            sem=("pjm" if k == KC - 1 else None))
                p.wait("vector", "pjm")
                p.op("vector", lambda e, sc=sc, tc=tc, s=s, bank=bank: e.tensor_tensor(
                    self.XT[:, sc * 4 + tc, s * 512:(s + 1) * 512],
                    self.XT[:, sc * 4 + tc, s * 512:(s + 1) * 512],
                    bank[:, :], ALU.add), "pje")
    p.wait("vector", "pje")
    p.barrier(name="bb")


MK.emit_l0attn = emit_l0attn


def emit_l1attn(self, tag):
    p = self.p
    PS, PSB = self.PS, self.PSB
    bse = p.base("se"); bavs = p.base("avs"); bpje = p.base("pje")
    p.wait("sync", "wl1a")
    _proj_fm_out(self, "q" + tag, self.l1aq_bf, 0, self.qfm_d)
    _proj_fm_out(self, "k" + tag, self.l1aq_bf, D, self.kfm_d)
    _proj_tm_out(self, "v" + tag, self.l1aq_bf, 2 * D, D, self.vtm_d)
    p.barrier(name="ba")
    p.op("vector", lambda e: e.memset(self.v_vt65[:, :, :, :, 64:65], 1.0), "vo")
    p.wait("tensor", "vo")
    rr32 = self.statef[:].rearrange("p a b -> p (a b)")[0:1, 0:512]
    g_se = g_pje = g_kl = 0
    for sc in range(NSC):
        if sc >= 1:
            p.wait("sync", "pjm")
            p.wait("tensor", "pje")
        p.dma("sync", lambda e, sc=sc: e.dma_start(
            out=self.v_qf, in_=self.qfm_d[:, sc * 512:(sc + 1) * 512].rearrange(
                "(k p) t -> p k t", p=128)), "bl")
        p.wait("tensor", "bl")
        for g in range(4):
            p.wait("tensor", "yse")
            for ksc in range(sc + 1):
                par = g_kl % 2
                g_kl += 1
                if g_kl >= 3:
                    p.wait("sync", "sm", None)
                p.dma("sync", lambda e, g=g, ksc=ksc, par=par: e.dma_start(
                    out=self.v_kf[:, par, 0:2, :],
                    in_=self.kfm_d[256 * g:256 * (g + 1),
                                    ksc * 512:(ksc + 1) * 512].rearrange(
                        "(k p) t -> p k t", p=128)), f"kl{par}")
            
                for kc4 in range(4):
                    p.dma("sync", lambda e, g=g, ksc=ksc, par=par, kc4=kc4: e.dma_start(
                        out=self.v_vt65[:, par, kc4, :, 0:64],
                        in_=self.vtm_d[ksc * 512 + kc4 * 128:ksc * 512 + (kc4 + 1) * 128,
                                        256 * g:256 * (g + 1)].rearrange(
                            "p (h w) -> p h w", w=64)), f"vl{par}")
                p.wait("tensor", f"kl{par}")
                p.wait("tensor", f"vl{par}")
                for hi in range(4):
                    h = 4 * g + hi
                    hb = (h % 2) * 64
                    hkk = hi // 2
                    psy = PS[hi]
                    for kc in range(4):
                        gs = g_se
                        g_se += 1
                        pss = PSB[:, (gs % 2) * 512:(gs % 2 + 1) * 512]
                        if gs >= 2:
                            p.wait("tensor", "se", bse + gs - 1)
                        self.mm(pss, self.v_kf[hb:hb + 64, par, hkk,
                                                kc * 128:(kc + 1) * 128],
                                self.v_qf[hb:hb + 64, h // 2, :],
                                start=True, stop=True, sem="sm")
                        p.wait("scalar", "sm")
                        if g_se >= 3:
                            p.wait("scalar", "avs", bavs + g_se - 2)
                        p.op("scalar", lambda e, gs=gs, pss=pss: e.activation(
                            self.v_at[:, gs % 2, :], pss, AF.Exp,
                            bias=self.negC[:], scale=0.125), "se")
                        if ksc == sc:
                            p.wait("gpsimd", "se")
                            p.op("gpsimd", lambda e, kc=kc, gs=gs: e.affine_select(
                                out=self.v_at[:, gs % 2, :], in_=self.v_at[:, gs % 2, :],
                                pattern=[[1, 512]], compare_op=ALU.is_ge, fill=0.0,
                                base=-kc * 128, channel_multiplier=-1), "msk")
                            p.wait("tensor", "msk")
                        else:
                            p.wait("tensor", "se")
                        first = (ksc == 0 and kc == 0)
                        self.mm(psy[0:65, :],
                                self.v_vt65[:, par, kc, hi, :],
                                self.v_at[:, gs % 2, :], start=first,
                                stop=(ksc == sc and kc == 3), sem="avs")
            for hi in range(4):
                h = 4 * g + hi
                hb = (h % 2) * 64
                psy = PS[hi]
                p.wait("vector", "avs")
                p.op("vector", lambda e, psy=psy, rr32=rr32: e.reciprocal(
                    rr32, psy[64:65, :]))
                p.op("vector", lambda e, rr32=rr32: e.tensor_copy(
                    self.v_rrb[0:1, :], rr32), "rrb")
                p.wait("tensor", "rrb")
                p.wait("tensor", "se")
                self.mm(PSB[:, 0:512], self.onesr_b[:], self.v_rrb[0:1, 0:512],
                        start=True, stop=True, sem="bcm")
                p.wait("scalar", "bcm")
                p.op("scalar", lambda e, psy=psy: e.activation(
                    self.v_ysb[0:64, :], psy[0:64, :], AF.Identity), "yse")
                p.wait("vector", "yse")
                p.op("vector", lambda e, hb=hb, h=h: e.tensor_tensor(
                    self.v_af[hb:hb + 64, h // 2, :], self.v_ysb[0:64, :],
                    PSB[0:64, 0:512], ALU.mult), "afw")
                p.wait("tensor", "afw")
        p.wait("tensor", "afw")
        for s in range(2):
            p.wait("sync", "pjm")
            p.dma("sync", lambda e, s=s: e.dma_start(
                out=self.wbuf[:],
                in_=self.l1ap_bf[:, s * 512:(s + 1) * 512].rearrange(
                    "(k p) n -> p k n", p=128)), "pjl")
            p.wait("tensor", "pjl")
            for tc in range(4):
                gp = g_pje
                g_pje += 1
                bank = PS[2 + gp % 2]
                if gp >= 2:
                    p.wait("tensor", "pje", bpje + gp - 1)
                for k in range(KC):
                    self.mm(bank[:, :], self.v_af[:, k, tc * 128:(tc + 1) * 128],
                            self.wbuf[:, k, :], start=(k == 0), stop=(k == KC - 1),
                            sem=("pjm" if k == KC - 1 else None))
                p.wait("vector", "pjm")
                p.op("vector", lambda e, sc=sc, tc=tc, s=s, bank=bank: e.tensor_tensor(
                    self.XT[:, sc * 4 + tc, s * 512:(s + 1) * 512],
                    self.XT[:, sc * 4 + tc, s * 512:(s + 1) * 512],
                    bank[:, :], ALU.add), "pje")
    p.wait("vector", "pje")
    p.barrier(name="bb")


MK.emit_l1attn = emit_l1attn


def emit_l1ffn(self, tag, full=True):
    p = self.p
    PS, PSB, PSM = self.PS, self.PSB, self.PSM
    gl = self.gl
    bge = p.base("ge"); bhe = p.base("he"); bcb = p.base("cb")
    p.wait("tensor", "wsm2")
    for c in range(NCH):
        if c >= 1:
            p.wait("tensor", "ge", bge + c)
        for k in range(KC):
            self.mm(PSM[:, 0:8], self.FM[:, k, c * 128:(c + 1) * 128],
                    self.wgb[:, k, :], start=(k == 0), stop=(k == KC - 1),
                    sem=("gm" if k == KC - 1 else None))
        p.wait("scalar", "gm")
        p.op("scalar", lambda e, c=c: e.activation(
            gl[:, c, :], PSM[:, 0:8], AF.Exp), "ge")
    p.wait("vector", "ge")
    p.op("vector", lambda e: e.tensor_reduce(
        self.gaux[:, 0, :].rearrange("p (n o) -> p n o", o=1), gl, AX.X, ALU.add))
    p.op("vector", lambda e: e.reciprocal(self.gaux[:, 1, :], self.gaux[:, 0, :]))
    idxf = self.rw8[:, :, 0]
    m1f = self.rw8[:, :, 1]
    for c in range(NCH):
        p.op("vector", lambda e, c=c: e.max(self.scr8[:], gl[:, c, :]))
        p.op("vector", lambda e, c=c: e.tensor_copy(m1f[:, c:c + 1], self.scr8[:, 0:1]))
        p.op("vector", lambda e, c=c: e.tensor_scalar(
            self.oh_f[:], gl[:, c, :], self.scr8[:, 0:1], None, ALU.is_ge))
        p.op("vector", lambda e: e.tensor_tensor(
            self.scr8[:], self.oh_f[:], self.iotam_f[:], ALU.mult))
        p.op("vector", lambda e: e.tensor_scalar(
            self.scr8[:], self.scr8[:], 999.0, None, ALU.add))
        p.op("vector", lambda e, c=c: e.tensor_reduce(
            idxf[:, c:c + 1].rearrange("p (n o) -> p n o", o=1),
            self.scr8[:].rearrange("p (n e) -> p n e", n=1), AX.X, ALU.min))
    p.op("vector", lambda e: e.tensor_tensor(
        self.gaux[:, 3, :], m1f[:], self.gaux[:, 1, :], ALU.mult), "gg")
    p.wait("tensor", "gg")
    if full:
        self.mm(PSM[0:1, 0:8], self.selprev_s[:], self.priors_s[:], sem="of")
        p.wait("vector", "of")
        p.op("vector", lambda e: e.tensor_copy(self.offrow[:], PSM[0:1, 0:8]))
        p.op("vector", lambda e: e.tensor_copy(self.carry[:], PSM[0:1, 0:8]), "cw")
        p.wait("tensor", "cw")
        self.mm(PSM[:, 8:16], self.onesr_f[:], self.offrow[:], sem="ob")
        p.wait("vector", "ob")
        p.op("vector", lambda e: e.tensor_copy(self.offb[:], PSM[:, 8:16]), "obs")
    else:
        p.op("vector", lambda e: e.memset(self.offrow[:], 0.0))
        p.op("vector", lambda e: e.memset(self.offb[:], 0.0))
        p.op("vector", lambda e: e.memset(self.carry[:], 0.0), "cw")
        p.wait("tensor", "cw")
    for c in range(NCH):
        p.op("vector", lambda e, c=c: e.tensor_scalar(
            self.oh_f[:], self.iota8_f[:], idxf[:, c:c + 1], None, ALU.is_equal))
        p.op("vector", lambda e: e.tensor_copy(self.scr8b_oh(), self.oh_f[:]), "ohb")
        p.wait("tensor", "ohb")
        p.wait("tensor", "pe", None)
        p.wait("tensor", "cw", None)
        self.mm(PSM[:, 0:8], self.ustr_b[:], self.scr8b_oh(), start=True, stop=False)
        self.mm(PSM[:, 0:8], self.onesr_f[:], self.carry[:], start=False, stop=True,
                sem="pm")
        self.mm(PS[0][0:1, 0:8], self.onesc_f[:], self.oh_f[:], start=True, stop=True,
                sem="cm")
        p.wait("vector", "pm")
        p.op("vector", lambda e: e.tensor_tensor(
            self.scr8[:], PSM[:, 0:8], self.oh_f[:], ALU.mult))
        p.op("vector", lambda e: e.tensor_reduce(
            self.poscol[:, 0:1].rearrange("p (n o) -> p n o", o=1),
            self.scr8[:].rearrange("p (n e) -> p n e", n=1), AX.X, ALU.add), "pe")
        p.op("vector", lambda e, c=c: e.tensor_scalar(
            self.poscol[:, 1:2], self.poscol[:, 0:1], float(CAPREF), None, ALU.is_lt))
        p.op("vector", lambda e, c=c: e.scalar_tensor_tensor(
            self.gsc[:, c:c + 1], self.gaux[:, 3, c:c + 1], 0.5, self.poscol[:, 1:2],
            ALU.mult, ALU.mult))
        p.op("vector", lambda e: e.tensor_tensor(
            self.scr8[:], self.offb[:], self.oh_f[:], ALU.mult))
        p.op("vector", lambda e: e.tensor_reduce(
            self.poscol[:, 2:3].rearrange("p (n o) -> p n o", o=1),
            self.scr8[:].rearrange("p (n e) -> p n e", n=1), AX.X, ALU.add))
        p.op("vector", lambda e: e.tensor_sub(
            self.poscol[:, 3:4], self.poscol[:, 0:1], self.poscol[:, 2:3]))
        p.op("vector", lambda e, c=c: e.scalar_tensor_tensor(
            self.slot_f[:, c:c + 1], idxf[:, c:c + 1], float(CAP1), self.poscol[:, 3:4],
            ALU.mult, ALU.add))
        p.op("vector", lambda e: e.tensor_scalar(
            self.poscol[:, 2:3], self.poscol[:, 3:4], float(CAP1), None, ALU.is_ge))
        p.op("vector", lambda e, c=c: e.scalar_tensor_tensor(
            self.slot_f[:, c:c + 1], self.poscol[:, 2:3], 1e7, self.slot_f[:, c:c + 1],
            ALU.mult, ALU.add))
        p.op("vector", lambda e, c=c: e.tensor_scalar(
            self.poscol[:, 2:3], self.poscol[:, 1:2], -1e7, 1e7, ALU.mult, ALU.add))
        p.op("vector", lambda e, c=c: e.tensor_tensor(
            self.slot_f[:, c:c + 1], self.slot_f[:, c:c + 1], self.poscol[:, 2:3],
            ALU.add))
        p.op("vector", lambda e, c=c: e.tensor_copy(
            self.slot_i[:, c:c + 1], self.slot_f[:, c:c + 1]), "sl")
        p.op("vector", lambda e, c=c: e.tensor_copy(
            self.posi[:, 0:1], self.poscol[:, 3:4]))
        p.op("vector", lambda e: e.tensor_scalar(
            self.posi[:, 0:1], self.posi[:, 0:1], 127, None, ALU.bitwise_and))
        p.op("vector", lambda e, c=c: e.tensor_copy(
            self.slot_f[:, c:c + 1], self.posi[:, 0:1]))
        p.op("vector", lambda e, c=c: e.tensor_sub(
            self.poscol[:, 2:3], self.poscol[:, 3:4], self.slot_f[:, c:c + 1]))
        p.op("vector", lambda e: e.tensor_scalar(
            self.poscol[:, 2:3], self.poscol[:, 2:3], 1.0 / 128.0, None, ALU.mult))
        p.op("vector", lambda e, c=c: e.scalar_tensor_tensor(
            self.poscol[:, 2:3], idxf[:, c:c + 1], float(KCAP), self.poscol[:, 2:3],
            ALU.mult, ALU.add))
        p.op("vector", lambda e, c=c: e.scalar_tensor_tensor(
            self.slot_f[:, c:c + 1], self.slot_f[:, c:c + 1], float(NE * KCAP),
            self.poscol[:, 2:3], ALU.mult, ALU.add))
        p.op("vector", lambda e, c=c: e.tensor_scalar(
            self.poscol[:, 2:3], self.poscol[:, 1:2], -1e7, 1e7, ALU.mult, ALU.add))
        p.op("vector", lambda e, c=c: e.tensor_tensor(
            self.slot_f[:, c:c + 1], self.slot_f[:, c:c + 1], self.poscol[:, 2:3],
            ALU.add))
        p.op("vector", lambda e, c=c: e.tensor_scalar(
            self.poscol[:, 2:3], self.poscol[:, 3:4], float(CAP1), None, ALU.is_ge))
        p.op("vector", lambda e, c=c: e.scalar_tensor_tensor(
            self.slot_f[:, c:c + 1], self.poscol[:, 2:3], 1e7, self.slot_f[:, c:c + 1],
            ALU.mult, ALU.add))
        p.op("vector", lambda e, c=c: e.tensor_copy(
            self.slot2_i[:, c:c + 1], self.slot_f[:, c:c + 1]), "sl")
        p.wait("vector", "cm")
        p.op("vector", lambda e: e.tensor_tensor(
            self.carry[:], self.carry[:], PS[0][0:1, 0:8], ALU.add), "cw")
    if self.emit_counts and not full:
        p.op("vector", lambda e: e.tensor_sub(self.cnt_s[:], self.carry[:], self.offrow[:]),
             "cnt")
        p.wait("sync", "cnt")
        p.dma("sync", lambda e: e.dma_start(out=self.outs["counts"], in_=self.cnt_s[:]), "cst")
        p.wait("sync", "cst")
        return
    p.dma("sync", lambda e: e.dma_start(out=self.idx_d, in_=self.c_senti.ap()), "sn")
    p.wait("gpsimd", "sn")
    p.wait("gpsimd", "sl")
    for c in range(NCH):
        p.dma("gpsimd", lambda e, c=c: e.indirect_dma_start(
            out=self.idx_d, out_offset=IndirectOffsetOnAxis(self.slot2_i[:, c:c + 1], 0),
            in_=self.tokid_s[:, c:c + 1], in_offset=None,
            bounds_check=self.bregs['slot'], oob_is_err=False), "sc")
    p.wait("sync", "sc")
    p.dma("sync", lambda e: e.dma_start(
        out=self.idxc64[:].rearrange("p (a n) -> p a n", n=1),
        in_=self.idx_d.rearrange("(p a) n -> p a n", p=128)), "ix")
    p.wait("gpsimd", "ix")
    p.wait("tensor", "cw")
    p.op("vector", lambda e: e.memset(self.xtmb[:], 0.0), "xz")
    p.wait("gpsimd", "xz")
    wl_marks = []

    def wload(eng, tag, mk):
        L = len(wl_marks)
        wb = (self.wbuf[:], self.wbufB)[L % 2]
        if L >= 2:
            s_, v_ = wl_marks[L - 2][1]
            p.wait(eng, s_, v_)
        p.dma(eng, mk(wb), f"{tag}{L % 2}")
        wl_marks.append([wb, None])
        return wb, f"{tag}{L % 2}"

    for ei in range(NE):
        for j in range(KCAP):
            p.wait("gpsimd", f"gst{j % 2}", None)
            p.op("gpsimd", lambda e, ei=ei, j=j: e.tensor_copy(
                self.idx1[:], self.idxc64[:, ei * KCAP + j:ei * KCAP + j + 1]), "ixc")
            p.dma("gpsimd", lambda e, ei=ei, j=j: e.indirect_dma_start(
                out=self.xtmb[:, j % 2, :], out_offset=None,
                in_=self.xhat_d,
                in_offset=IndirectOffsetOnAxis(self.idx1[:], 0),
                bounds_check=self.bregs['tok'], oob_is_err=False), "gth")
            p.wait("sync", "gth")
            p.dma("sync", lambda e, j=j: e.dma_start(
                out=self.ddisp_d[j * 128:(j + 1) * 128, :],
                in_=self.xtmb[:, j % 2, :]), f"gst{j % 2}")
        p.wait("sync", "gst0")
        p.wait("sync", "gst1")
        p.wait("sync", "w1m")
        for k in range(KC):
            p.dma("sync", lambda e, k=k: e.dma_start(
                out=self.v_dfm[:, k, :],
                in_=self.ddisp_d[:, k * 128:(k + 1) * 128], transpose=True), "dt")
        p.wait("sync", "he")
        p.dma("sync", lambda e, ei=ei: e.dma_start(
            out=self.b1c[:], in_=self.ap["l1f_b1T"][ei]), "dt")
        p.wait("tensor", "dt")
        if ei >= 1:
            p.wait("tensor", "ee", None)
        for fs in range(8):
            wb, wsem = wload("sync", "w1l", lambda wb, ei=ei, fs=fs:
                             (lambda e: e.dma_start(
                                 out=wb,
                                 in_=self.w1_bf[ei, :, fs * 512:(fs + 1) * 512].rearrange(
                                     "(k p) n -> p k n", p=128))))
            p.wait("tensor", wsem)
            for fc in range(4):
                slot = fs * 4 + fc
                bank = PS[slot % 2]
                if ei * 32 + slot >= 2:
                    p.wait("tensor", "he", bhe + ei * 32 + slot - 1)
                for k in range(KC):
                    self.mm(bank[:, :], wb[:, k, fc * 128:(fc + 1) * 128],
                            self.v_dfm[:, k, :],
                            start=(k == 0), stop=(k == KC - 1),
                            sem=("w1m" if k == KC - 1 else None))
                p.wait("scalar", "w1m")
                p.op("scalar", lambda e, slot=slot, bank=bank: e.activation(
                    self.v_hfm[:, slot, :], bank[:, :], AF.Relu,
                    bias=self.b1c[:, slot:slot + 1]), "he")
            wl_marks[-1][1] = ("w1m", p.base("w1m"))
        p.wait("tensor", "he")
        for mh in range(2):
            p.wait("sync", "em")
            p.dma("sync", lambda e, ei=ei, mh=mh: e.dma_start(
                out=self.b2sm[:], in_=self.b2_bf[ei:ei + 1, mh * 512:(mh + 1) * 512]),
                "b2l")
            for kg in range(4):
                wb, wsem = wload("gpsimd", "w2l", lambda wb, ei=ei, mh=mh, kg=kg:
                                 (lambda e: e.dma_start(
                                     out=wb,
                                     in_=self.w2_bf[ei, kg * 1024:(kg + 1) * 1024,
                                                    mh * 512:(mh + 1) * 512].rearrange(
                                         "(k p) n -> p k n", p=128))))
                p.wait("tensor", wsem)
                if kg == 0:
                    p.wait("tensor", "ee", None)
                    p.wait("tensor", "b2l")
                for tc in range(4):
                    for k in range(KC):
                        self.mm(PS[tc][:, :],
                                self.v_hfm[:, kg * 8 + k, tc * 128:(tc + 1) * 128],
                                wb[:, k, :],
                                start=(kg == 0 and k == 0), stop=False,
                                sem=("w2m" if k == KC - 1 else None))
                if kg == 3:
                    for tc in range(4):
                        self.mm(PS[tc][:, :], self.onesr_b[:], self.b2sm[:, 0:512],
                                start=False, stop=True, sem="em")
                wl_marks[-1][1] = ("w2m", p.base("w2m"))
            p.wait("scalar", "em")
            for tc in range(4):
                p.op("scalar", lambda e, tc=tc: e.activation(
                    self.v_eob[:, tc % 2, :], PS[tc][:, :], AF.Identity), "ee")
                p.wait("sync", "ee")
                p.dma("sync", lambda e, ei=ei, mh=mh, tc=tc: e.dma_start(
                    out=self.eo_d[ei * CAP1 + tc * 128:ei * CAP1 + (tc + 1) * 128,
                                  mh * 512:(mh + 1) * 512],
                    in_=self.v_eob[:, tc % 2, :]), f"eos{tc % 2}")
                p.wait("scalar", f"eos{tc % 2}")
    p.wait("gpsimd", "eos0")
    p.wait("gpsimd", "eos1")
    for c in range(NCH):
        p.op("vector", lambda e, c=c: e.memset(self.v_gath[:, c % 2, :], 0.0), "gz")
        p.wait("gpsimd", "gz")
        if c >= 2:
            p.wait("gpsimd", "cb", bcb + c - 1)
        p.dma("gpsimd", lambda e, c=c: e.indirect_dma_start(
            out=self.v_gath[:, c % 2, :], out_offset=None,
            in_=self.eo_d, in_offset=IndirectOffsetOnAxis(self.slot_i[:, c:c + 1], 0),
            bounds_check=self.bregs['slot'], oob_is_err=False), "gb")
        p.wait("vector", "gb")
        for s in range(2):
            p.op("vector", lambda e, c=c, s=s: e.scalar_tensor_tensor(
                self.XT[:, c, s * 512:(s + 1) * 512],
                self.v_gath[:, c % 2, s * 512:(s + 1) * 512],
                self.gsc[:, c:c + 1],
                self.XT[:, c, s * 512:(s + 1) * 512],
                ALU.mult, ALU.add), "cb" if s == 1 else None)
    p.wait("vector", "cb")
    p.barrier(name="bf")


def scr8b_oh(self):
    if not hasattr(self, "_oh_b"):
        self._oh_b = self.ctx.enter_context(self.nc.sbuf_tensor("oh_b", [128, NE], BF16))
    return self._oh_b[:]


MK.emit_l1ffn = emit_l1ffn
MK.scr8b_oh = scr8b_oh


_PROGRAM_CACHE = {}


def _get_program():
    if "nc" not in _PROGRAM_CACHE:
        mk = MK(["l0ffn_a", "l0attn", "l0ffn_b", "l1ffn_a", "l1attn", "l1ffn_b"])
        nc = mk.emit_all()
        _PROGRAM_CACHE["nc"] = nc
        _PROGRAM_CACHE["mk"] = mk
    return _PROGRAM_CACHE["nc"], _PROGRAM_CACHE["mk"]


def kernel(**inputs):
    from concourse.bass_utils import run_bass_kernel_spmd
    nc, mk = _get_program()
    zpri = np.zeros((4, NE), np.float32)
    zsel = np.zeros((4, 1), np.float32)
    in_maps = []
    for c in range(8):
        m = host_inputs(inputs, c // 2)
        m["priors"] = zpri
        m["selprev"] = zsel
        m = {k: v for k, v in m.items() if k in mk.ap}
        in_maps.append(m)
    res = run_bass_kernel_spmd(nc, in_maps, core_ids=list(range(8)))
    out = np.empty((4, 4096, 1024), dtype=np.float32)
    for c in range(8):
        h = c % 2
        out[c // 2, h * 2048:(h + 1) * 2048, :] = res.results[c]["xout"][h * 2048:(h + 1) * 2048, :]
    return out


# version 05
# speedup vs baseline: 1.4214x; 1.0830x over previous
import numpy as np
import contextlib
import sys
sys.path.insert(0, '/opt/trn_rl_repo')
import concourse.bass as bass
import concourse.mybir as mybir
from concourse.bass_core_rust import IndirectOffsetOnAxis

F32 = mybir.dt.float32
BF16 = mybir.dt.bfloat16
I32 = mybir.dt.int32
AF = mybir.ActivationFunctionType
ALU = mybir.AluOpType
AX = mybir.AxisListType

ENGINES = ("sync", "tensor", "vector", "scalar", "gpsimd")

D = 1024
T = 4096
NCH = 32
KC = 8
NE = 8
CAP1 = 512
KCAP = CAP1 // 128
CAPREF = 2560
H = 16
DH = 64
NSC = 8
EXPC = 18.0


class P:
    def __init__(self, nc):
        self.nc = nc
        self.q = {e: [] for e in ENGINES}
        self.sems = {}
        self.cnt = {}
        self.virt = {}
        self.lastw = {}
        self._ctx = contextlib.ExitStack()

    def sem(self, name):
        if name not in self.sems:
            self.sems[name] = self._ctx.enter_context(self.nc.semaphore(name))
            self.cnt[name] = 0
        return self.sems[name]

    CHAINED = ("vector", "scalar", "gpsimd")

    def op(self, eng, fn, sem=None, n=1, _dma=False):
        inc = 16 if _dma else 1
        if eng in self.CHAINED:
            chname = "chain_" + eng + ("d" if (_dma and eng == "gpsimd") else "")
            other = None
            if eng == "gpsimd":
                other = "chain_gpsimdd" if not _dma else "chain_gpsimd"
            assert not (_dma and eng in ("vector", "scalar")), "no DMAs on vector/scalar"
            ch = self.sem(chname)
            cv = self.cnt[chname]
            self.cnt[chname] = cv + inc
            if sem is not None:
                reg = self.virt.setdefault(sem, {"ch": chname, "marks": []})
                assert reg["ch"] == chname, f"sem {sem} has mixed producers"
                units = (reg["marks"][-1][0] if reg["marks"] else 0) + n
                reg["marks"].append((units, self.cnt[chname]))
            if other is not None and self.cnt.get(other, 0) > self.lastw.get((eng, other), 0):
                oc = self.sem(other)
                ov = self.cnt[other]
                self.lastw[(eng, other)] = ov
                self.q[eng].append(lambda e, s=oc, v=ov: e.wait_ge(s, v))
            def run(e, fn=fn, ch=ch, cv=cv, inc=inc):
                inst = fn(e)
                if cv > 0:
                    inst = inst._wait_ge(ch, cv)
                return inst.then_inc(ch, inc)
            self.q[eng].append(run)
            return None
        if sem is not None:
            assert sem not in self.virt, f"sem {sem} already virtual"
            s = self.sem(sem)
            self.cnt[sem] += n
            self.q[eng].append(lambda e, fn=fn, s=s, n=n: fn(e).then_inc(s, n))
            return self.cnt[sem]
        self.q[eng].append(lambda e, fn=fn: fn(e))
        return None

    def dma(self, eng, fn, sem=None):
        return self.op(eng, fn, sem, 16, _dma=True)

    def base(self, sem):
        if sem in self.virt:
            marks = self.virt[sem]["marks"]
            return marks[-1][0] if marks else 0
        return self.cnt.get(sem, 0)

    def wait(self, eng, sem, val=None):
        if sem in self.virt:
            reg = self.virt[sem]
            marks = reg["marks"]
            if not marks:
                return
            if val is None:
                tgt = marks[-1][1]
            else:
                tgt = None
                for units, chaincum in marks:
                    if units >= val:
                        tgt = chaincum
                        break
                if tgt is None:
                    tgt = marks[-1][1]
            chname = reg["ch"]
            producer_eng = chname.replace("chain_", "").rstrip("d") if chname.endswith("dd") is False else ""
            producer_eng = chname[6:-1] if chname.endswith("d") and chname != "chain_gpsimd" else chname[6:]
            if chname == "chain_gpsimdd":
                producer_eng = "gpsimd"
            ch = self.sem(chname)
            if eng == producer_eng and chname in ("chain_vector", "chain_scalar", "chain_gpsimd"):
                return
            self.q[eng].append(lambda e, s=ch, v=tgt: e.wait_ge(s, v))
            return
        s = self.sem(sem)
        v = self.cnt[sem] if val is None else val
        if v > 0:
            self.q[eng].append(lambda e, s=s, v=v: e.wait_ge(s, v))

    def set_barrier_tiles(self, tiles):
        self._btiles = tiles

    def barrier(self, engines=ENGINES, name="bar"):
        s = self.sem(name)
        tgt = self.cnt[name] + len(engines)
        self.cnt[name] = tgt
        for eng in engines:
            t = self._btiles.get(eng)
            if eng in ("sync", "tensor") or t is None:
                self.q[eng].append(lambda e, s=s: e.nop().then_inc(s, 1))
            else:
                if eng == "scalar":
                    self.op(eng, lambda e, t=t: e.activation(t, t, AF.Identity))
                else:
                    self.op(eng, lambda e, t=t: e.tensor_copy(t, t))
                chname = "chain_" + eng
                ch = self.sem(chname)
                cv = self.cnt[chname]
                self.q[eng].append(lambda e, ch=ch, cv=cv: e.wait_ge(ch, cv))
                self.q[eng].append(lambda e, s=s: e.sem_inc(s, 1))
        for eng in engines:
            self.q[eng].append(lambda e, s=s, v=tgt: e.wait_ge(s, v))

    def build(self):
        with self.nc.Block() as block:
            for eng in ENGINES:
                qe = self.q[eng]
                def body(handle, qe=qe):
                    for fn in qe:
                        fn(handle)
                getattr(block, eng)(body)


class MK:
    def __init__(self, stages, out_half=False, emit_counts=False):
        self.stages = stages
        self.out_half = out_half
        self.emit_counts = emit_counts
        nc = bass.Bass("TRN2", target_bir_lowering=False, debug=False, dynamic_dma_scratch_size=8192)
        self.nc = nc
        self.p = P(nc)
        ctx = self.p._ctx
        self.ctx = ctx
        sbuf = lambda *a: ctx.enter_context(nc.sbuf_tensor(*a))
        psum = lambda *a: ctx.enter_context(nc.psum_tensor(*a))

        ap = {}
        def din(nm, shape, dt=F32):
            ap[nm] = nc.dram_tensor(nm, shape, dt, kind="ExternalInput").ap()
        self.ap = ap

        st = set(stages)
        self.need_l0ffn = bool(st & {"l0ffn_a", "l0ffn_b"})
        self.need_l0attn = "l0attn" in st
        self.need_l1attn = "l1attn" in st
        self.need_l1ffn = bool(st & {"l1ffn_a", "l1ffn_b"})
        self.need_gate = self.need_l1ffn or emit_counts

        din("x0", [T, D])
        if self.need_l0ffn:
            din("l0r_wT", [D, NE]); din("l0r_b", [1, NE])
            din("l0e_wT", [NE, D, D]); din("l0e_b", [NE, D])
        if self.need_l0attn:
            din("l0a_wq", [D, D]); din("l0a_wk", [D, D])
            din("l0a_wv", [D, D]); din("l0a_wo", [D, D])
        if self.need_l1attn:
            din("l1a_wqkvT", [D, 3 * D]); din("l1a_wprojT", [D, D])
        if self.need_gate:
            din("l1f_wg", [D, NE])
        if self.need_l1ffn:
            din("l1f_w1", [NE, D, 4 * D]); din("l1f_b1T", [NE, 128, 32])
            din("l1f_w2", [NE, 4 * D, D]); din("l1f_b2", [NE, D])
            din("priors", [4, NE]); din("selprev", [4, 1])
        self.outs = {}
        rows = T // 2 if out_half else T
        self.outs["xout"] = nc.dram_tensor("xout", [rows, D], F32, kind="ExternalOutput").ap()
        if emit_counts:
            self.outs["counts"] = nc.dram_tensor("counts", [1, NE], F32, kind="ExternalOutput").ap()
        self.halfsel = None
        if out_half:
            din("halfsel", [1, 1], F32)

        dram = lambda nm, shape, dt: nc.dram_tensor(nm, shape, dt, kind="Internal").ap()
        self.xhat_d = dram("xhat_d", [T, D], BF16)
        if self.need_l0ffn:
            self.l0e_bf = dram("l0e_bf", [NE, D, D], BF16)
        if self.need_l0attn or self.need_l1attn:
            self.qfm_d = dram("qfm_d", [D, T], BF16)
            self.kfm_d = dram("kfm_d", [D, T], BF16)
            self.vtm_d = dram("vtm_d", [T, D], BF16)
        if self.need_l0attn:
            self.ktm_d = dram("ktm_d", [T, D], BF16)
            self.l0a_bf = dram("l0a_bf", [4, D, D], BF16)
        if self.need_l1attn:
            self.l1aq_bf = dram("l1aq_bf", [D, 3 * D], BF16)
            self.l1ap_bf = dram("l1ap_bf", [D, D], BF16)
        if self.need_l1ffn:
            self.w1_bf = dram("w1_bf", [NE, D, 4 * D], BF16)
            self.w2_bf = dram("w2_bf", [NE, 4 * D, D], BF16)
            self.b2_bf = dram("b2_bf", [NE, D], BF16)
            self.idx_d = dram("idx_d", [NE * CAP1, 1], I32)
            self.eo_d = dram("eo_d", [NE * CAP1, D], BF16)
            self.ddisp_d = dram("ddisp_d", [CAP1, D], BF16)

        it = nc.inline_tensor
        self.c_ident = it(np.eye(128, dtype=np.float32), "c_ident")
        self.c_ones_row = it(np.ones((1, 128), dtype=np.float32), "c_ones_row")
        self.c_ones_col = it(np.ones((128, 1), dtype=np.float32), "c_ones_col")
        self.c_ustrict = it(np.triu(np.ones((128, 128), dtype=np.float32), 1), "c_ustrict")
        self.c_iota8 = it(np.tile(np.arange(NE, dtype=np.float32), (128, 1)), "c_iota8")
        self.c_iotam = it(np.tile(np.arange(NE, dtype=np.float32) - 999.0, (128, 1)), "c_iotam")
        self.c_tokid = it(np.arange(T, dtype=np.int32).reshape(NCH, 128).T.copy(), "c_tokid")
        self.c_senti = it(np.full((NE * CAP1, 1), 1 << 29, dtype=np.int32), "c_senti")

        self.XT = sbuf("XT", [128, NCH, D], F32)
        self.FM = sbuf("FM", [128, KC, T], BF16)
        import os as _os
        _wb = int(_os.environ.get("DEVK_WB", "512"))
        self.wbuf = sbuf("wbuf", [128, KC, _wb], BF16)
        self._xtmb_ded = sbuf("xtmb_d", [128, 2, D], BF16)
        self.lncol = sbuf("lncol", [128, 6, NCH], F32)
        self.gaux = sbuf("gaux", [128, 4, NCH], F32)
        self.rw8 = sbuf("rw8", [128, NCH, NE], F32)
        self.rwT1 = sbuf("rwT1", [8, 128], BF16)
        self.rwb1 = sbuf("rwb1", [128, NE], BF16)
        self.bexp_b = sbuf("bexp_b", [NE, D], BF16)
        self.oh_f = sbuf("oh_f", [128, NE], F32)
        self.scr8 = sbuf("scr8", [128, NE], F32)
        self.gsc = sbuf("gsc", [128, NCH], F32)
        self.slot_f = sbuf("slot_f", [128, NCH], F32)
        self.slot_i = sbuf("slot_i", [128, NCH], I32)
        self.slot2_i = sbuf("slot2_i", [128, NCH], I32)
        self.idxc64 = sbuf("idxc64", [128, NE * KCAP], I32)
        self.idx1 = sbuf("idx1", [128, 1], I32)
        self.poscol = sbuf("poscol", [128, 4], F32)
        self.posi = sbuf("posi", [128, 1], I32)
        self.carry = sbuf("carry", [1, NE], F32)
        self.offrow = sbuf("offrow", [1, NE], F32)
        self.offb = sbuf("offb", [128, NE], F32)
        self.selprev_s = sbuf("selprev_s", [4, 1], F32)
        self.priors_s = sbuf("priors_s", [4, NE], F32)
        self.epscol = sbuf("epscol", [128, 1], F32)
        self.negC = sbuf("negC", [128, 1], F32)
        self.onesr_b = sbuf("onesr_b", [1, 128], BF16)
        self.onesc_b = sbuf("onesc_b", [128, 1], BF16)
        self.onesr_f = sbuf("onesr_f", [1, 128], F32)
        self.onesc_f = sbuf("onesc_f", [128, 1], F32)
        self.ident_b = sbuf("ident_b", [128, 128], BF16)
        self.ustr_b = sbuf("ustr_b", [128, 128], BF16)
        self.iota8_f = sbuf("iota8_f", [128, NE], F32)
        self.big8_f = sbuf("big8_f", [128, NE], F32)
        self.iotam_f = sbuf("iotam_f", [128, NE], F32)
        self.tokid_s = sbuf("tokid_s", [128, NCH], I32)
        self.smallb = sbuf("smallb", [128, KC, NE], BF16)
        self.wgb = sbuf("wgb", [128, KC, NE], BF16)
        self.l0rb_b = sbuf("l0rb_b", [1, NE], BF16)
        self.b1c = sbuf("b1c", [128, 32], F32)
        self.b2sm = sbuf("b2sm", [1, 512], BF16)
        self.idxc = sbuf("idxc", [128, 8], I32)
        self.cnt_s = sbuf("cnt_s", [1, NE], F32)
        self.btile = sbuf("btile", [128, 4], F32)
        self.p.set_barrier_tiles({
            "vector": self.btile[0:1, 0:1],
            "scalar": self.btile[0:1, 1:2],
            "gpsimd": self.btile[0:1, 2:3],
        })

        self.PS = [psum(f"ps{i}", [128, 512], F32) for i in range(4)]
        self.PSB = psum("psb", [128, 1024], F32)
        self.PT = psum("pt", [128, 128], BF16)
        self.PSM = psum("psm", [128, 16], F32)

        if self._xtmb_ded is not None:
            self.xtmb = self._xtmb_ded[:]
        else:
            self.xtmb = self.wbuf[:].rearrange("p k n -> p (k n)")[:, 0:2048].rearrange(
                "p (b n) -> p b n", b=2)
        self.gl = self.PSB[:, 0:256].rearrange("p (n e) -> p n e", e=NE)
        fmf = self.FM[:].rearrange("p k t -> p (k t)")
        self.fmf = fmf
        def fv(off, n, spec=None, **kw):
            v = fmf[:, off:off + n]
            if spec:
                v = v.rearrange(spec, **kw)
            return v
        self.v_qf = fv(0, 4096, "p (k t) -> p k t", t=512)
        self.v_kf = fv(4096, 8192, "p (b k t) -> p b k t", b=2, t=512)
        self.v_vt = fv(12288, 8192, "p (b a o) -> p b a o", b=2, o=D)
        self.v_kt = fv(20480, 4096, "p (a o) -> p a o", o=D)
        self.v_af = fv(24576, 4096, "p (k t) -> p k t", t=512)
        self.v_at = fv(28672, 1024, "p (b t) -> p b t", t=512)
        self.v_ysb = fv(29696, 512)
        self.v_rrb = fv(30208, 512)
        self.statef = fv(30720, 2048).bitcast(F32).rearrange("p (k d) -> p k d", d=DH)
        self.stateb = fv(29696, 512).rearrange("p (k d) -> p k d", d=DH)
        self.fv = fv
        self.v_vt65 = fv(12288, 2080, "p (b a h w) -> p b a h w", b=2, a=4, h=4)
        self.v_dfm = fv(0, 4096, "p (k t) -> p k t", t=512)
        self.wbufB = fv(4096, 4096, "p (k t) -> p k t", t=512)
        self.v_hfm = fv(8192, 16384, "p (k t) -> p k t", t=512)
        self.v_gath = fv(24576, 2048, "p (b t) -> p b t", t=1024)
        self.v_eob = fv(26624, 1024, "p (b t) -> p b t", t=512)

    def mm(self, out, lhsT, rhs, start=True, stop=True, sem=None):
        return self.p.op("tensor", lambda e: e.matmul(out, lhsT, rhs, start=start, stop=stop), sem)

    def emit_prep(self):
        p = self.p
        def _mkregs(e):
            self.bregs = {"tok": e.to_reg(T - 1), "slot": e.to_reg(NE * CAP1 - 1),
                          "half": e.to_reg(T // 2 - 1)}
        p.q["gpsimd"].append(_mkregs)
        p.dma("sync", lambda e: e.dma_start(out=self.onesr_f[:], in_=self.c_ones_row.ap()), "ld0")
        p.dma("sync", lambda e: e.dma_start(out=self.onesc_f[:], in_=self.c_ones_col.ap()), "ld0")
        p.dma("gpsimd", lambda e: e.dma_start(out=self.ustr_b[:], in_=self.c_ustrict.ap()), "ld0g")
        p.dma("sync", lambda e: e.dma_start(out=self.iota8_f[:], in_=self.c_iota8.ap()), "ld0")
        p.dma("sync", lambda e: e.dma_start(out=self.iotam_f[:], in_=self.c_iotam.ap()), "ld0")
        p.dma("sync", lambda e: e.dma_start(out=self.tokid_s[:], in_=self.c_tokid.ap()), "ld0")
        p.dma("gpsimd", lambda e: e.dma_start(out=self.onesr_b[:], in_=self.c_ones_row.ap()), "ld0g")
        p.dma("gpsimd", lambda e: e.dma_start(out=self.onesc_b[:], in_=self.c_ones_col.ap()), "ld0g")
        p.dma("gpsimd", lambda e: e.dma_start(out=self.ident_b[:], in_=self.c_ident.ap()), "ld0g")
        p.op("vector", lambda e: e.memset(self.btile[:, 0:1], 0.0))
        p.op("scalar", lambda e: e.memzero(self.btile[:, 1:2]))
        p.op("gpsimd", lambda e: e.memset(self.btile[:, 2:3], 0.0))
        p.op("vector", lambda e: e.memset(self.epscol[:], 1e-5))
        p.op("vector", lambda e: e.memset(self.negC[:], -EXPC))
        p.op("vector", lambda e: e.memset(self.big8_f[:], 999.0))
        for c in range(NCH):
            p.dma("sync", lambda e, c=c: e.dma_start(
                out=self.XT[:, c, :], in_=self.ap["x0"][c * 128:(c + 1) * 128, :]), "ldx")
        if self.need_l1ffn:
            p.dma("sync", lambda e: e.dma_start(out=self.priors_s[:], in_=self.ap["priors"]), "ld0")
            p.dma("sync", lambda e: e.dma_start(out=self.selprev_s[:], in_=self.ap["selprev"]), "ld0")
        p.wait("sync", "ldx")
        p.wait("sync", "ld0")
        p.wait("gpsimd", "ld0g")
        p.barrier(name="prep")
        if self.need_l0ffn:
            p.dma("gpsimd", lambda e: e.dma_start(
                out=self.smallb[:],
                in_=self.ap["l0r_wT"].rearrange("(k p) e -> p k e", p=128)), "wsm")
            p.dma("gpsimd", lambda e: e.dma_start(out=self.l0rb_b[:], in_=self.ap["l0r_b"]), "wsm")
            p.dma("gpsimd", lambda e: e.dma_start(out=self.bexp_b[:], in_=self.ap["l0e_b"]), "wsm")
        if self.need_gate:
            p.dma("gpsimd", lambda e: e.dma_start(
                out=self.wgb[:],
                in_=self.ap["l1f_wg"].rearrange("(k p) e -> p k e", p=128)), "wsm2")
        if self.need_l1ffn:
            p.dma("gpsimd", lambda e: e.dma_start(out=self.b2_bf, in_=self.ap["l1f_b2"]), "wsm2")
        if self.need_l0ffn:
            for i in range(NE):
                p.dma("gpsimd", lambda e, i=i: e.dma_start(
                    out=self.l0e_bf[i], in_=self.ap["l0e_wT"][i]), "wl0e")
        if self.need_l0attn:
            for i, nm in enumerate(("l0a_wq", "l0a_wk", "l0a_wv", "l0a_wo")):
                p.dma("gpsimd", lambda e, i=i, nm=nm: e.dma_start(
                    out=self.l0a_bf[i], in_=self.ap[nm]), "wl0a")
        if self.need_l1attn:
            p.dma("gpsimd", lambda e: e.dma_start(out=self.l1aq_bf, in_=self.ap["l1a_wqkvT"]), "wl1a")
            p.dma("gpsimd", lambda e: e.dma_start(out=self.l1ap_bf, in_=self.ap["l1a_wprojT"]), "wl1a")
        if self.need_l1ffn:
            for i in range(NE):
                p.dma("gpsimd", lambda e, i=i: e.dma_start(
                    out=self.w1_bf[i], in_=self.ap["l1f_w1"][i]), "ww1")
                p.dma("gpsimd", lambda e, i=i: e.dma_start(
                    out=self.w2_bf[i], in_=self.ap["l1f_w2"][i]), "ww2")

    def emit_ln(self, tag):
        p = self.p
        blt = [p.base("lt0"), p.base("lt1")]
        for c in range(NCH):
            p.op("vector", lambda e, c=c: e.reduce_sum(
                self.lncol[:, 0, c:c + 1], self.XT[:, c, :], axis=AX.X), "lv")
            p.op("scalar", lambda e, c=c: e.activation(
                self.PSB[:, :], self.XT[:, c, :], AF.Square,
                accum_out=self.lncol[:, 1, c:c + 1]), "ls")
        p.wait("vector", "ls")
        p.op("vector", lambda e: e.tensor_scalar(
            self.lncol[:, 2, :], self.lncol[:, 0, :], 1.0 / D, None, ALU.mult))
        p.op("vector", lambda e: e.tensor_tensor(
            self.lncol[:, 3, :], self.lncol[:, 2, :], self.lncol[:, 2, :], ALU.mult))
        p.op("vector", lambda e: e.scalar_tensor_tensor(
            self.lncol[:, 4, :], self.lncol[:, 1, :], 1.0 / D, self.lncol[:, 3, :],
            ALU.mult, ALU.subtract), "lv")
        p.wait("scalar", "lv")
        p.op("scalar", lambda e: e.activation(
            self.lncol[:, 5, :], self.lncol[:, 4, :], AF.Sqrt, bias=self.epscol[:]), "ls")
        p.wait("vector", "ls")
        p.op("vector", lambda e: e.reciprocal(self.lncol[:, 4, :], self.lncol[:, 5, :]))
        for c in range(NCH):
            if c >= 2:
                p.wait("vector", f"lt{c % 2}", blt[c % 2] + 16 * (c // 2))
            p.op("vector", lambda e, c=c: e.tensor_scalar(
                self.xtmb[:, c % 2, :], self.XT[:, c, :],
                self.lncol[:, 2, c:c + 1], self.lncol[:, 4, c:c + 1],
                ALU.subtract, ALU.mult), "ln")
            p.wait("sync", "ln")
            p.dma("sync", lambda e, c=c: e.dma_start(
                out=self.xhat_d[c * 128:(c + 1) * 128, :], in_=self.xtmb[:, c % 2, :]),
                f"lt{c % 2}")
        p.wait("sync", "lt0")
        p.wait("sync", "lt1")
        for k in range(KC):
            p.dma("sync", lambda e, k=k: e.dma_start(
                out=self.FM[:, k, :], in_=self.xhat_d[:, k * 128:(k + 1) * 128],
                transpose=True), "lf")
        p.wait("sync", "lf")
        p.barrier(name="b")

    def emit_l0ffn(self, tag):
        p = self.p
        PS, PSM = self.PS, self.PSM
        bge = p.base("ge"); bbbe = p.base("bbe"); bew = p.base("ew")
        p.wait("tensor", "wsm", 16 * 2)
        for c in range(NCH):
            if c >= 1:
                p.wait("tensor", "ge", bge + c)
            for k in range(KC):
                self.mm(PSM[:, 0:8], self.FM[:, k, c * 128:(c + 1) * 128],
                        self.smallb[:, k, :], start=(k == 0), stop=False)
            self.mm(PSM[:, 0:8], self.onesr_b[:], self.l0rb_b[:], start=False, stop=True,
                    sem="gm")
            p.wait("scalar", "gm")
            p.op("scalar", lambda e, c=c: e.activation(
                self.gl[:, c, :], PSM[:, 0:8], AF.Exp), "ge")
        p.wait("vector", "ge")
        p.op("vector", lambda e: e.tensor_reduce(
            self.gaux[:, 0, :].rearrange("p (n o) -> p n o", o=1), self.gl, AX.X, ALU.add))
        p.op("vector", lambda e: e.reciprocal(self.gaux[:, 1, :], self.gaux[:, 0, :]))
        for c in range(NCH):
            p.op("vector", lambda e, c=c: e.max(self.scr8[:], self.gl[:, c, :]))
            p.op("vector", lambda e, c=c: e.tensor_scalar(
                self.oh_f[:], self.gl[:, c, :], self.scr8[:, 1:2], None, ALU.is_ge))
            p.op("vector", lambda e, c=c: e.tensor_scalar(
                self.rw8[:, c, :], self.gl[:, c, :], self.gaux[:, 1, c:c + 1], 0.5,
                ALU.mult, ALU.mult))
            p.op("vector", lambda e, c=c: e.tensor_tensor(
                self.rw8[:, c, :], self.rw8[:, c, :], self.oh_f[:], ALU.mult))
        p.wait("tensor", "wsm")
        for c in range(NCH):
            p.op("vector", lambda e, c=c: e.tensor_copy(
                self.rwb1[:], self.rw8[:, c, :]), "gw")
            p.wait("tensor", "gw")
            p.op("tensor", lambda e, c=c: e.transpose(
                self.PT[0:8, :], self.rwb1[:], self.ident_b[:]), "bt")
            p.wait("vector", "bt")
            p.op("vector", lambda e: e.tensor_copy(self.rwT1[:], self.PT[0:8, :]), "bte")
            p.wait("tensor", "bte")
            for s in range(2):
                slot = c * 2 + s
                bank = PS[slot % 2]
                if slot >= 2:
                    p.wait("tensor", "bbe", bbbe + slot - 1)
                self.mm(bank[:, :], self.rwT1[:], self.bexp_b[:, s * 512:(s + 1) * 512],
                        sem="bbm")
                p.wait("vector", "bbm")
                p.op("vector", lambda e, c=c, s=s, bank=bank: e.tensor_tensor(
                    self.XT[:, c, s * 512:(s + 1) * 512], self.XT[:, c, s * 512:(s + 1) * 512],
                    bank[:, :], ALU.add), "bbe")
        p.wait("sync", "wl0e")
        for i in range(NE):
            for half in range(2):
                slot = i * 2 + half
                if slot >= 1:
                    p.wait("sync", "ew", bew + slot * NCH)
                p.dma("sync", lambda e, i=i, h=half: e.dma_start(
                    out=self.wbuf[:],
                    in_=self.l0e_bf[i, :, h * 512:(h + 1) * 512].rearrange(
                        "(k p) n -> p k n", p=128)), "el")
                p.wait("tensor", "el")
                for c in range(NCH):
                    pb = PS[2 + (c % 2)]
                    if c >= 2:
                        p.wait("tensor", "ew", bew + slot * NCH + c - 1)
                    for k in range(KC):
                        self.mm(pb[:, :], self.FM[:, k, c * 128:(c + 1) * 128],
                                self.wbuf[:, k, :], start=(k == 0), stop=(k == KC - 1),
                                sem=("em" if k == KC - 1 else None))
                    p.wait("vector", "em")
                    p.op("vector", lambda e, c=c, h=half, pb=pb, i=i: e.scalar_tensor_tensor(
                        self.XT[:, c, h * 512:(h + 1) * 512], pb[:, :],
                        self.rw8[:, c, i:i + 1],
                        self.XT[:, c, h * 512:(h + 1) * 512],
                        ALU.mult, ALU.add), "ew")
        p.wait("vector", "ew")
        p.barrier(name="bf")

    def emit_finish(self):
        p = self.p
        if self.out_half:
            raise NotImplementedError
        for c in range(NCH):
            p.dma("sync", lambda e, c=c: e.dma_start(
                out=self.outs["xout"][c * 128:(c + 1) * 128, :], in_=self.XT[:, c, :]), "fst")
        p.wait("sync", "fst")

    def emit_dump_dram(self, src_ap):
        p = self.p
        for c in range(NCH):
            p.dma("sync", lambda e, c=c: e.dma_start(
                out=self.xtmb[:, 0, :], in_=src_ap[c * 128:(c + 1) * 128, :]), "dmp1")
            p.wait("vector", "dmp1")
            p.op("vector", lambda e, c=c: e.tensor_copy(self.XT[:, c, :], self.xtmb[:, 0, :]), "dmp2")
            p.wait("sync", "dmp2")
            p.dma("sync", lambda e, c=c: e.dma_start(
                out=self.outs["xout"][c * 128:(c + 1) * 128, :], in_=self.XT[:, c, :]), "fst")
            p.wait("sync", "fst")
        p.wait("sync", "fst")

    def emit_dump_sbuf_f32(self, get_slice, nch=NCH):
        p = self.p
        for c in range(nch):
            p.dma("sync", lambda e, c=c: e.dma_start(
                out=self.outs["xout"][c * 128:(c + 1) * 128, :], in_=get_slice(c)), "fst")
        p.wait("sync", "fst")

    def emit_all(self):
        import os
        dump = os.environ.get("DEVK_DUMP", "")
        self.emit_prep()
        for s in self.stages:
            self.emit_ln(s)
            if dump == "xhat":
                self.emit_dump_dram(self.xhat_d)
                self.p.build()
                return self.nc
            if dump == "fm":
                xv = self.outs["xout"].rearrange("(k a p) d -> (k) (a p d)", k=KC, a=4)
                for k in range(KC):
                    self.p.dma("gpsimd", lambda e, k=k: e.dma_start(
                        out=xv[k:k + 1, :].rearrange("a (p t) -> (a p) t", p=128),
                        in_=self.FM[:, k, :]), "fst")
                self.p.wait("gpsimd", "fst")
                self.p.build()
                return self.nc
            if s in ("l0ffn_a", "l0ffn_b"):
                self.emit_l0ffn(s)
            elif s == "l0attn":
                self.emit_l0attn(s)
            elif s == "l1attn":
                self.emit_l1attn(s)
            elif s in ("l1ffn_a", "l1ffn_b"):
                self.emit_l1ffn(s)
            elif s == "l1gate":
                self.emit_l1ffn(s, full=False)
        self.emit_finish()
        self.p.build()
        return self.nc


def host_inputs(d, batch):
    m = {"x0": np.ascontiguousarray(np.asarray(d["L0_attn_X"])[batch])}
    m["l0r_wT"] = np.ascontiguousarray(np.asarray(d["L0_ffn_W_router"]).T)
    m["l0r_b"] = np.ascontiguousarray(np.asarray(d["L0_ffn_b_router"])[None, :])
    m["l0e_wT"] = np.ascontiguousarray(np.asarray(d["L0_ffn_W_experts"]).transpose(0, 2, 1))
    m["l0e_b"] = np.ascontiguousarray(np.asarray(d["L0_ffn_b_experts"]))
    m["l0a_wq"] = np.ascontiguousarray(np.asarray(d["L0_attn_Wq"]))
    m["l0a_wk"] = np.ascontiguousarray(np.asarray(d["L0_attn_Wk"]))
    m["l0a_wv"] = np.ascontiguousarray(np.asarray(d["L0_attn_Wv"]))
    m["l0a_wo"] = np.ascontiguousarray(np.asarray(d["L0_attn_Wo"]))
    m["l1a_wqkvT"] = np.ascontiguousarray(np.asarray(d["L1_attn_W_qkv"]).T)
    m["l1a_wprojT"] = np.ascontiguousarray(np.asarray(d["L1_attn_W_proj"]).T)
    m["l1f_wg"] = np.ascontiguousarray(np.asarray(d["L1_ffn_wg"]))
    m["l1f_w1"] = np.ascontiguousarray(np.asarray(d["L1_ffn_w1"]))
    m["l1f_b1T"] = np.ascontiguousarray(
        np.asarray(d["L1_ffn_b1"]).reshape(8, 32, 128).transpose(0, 2, 1))
    m["l1f_w2"] = np.ascontiguousarray(np.asarray(d["L1_ffn_w2"]))
    m["l1f_b2"] = np.ascontiguousarray(np.asarray(d["L1_ffn_b2"]))
    return m


def _proj_fm_out(self, tag, wsrc_d, col0, dst_d):
    p = self.p
    PS = self.PS
    bpae = p.base("pae"); bpas = [p.base("pas0"), p.base("pas1")]
    for oc in range(8):
        p.wait("sync", "pam")
        p.dma("sync", lambda e, oc=oc: e.dma_start(
            out=self.wbuf[:, :, 0:128],
            in_=wsrc_d[:, col0 + oc * 128:col0 + (oc + 1) * 128].rearrange(
                "(k p) n -> p k n", p=128)), "pal")
        p.wait("tensor", "pal")
        for ts in range(8):
            slot = oc * 8 + ts
            bank = PS[slot % 2]
            if slot >= 2:
                p.wait("tensor", "pae", bpae + slot - 1)
            for k in range(KC):
                self.mm(bank[:, :], self.wbuf[:, k, 0:128],
                        self.FM[:, k, ts * 512:(ts + 1) * 512],
                        start=(k == 0), stop=(k == KC - 1),
                        sem=("pam" if k == KC - 1 else None))
            p.wait("scalar", "pam")
            if slot >= 2:
                p.wait("scalar", f"pas{slot % 2}", bpas[slot % 2] + 16 * (slot // 2))
            p.op("scalar", lambda e, slot=slot, bank=bank: e.activation(
                self.xtmb[:, slot % 2, 0:512], bank[:, :], AF.Identity), "pae")
            p.wait("sync", "pae")
            p.dma("sync", lambda e, oc=oc, ts=ts, slot=slot: e.dma_start(
                out=dst_d[oc * 128:(oc + 1) * 128, ts * 512:(ts + 1) * 512],
                in_=self.xtmb[:, slot % 2, 0:512]), f"pas{slot % 2}")
    p.wait("sync", "pam")
    p.wait("sync", "pas0")
    p.wait("sync", "pas1")


def _proj_tm_out(self, tag, wsrc_d, col0, width, dst_d):
    p = self.p
    PS = self.PS
    bpte = p.base("pte"); bpts = [p.base("pts0"), p.base("pts1")]
    for half in range(width // 512):
        p.wait("sync", "ptm")
        p.dma("sync", lambda e, half=half: e.dma_start(
            out=self.wbuf[:],
            in_=wsrc_d[:, col0 + half * 512:col0 + (half + 1) * 512].rearrange(
                "(k p) n -> p k n", p=128)), "ptl")
        p.wait("tensor", "ptl")
        for c in range(NCH):
            slot = half * NCH + c
            bank = PS[2 + slot % 2]
            if slot >= 2:
                p.wait("tensor", "pte", bpte + slot - 1)
            for k in range(KC):
                self.mm(bank[:, :], self.FM[:, k, c * 128:(c + 1) * 128],
                        self.wbuf[:, k, :], start=(k == 0), stop=(k == KC - 1),
                        sem=("ptm" if k == KC - 1 else None))
            p.wait("scalar", "ptm")
            if slot >= 2:
                p.wait("scalar", f"pts{slot % 2}", bpts[slot % 2] + 16 * (slot // 2))
            p.op("scalar", lambda e, slot=slot, bank=bank: e.activation(
                self.xtmb[:, slot % 2, 512:1024], bank[:, :], AF.Identity), "pte")
            p.wait("sync", "pte")
            p.dma("sync", lambda e, c=c, half=half, slot=slot: e.dma_start(
                out=dst_d[c * 128:(c + 1) * 128, half * 512:(half + 1) * 512],
                in_=self.xtmb[:, slot % 2, 512:1024]), f"pts{slot % 2}")
    p.wait("sync", "ptm")
    p.wait("sync", "pts0")
    p.wait("sync", "pts1")


def emit_l0attn(self, tag):
    p = self.p
    PS, PSB = self.PS, self.PSB
    bse = p.base("se"); bye = p.base("ye"); bsup = p.base("sup")
    bavm = p.base("avm"); bpje = p.base("pje")
    p.wait("sync", "wl0a")
    _proj_fm_out(self, "q" + tag, self.l0a_bf[0], 0, self.qfm_d)
    _proj_fm_out(self, "k" + tag, self.l0a_bf[1], 0, self.kfm_d)
    _proj_tm_out(self, "v" + tag, self.l0a_bf[2], 0, D, self.vtm_d)
    _proj_tm_out(self, "kt" + tag, self.l0a_bf[1], 0, D, self.ktm_d)
    p.barrier(name="ba")
    p.op("vector", lambda e: e.memset(self.statef, 0.0))
    p.op("vector", lambda e: e.memset(self.stateb, 0.0), "stz")
    p.wait("tensor", "stz")
    bmsk = p.base("msk")
    g_se = g_ye = g_sup = g_av = g_pje = g_msk = 0
    for sc in range(NSC):
        if sc >= 1:
            p.wait("sync", "pjm")
            p.wait("tensor", "pje")
        p.dma("sync", lambda e, sc=sc: e.dma_start(
            out=self.v_qf, in_=self.qfm_d[:, sc * 512:(sc + 1) * 512].rearrange(
                "(k p) t -> p k t", p=128)), "bl")
        p.dma("sync", lambda e, sc=sc: e.dma_start(
            out=self.v_kf[:, 0], in_=self.kfm_d[:, sc * 512:(sc + 1) * 512].rearrange(
                "(k p) t -> p k t", p=128)), "bl")
        p.dma("sync", lambda e, sc=sc: e.dma_start(
            out=self.v_vt[:, 0], in_=self.vtm_d[sc * 512:(sc + 1) * 512, :].rearrange(
                "(a p) o -> p a o", p=128)), "bl")
        p.dma("sync", lambda e, sc=sc: e.dma_start(
            out=self.v_kt, in_=self.ktm_d[sc * 512:(sc + 1) * 512, :].rearrange(
                "(a p) o -> p a o", p=128)), "bl")
        p.wait("tensor", "bl")
        p.wait("tensor", "sbu")
        for h in range(H):
            hb = (h % 2) * 64
            hk = h // 2
            gh = sc * H + h
            psy = PS[h % 2]
            if gh >= 2:
                p.wait("tensor", "ye", bye + gh - 1)
            self.mm(psy[0:64, :], self.stateb[hb:hb + 64, hk, :],
                    self.v_qf[hb:hb + 64, hk, :], start=True, stop=False)
            pend = None
            for kc in range(4):
                gs = g_se
                g_se += 1
                pss = PS[2 + gs % 2]
                if gs >= 2:
                    p.wait("tensor", "se", bse + gs - 1)
                self.mm(pss[:, :],
                        self.v_kf[hb:hb + 64, 0, hk, kc * 128:(kc + 1) * 128],
                        self.v_qf[hb:hb + 64, hk, :], start=True, stop=True,
                        sem="sm")
                p.wait("scalar", "sm")
                if g_av >= 2:
                    p.wait("scalar", "avm", bavm + g_av - 1)
                p.op("scalar", lambda e, gs=gs, pss=pss: e.activation(
                    self.v_at[:, gs % 2, :], pss[:, :], AF.Identity), "se")
                p.wait("gpsimd", "se")
                p.op("gpsimd", lambda e, kc=kc, gs=gs: e.affine_select(
                    out=self.v_at[:, gs % 2, :], in_=self.v_at[:, gs % 2, :],
                    pattern=[[1, 512]], compare_op=ALU.is_ge, fill=0.0,
                    base=-kc * 128, channel_multiplier=-1), "msk")
                g_msk += 1
                mi = g_msk
                g_av += 1
                if pend is not None:
                    pend()
                def _pend(gs=gs, mi=mi, psy=psy, h=h, kc=kc):
                    p.wait("tensor", "msk", bmsk + mi)
                    self.mm(psy[0:64, :], self.v_vt[:, 0, kc, h * 64:(h + 1) * 64],
                            self.v_at[:, gs % 2, :], start=False, stop=(kc == 3),
                            sem="avm")
                pend = _pend
            if gh >= 1:
                p.wait("tensor", "sup", bsup + gh)
            for kc in range(4):
                self.mm(PSB[0:64, 0:64], self.v_kt[:, kc, h * 64:(h + 1) * 64],
                        self.v_vt[:, 0, kc, h * 64:(h + 1) * 64],
                        start=(kc == 0), stop=(kc == 3),
                        sem=("sum" if kc == 3 else None))
            pend()
            p.wait("scalar", "avm")
            p.op("scalar", lambda e, hb=hb, hk=hk, psy=psy: e.activation(
                self.v_af[hb:hb + 64, hk, :], psy[0:64, :], AF.Identity), "ye")
            p.wait("vector", "sum")
            p.op("vector", lambda e, hb=hb, hk=hk: e.tensor_tensor(
                self.statef[hb:hb + 64, hk, :], self.statef[hb:hb + 64, hk, :],
                PSB[0:64, 0:64], ALU.add), "sup")
            p.op("vector", lambda e, hb=hb, hk=hk: e.tensor_copy(
                self.stateb[hb:hb + 64, hk, :], self.statef[hb:hb + 64, hk, :]),
                "sbu")
        p.wait("tensor", "ye")
        p.wait("tensor", "se")
        for s in range(2):
            p.wait("sync", "pjm")
            p.dma("sync", lambda e, s=s: e.dma_start(
                out=self.wbuf[:],
                in_=self.l0a_bf[3][:, s * 512:(s + 1) * 512].rearrange(
                    "(k p) n -> p k n", p=128)), "pjl")
            p.wait("tensor", "pjl")
            for tc in range(4):
                gp = g_pje
                g_pje += 1
                bank = PS[2 + gp % 2]
                if gp >= 2:
                    p.wait("tensor", "pje", bpje + gp - 1)
                for k in range(KC):
                    self.mm(bank[:, :], self.v_af[:, k, tc * 128:(tc + 1) * 128],
                            self.wbuf[:, k, :], start=(k == 0), stop=(k == KC - 1),
                            sem=("pjm" if k == KC - 1 else None))
                p.wait("vector", "pjm")
                p.op("vector", lambda e, sc=sc, tc=tc, s=s, bank=bank: e.tensor_tensor(
                    self.XT[:, sc * 4 + tc, s * 512:(s + 1) * 512],
                    self.XT[:, sc * 4 + tc, s * 512:(s + 1) * 512],
                    bank[:, :], ALU.add), "pje")
    p.wait("vector", "pje")
    p.barrier(name="bb")


MK.emit_l0attn = emit_l0attn


def emit_l1attn(self, tag):
    p = self.p
    PS, PSB = self.PS, self.PSB
    bse = p.base("se"); bavs = p.base("avs"); bpje = p.base("pje")
    p.wait("sync", "wl1a")
    _proj_fm_out(self, "q" + tag, self.l1aq_bf, 0, self.qfm_d)
    _proj_fm_out(self, "k" + tag, self.l1aq_bf, D, self.kfm_d)
    _proj_tm_out(self, "v" + tag, self.l1aq_bf, 2 * D, D, self.vtm_d)
    p.barrier(name="ba")
    p.op("vector", lambda e: e.memset(self.v_vt65[:, :, :, :, 64:65], 1.0), "vo")
    p.wait("tensor", "vo")
    rr32 = self.statef[:].rearrange("p a b -> p (a b)")[0:1, 0:512]
    bmsk = p.base("msk")
    g_se = g_pje = g_kl = g_msk = 0
    for sc in range(NSC):
        if sc >= 1:
            p.wait("sync", "pjm")
            p.wait("tensor", "pje")
        p.dma("sync", lambda e, sc=sc: e.dma_start(
            out=self.v_qf, in_=self.qfm_d[:, sc * 512:(sc + 1) * 512].rearrange(
                "(k p) t -> p k t", p=128)), "bl")
        p.wait("tensor", "bl")
        for g in range(4):
            p.wait("tensor", "yse")
            for ksc in range(sc + 1):
                par = g_kl % 2
                g_kl += 1
                if g_kl >= 3:
                    p.wait("sync", "sm", None)
                p.dma("sync", lambda e, g=g, ksc=ksc, par=par: e.dma_start(
                    out=self.v_kf[:, par, 0:2, :],
                    in_=self.kfm_d[256 * g:256 * (g + 1),
                                    ksc * 512:(ksc + 1) * 512].rearrange(
                        "(k p) t -> p k t", p=128)), f"kl{par}")
            
                for kc4 in range(4):
                    p.dma("sync", lambda e, g=g, ksc=ksc, par=par, kc4=kc4: e.dma_start(
                        out=self.v_vt65[:, par, kc4, :, 0:64],
                        in_=self.vtm_d[ksc * 512 + kc4 * 128:ksc * 512 + (kc4 + 1) * 128,
                                        256 * g:256 * (g + 1)].rearrange(
                            "p (h w) -> p h w", w=64)), f"vl{par}")
                p.wait("tensor", f"kl{par}")
                p.wait("tensor", f"vl{par}")
                pend = None
                for hi in range(4):
                    h = 4 * g + hi
                    hb = (h % 2) * 64
                    hkk = hi // 2
                    psy = PS[hi]
                    for kc in range(4):
                        gs = g_se
                        g_se += 1
                        pss = PSB[:, (gs % 2) * 512:(gs % 2 + 1) * 512]
                        if gs >= 2:
                            p.wait("tensor", "se", bse + gs - 1)
                        self.mm(pss, self.v_kf[hb:hb + 64, par, hkk,
                                                kc * 128:(kc + 1) * 128],
                                self.v_qf[hb:hb + 64, h // 2, :],
                                start=True, stop=True, sem="sm")
                        p.wait("scalar", "sm")
                        if g_se >= 3:
                            p.wait("scalar", "avs", bavs + g_se - 2)
                        p.op("scalar", lambda e, gs=gs, pss=pss: e.activation(
                            self.v_at[:, gs % 2, :], pss, AF.Exp,
                            bias=self.negC[:], scale=0.125), "se")
                        mi = None
                        if ksc == sc:
                            p.wait("gpsimd", "se")
                            p.op("gpsimd", lambda e, kc=kc, gs=gs: e.affine_select(
                                out=self.v_at[:, gs % 2, :], in_=self.v_at[:, gs % 2, :],
                                pattern=[[1, 512]], compare_op=ALU.is_ge, fill=0.0,
                                base=-kc * 128, channel_multiplier=-1), "msk")
                            g_msk += 1
                            mi = g_msk
                        if pend is not None:
                            pend()
                        def _pend(gs=gs, mi=mi, psy=psy, hi=hi, kc=kc, par=par,
                                  first=(ksc == 0 and kc == 0),
                                  last=(ksc == sc and kc == 3)):
                            if mi is not None:
                                p.wait("tensor", "msk", bmsk + mi)
                            else:
                                p.wait("tensor", "se", bse + gs + 1)
                            self.mm(psy[0:65, :],
                                    self.v_vt65[:, par, kc, hi, :],
                                    self.v_at[:, gs % 2, :], start=first,
                                    stop=last, sem="avs")
                        pend = _pend
                pend()
            for hi in range(4):
                h = 4 * g + hi
                hb = (h % 2) * 64
                psy = PS[hi]
                p.wait("vector", "avs")
                p.op("vector", lambda e, psy=psy, rr32=rr32: e.reciprocal(
                    rr32, psy[64:65, :]))
                p.op("vector", lambda e, rr32=rr32: e.tensor_copy(
                    self.v_rrb[0:1, :], rr32), "rrb")
                p.wait("tensor", "rrb")
                p.wait("tensor", "se")
                self.mm(PSB[:, 0:512], self.onesr_b[:], self.v_rrb[0:1, 0:512],
                        start=True, stop=True, sem="bcm")
                p.wait("scalar", "bcm")
                p.op("scalar", lambda e, psy=psy: e.activation(
                    self.v_ysb[0:64, :], psy[0:64, :], AF.Identity), "yse")
                p.wait("vector", "yse")
                p.op("vector", lambda e, hb=hb, h=h: e.tensor_tensor(
                    self.v_af[hb:hb + 64, h // 2, :], self.v_ysb[0:64, :],
                    PSB[0:64, 0:512], ALU.mult), "afw")
                p.wait("tensor", "afw")
        p.wait("tensor", "afw")
        for s in range(2):
            p.wait("sync", "pjm")
            p.dma("sync", lambda e, s=s: e.dma_start(
                out=self.wbuf[:],
                in_=self.l1ap_bf[:, s * 512:(s + 1) * 512].rearrange(
                    "(k p) n -> p k n", p=128)), "pjl")
            p.wait("tensor", "pjl")
            for tc in range(4):
                gp = g_pje
                g_pje += 1
                bank = PS[2 + gp % 2]
                if gp >= 2:
                    p.wait("tensor", "pje", bpje + gp - 1)
                for k in range(KC):
                    self.mm(bank[:, :], self.v_af[:, k, tc * 128:(tc + 1) * 128],
                            self.wbuf[:, k, :], start=(k == 0), stop=(k == KC - 1),
                            sem=("pjm" if k == KC - 1 else None))
                p.wait("vector", "pjm")
                p.op("vector", lambda e, sc=sc, tc=tc, s=s, bank=bank: e.tensor_tensor(
                    self.XT[:, sc * 4 + tc, s * 512:(s + 1) * 512],
                    self.XT[:, sc * 4 + tc, s * 512:(s + 1) * 512],
                    bank[:, :], ALU.add), "pje")
    p.wait("vector", "pje")
    p.barrier(name="bb")


MK.emit_l1attn = emit_l1attn


def emit_l1ffn(self, tag, full=True):
    p = self.p
    PS, PSB, PSM = self.PS, self.PSB, self.PSM
    gl = self.gl
    bge = p.base("ge"); bhe = p.base("he"); bcb = p.base("cb")
    p.wait("tensor", "wsm2")
    for c in range(NCH):
        if c >= 1:
            p.wait("tensor", "ge", bge + c)
        for k in range(KC):
            self.mm(PSM[:, 0:8], self.FM[:, k, c * 128:(c + 1) * 128],
                    self.wgb[:, k, :], start=(k == 0), stop=(k == KC - 1),
                    sem=("gm" if k == KC - 1 else None))
        p.wait("scalar", "gm")
        p.op("scalar", lambda e, c=c: e.activation(
            gl[:, c, :], PSM[:, 0:8], AF.Exp), "ge")
    p.wait("vector", "ge")
    p.op("vector", lambda e: e.tensor_reduce(
        self.gaux[:, 0, :].rearrange("p (n o) -> p n o", o=1), gl, AX.X, ALU.add))
    p.op("vector", lambda e: e.reciprocal(self.gaux[:, 1, :], self.gaux[:, 0, :]))
    idxf = self.rw8[:, :, 0]
    m1f = self.rw8[:, :, 1]
    for c in range(NCH):
        p.op("vector", lambda e, c=c: e.max(self.scr8[:], gl[:, c, :]))
        p.op("vector", lambda e, c=c: e.tensor_copy(m1f[:, c:c + 1], self.scr8[:, 0:1]))
        p.op("vector", lambda e, c=c: e.tensor_scalar(
            self.oh_f[:], gl[:, c, :], self.scr8[:, 0:1], None, ALU.is_ge))
        p.op("vector", lambda e: e.tensor_tensor(
            self.scr8[:], self.oh_f[:], self.iotam_f[:], ALU.mult))
        p.op("vector", lambda e: e.tensor_scalar(
            self.scr8[:], self.scr8[:], 999.0, None, ALU.add))
        p.op("vector", lambda e, c=c: e.tensor_reduce(
            idxf[:, c:c + 1].rearrange("p (n o) -> p n o", o=1),
            self.scr8[:].rearrange("p (n e) -> p n e", n=1), AX.X, ALU.min))
    p.op("vector", lambda e: e.tensor_tensor(
        self.gaux[:, 3, :], m1f[:], self.gaux[:, 1, :], ALU.mult), "gg")
    p.wait("tensor", "gg")
    if full:
        self.mm(PSM[0:1, 0:8], self.selprev_s[:], self.priors_s[:], sem="of")
        p.wait("vector", "of")
        p.op("vector", lambda e: e.tensor_copy(self.offrow[:], PSM[0:1, 0:8]))
        p.op("vector", lambda e: e.tensor_copy(self.carry[:], PSM[0:1, 0:8]), "cw")
        p.wait("tensor", "cw")
        self.mm(PSM[:, 8:16], self.onesr_f[:], self.offrow[:], sem="ob")
        p.wait("vector", "ob")
        p.op("vector", lambda e: e.tensor_copy(self.offb[:], PSM[:, 8:16]), "obs")
    else:
        p.op("vector", lambda e: e.memset(self.offrow[:], 0.0))
        p.op("vector", lambda e: e.memset(self.offb[:], 0.0))
        p.op("vector", lambda e: e.memset(self.carry[:], 0.0), "cw")
        p.wait("tensor", "cw")
    for c in range(NCH):
        p.op("vector", lambda e, c=c: e.tensor_scalar(
            self.oh_f[:], self.iota8_f[:], idxf[:, c:c + 1], None, ALU.is_equal))
        p.op("vector", lambda e: e.tensor_copy(self.scr8b_oh(), self.oh_f[:]), "ohb")
        p.wait("tensor", "ohb")
        p.wait("tensor", "pe", None)
        p.wait("tensor", "cw", None)
        self.mm(PSM[:, 0:8], self.ustr_b[:], self.scr8b_oh(), start=True, stop=False)
        self.mm(PSM[:, 0:8], self.onesr_f[:], self.carry[:], start=False, stop=True,
                sem="pm")
        self.mm(PS[0][0:1, 0:8], self.onesc_f[:], self.oh_f[:], start=True, stop=True,
                sem="cm")
        p.wait("vector", "pm")
        p.op("vector", lambda e: e.tensor_tensor(
            self.scr8[:], PSM[:, 0:8], self.oh_f[:], ALU.mult))
        p.op("vector", lambda e: e.tensor_reduce(
            self.poscol[:, 0:1].rearrange("p (n o) -> p n o", o=1),
            self.scr8[:].rearrange("p (n e) -> p n e", n=1), AX.X, ALU.add), "pe")
        p.op("vector", lambda e, c=c: e.tensor_scalar(
            self.poscol[:, 1:2], self.poscol[:, 0:1], float(CAPREF), None, ALU.is_lt))
        p.op("vector", lambda e, c=c: e.scalar_tensor_tensor(
            self.gsc[:, c:c + 1], self.gaux[:, 3, c:c + 1], 0.5, self.poscol[:, 1:2],
            ALU.mult, ALU.mult))
        p.op("vector", lambda e: e.tensor_tensor(
            self.scr8[:], self.offb[:], self.oh_f[:], ALU.mult))
        p.op("vector", lambda e: e.tensor_reduce(
            self.poscol[:, 2:3].rearrange("p (n o) -> p n o", o=1),
            self.scr8[:].rearrange("p (n e) -> p n e", n=1), AX.X, ALU.add))
        p.op("vector", lambda e: e.tensor_sub(
            self.poscol[:, 3:4], self.poscol[:, 0:1], self.poscol[:, 2:3]))
        p.op("vector", lambda e, c=c: e.scalar_tensor_tensor(
            self.slot_f[:, c:c + 1], idxf[:, c:c + 1], float(CAP1), self.poscol[:, 3:4],
            ALU.mult, ALU.add))
        p.op("vector", lambda e: e.tensor_scalar(
            self.poscol[:, 2:3], self.poscol[:, 3:4], float(CAP1), None, ALU.is_ge))
        p.op("vector", lambda e, c=c: e.scalar_tensor_tensor(
            self.slot_f[:, c:c + 1], self.poscol[:, 2:3], 1e7, self.slot_f[:, c:c + 1],
            ALU.mult, ALU.add))
        p.op("vector", lambda e, c=c: e.tensor_scalar(
            self.poscol[:, 2:3], self.poscol[:, 1:2], -1e7, 1e7, ALU.mult, ALU.add))
        p.op("vector", lambda e, c=c: e.tensor_tensor(
            self.slot_f[:, c:c + 1], self.slot_f[:, c:c + 1], self.poscol[:, 2:3],
            ALU.add))
        p.op("vector", lambda e, c=c: e.tensor_copy(
            self.slot_i[:, c:c + 1], self.slot_f[:, c:c + 1]), "sl")
        p.op("vector", lambda e, c=c: e.tensor_copy(
            self.posi[:, 0:1], self.poscol[:, 3:4]))
        p.op("vector", lambda e: e.tensor_scalar(
            self.posi[:, 0:1], self.posi[:, 0:1], 127, None, ALU.bitwise_and))
        p.op("vector", lambda e, c=c: e.tensor_copy(
            self.slot_f[:, c:c + 1], self.posi[:, 0:1]))
        p.op("vector", lambda e, c=c: e.tensor_sub(
            self.poscol[:, 2:3], self.poscol[:, 3:4], self.slot_f[:, c:c + 1]))
        p.op("vector", lambda e: e.tensor_scalar(
            self.poscol[:, 2:3], self.poscol[:, 2:3], 1.0 / 128.0, None, ALU.mult))
        p.op("vector", lambda e, c=c: e.scalar_tensor_tensor(
            self.poscol[:, 2:3], idxf[:, c:c + 1], float(KCAP), self.poscol[:, 2:3],
            ALU.mult, ALU.add))
        p.op("vector", lambda e, c=c: e.scalar_tensor_tensor(
            self.slot_f[:, c:c + 1], self.slot_f[:, c:c + 1], float(NE * KCAP),
            self.poscol[:, 2:3], ALU.mult, ALU.add))
        p.op("vector", lambda e, c=c: e.tensor_scalar(
            self.poscol[:, 2:3], self.poscol[:, 1:2], -1e7, 1e7, ALU.mult, ALU.add))
        p.op("vector", lambda e, c=c: e.tensor_tensor(
            self.slot_f[:, c:c + 1], self.slot_f[:, c:c + 1], self.poscol[:, 2:3],
            ALU.add))
        p.op("vector", lambda e, c=c: e.tensor_scalar(
            self.poscol[:, 2:3], self.poscol[:, 3:4], float(CAP1), None, ALU.is_ge))
        p.op("vector", lambda e, c=c: e.scalar_tensor_tensor(
            self.slot_f[:, c:c + 1], self.poscol[:, 2:3], 1e7, self.slot_f[:, c:c + 1],
            ALU.mult, ALU.add))
        p.op("vector", lambda e, c=c: e.tensor_copy(
            self.slot2_i[:, c:c + 1], self.slot_f[:, c:c + 1]), "sl")
        p.wait("vector", "cm")
        p.op("vector", lambda e: e.tensor_tensor(
            self.carry[:], self.carry[:], PS[0][0:1, 0:8], ALU.add), "cw")
    if self.emit_counts and not full:
        p.op("vector", lambda e: e.tensor_sub(self.cnt_s[:], self.carry[:], self.offrow[:]),
             "cnt")
        p.wait("sync", "cnt")
        p.dma("sync", lambda e: e.dma_start(out=self.outs["counts"], in_=self.cnt_s[:]), "cst")
        p.wait("sync", "cst")
        return
    p.dma("sync", lambda e: e.dma_start(out=self.idx_d, in_=self.c_senti.ap()), "sn")
    p.wait("gpsimd", "sn")
    p.wait("gpsimd", "sl")
    for c in range(NCH):
        p.dma("gpsimd", lambda e, c=c: e.indirect_dma_start(
            out=self.idx_d, out_offset=IndirectOffsetOnAxis(self.slot2_i[:, c:c + 1], 0),
            in_=self.tokid_s[:, c:c + 1], in_offset=None,
            bounds_check=self.bregs['slot'], oob_is_err=False), "sc")
    p.wait("sync", "sc")
    p.dma("sync", lambda e: e.dma_start(
        out=self.idxc64[:].rearrange("p (a n) -> p a n", n=1),
        in_=self.idx_d.rearrange("(p a) n -> p a n", p=128)), "ix")
    p.wait("gpsimd", "ix")
    p.wait("tensor", "cw")
    p.op("vector", lambda e: e.memset(self.xtmb[:], 0.0), "xz")
    p.wait("gpsimd", "xz")
    wl_marks = []

    def wload(eng, tag, mk):
        L = len(wl_marks)
        wb = (self.wbuf[:], self.wbufB)[L % 2]
        if L >= 2:
            s_, v_ = wl_marks[L - 2][1]
            p.wait(eng, s_, v_)
        p.dma(eng, mk(wb), f"{tag}{L % 2}")
        wl_marks.append([wb, None])
        return wb, f"{tag}{L % 2}"

    for ei in range(NE):
        for j in range(KCAP):
            p.wait("gpsimd", f"gst{j % 2}", None)
            p.op("gpsimd", lambda e, ei=ei, j=j: e.tensor_copy(
                self.idx1[:], self.idxc64[:, ei * KCAP + j:ei * KCAP + j + 1]), "ixc")
            p.dma("gpsimd", lambda e, ei=ei, j=j: e.indirect_dma_start(
                out=self.xtmb[:, j % 2, :], out_offset=None,
                in_=self.xhat_d,
                in_offset=IndirectOffsetOnAxis(self.idx1[:], 0),
                bounds_check=self.bregs['tok'], oob_is_err=False), "gth")
            p.wait("sync", "gth")
            p.dma("sync", lambda e, j=j: e.dma_start(
                out=self.ddisp_d[j * 128:(j + 1) * 128, :],
                in_=self.xtmb[:, j % 2, :]), f"gst{j % 2}")
        p.wait("sync", "gst0")
        p.wait("sync", "gst1")
        p.wait("sync", "w1m")
        for k in range(KC):
            p.dma("sync", lambda e, k=k: e.dma_start(
                out=self.v_dfm[:, k, :],
                in_=self.ddisp_d[:, k * 128:(k + 1) * 128], transpose=True), "dt")
        p.wait("sync", "he")
        p.dma("sync", lambda e, ei=ei: e.dma_start(
            out=self.b1c[:], in_=self.ap["l1f_b1T"][ei]), "dt")
        p.wait("tensor", "dt")
        if ei >= 1:
            p.wait("tensor", "ee", None)
        for fs in range(8):
            wb, wsem = wload("sync", "w1l", lambda wb, ei=ei, fs=fs:
                             (lambda e: e.dma_start(
                                 out=wb,
                                 in_=self.w1_bf[ei, :, fs * 512:(fs + 1) * 512].rearrange(
                                     "(k p) n -> p k n", p=128))))
            p.wait("tensor", wsem)
            for fc in range(4):
                slot = fs * 4 + fc
                bank = PS[slot % 2]
                if ei * 32 + slot >= 2:
                    p.wait("tensor", "he", bhe + ei * 32 + slot - 1)
                for k in range(KC):
                    self.mm(bank[:, :], wb[:, k, fc * 128:(fc + 1) * 128],
                            self.v_dfm[:, k, :],
                            start=(k == 0), stop=(k == KC - 1),
                            sem=("w1m" if k == KC - 1 else None))
                p.wait("scalar", "w1m")
                p.op("scalar", lambda e, slot=slot, bank=bank: e.activation(
                    self.v_hfm[:, slot, :], bank[:, :], AF.Relu,
                    bias=self.b1c[:, slot:slot + 1]), "he")
            wl_marks[-1][1] = ("w1m", p.base("w1m"))
        p.wait("tensor", "he")
        for mh in range(2):
            p.wait("sync", "em")
            p.dma("sync", lambda e, ei=ei, mh=mh: e.dma_start(
                out=self.b2sm[:], in_=self.b2_bf[ei:ei + 1, mh * 512:(mh + 1) * 512]),
                "b2l")
            for kg in range(4):
                wb, wsem = wload("gpsimd", "w2l", lambda wb, ei=ei, mh=mh, kg=kg:
                                 (lambda e: e.dma_start(
                                     out=wb,
                                     in_=self.w2_bf[ei, kg * 1024:(kg + 1) * 1024,
                                                    mh * 512:(mh + 1) * 512].rearrange(
                                         "(k p) n -> p k n", p=128))))
                p.wait("tensor", wsem)
                if kg == 0:
                    p.wait("tensor", "ee", None)
                    p.wait("tensor", "b2l")
                for tc in range(4):
                    for k in range(KC):
                        self.mm(PS[tc][:, :],
                                self.v_hfm[:, kg * 8 + k, tc * 128:(tc + 1) * 128],
                                wb[:, k, :],
                                start=(kg == 0 and k == 0), stop=False,
                                sem=("w2m" if k == KC - 1 else None))
                if kg == 3:
                    for tc in range(4):
                        self.mm(PS[tc][:, :], self.onesr_b[:], self.b2sm[:, 0:512],
                                start=False, stop=True, sem="em")
                wl_marks[-1][1] = ("w2m", p.base("w2m"))
            p.wait("scalar", "em")
            for tc in range(4):
                p.op("scalar", lambda e, tc=tc: e.activation(
                    self.v_eob[:, tc % 2, :], PS[tc][:, :], AF.Identity), "ee")
                p.wait("sync", "ee")
                p.dma("sync", lambda e, ei=ei, mh=mh, tc=tc: e.dma_start(
                    out=self.eo_d[ei * CAP1 + tc * 128:ei * CAP1 + (tc + 1) * 128,
                                  mh * 512:(mh + 1) * 512],
                    in_=self.v_eob[:, tc % 2, :]), f"eos{tc % 2}")
                p.wait("scalar", f"eos{tc % 2}")
    p.wait("gpsimd", "eos0")
    p.wait("gpsimd", "eos1")
    for c in range(NCH):
        p.op("vector", lambda e, c=c: e.memset(self.v_gath[:, c % 2, :], 0.0), "gz")
        p.wait("gpsimd", "gz")
        if c >= 2:
            p.wait("gpsimd", "cb", bcb + c - 1)
        p.dma("gpsimd", lambda e, c=c: e.indirect_dma_start(
            out=self.v_gath[:, c % 2, :], out_offset=None,
            in_=self.eo_d, in_offset=IndirectOffsetOnAxis(self.slot_i[:, c:c + 1], 0),
            bounds_check=self.bregs['slot'], oob_is_err=False), "gb")
        p.wait("vector", "gb")
        for s in range(2):
            p.op("vector", lambda e, c=c, s=s: e.scalar_tensor_tensor(
                self.XT[:, c, s * 512:(s + 1) * 512],
                self.v_gath[:, c % 2, s * 512:(s + 1) * 512],
                self.gsc[:, c:c + 1],
                self.XT[:, c, s * 512:(s + 1) * 512],
                ALU.mult, ALU.add), "cb" if s == 1 else None)
    p.wait("vector", "cb")
    p.barrier(name="bf")


def scr8b_oh(self):
    if not hasattr(self, "_oh_b"):
        self._oh_b = self.ctx.enter_context(self.nc.sbuf_tensor("oh_b", [128, NE], BF16))
    return self._oh_b[:]


MK.emit_l1ffn = emit_l1ffn
MK.scr8b_oh = scr8b_oh


_PROGRAM_CACHE = {}


def _get_program():
    if "nc" not in _PROGRAM_CACHE:
        mk = MK(["l0ffn_a", "l0attn", "l0ffn_b", "l1ffn_a", "l1attn", "l1ffn_b"])
        nc = mk.emit_all()
        _PROGRAM_CACHE["nc"] = nc
        _PROGRAM_CACHE["mk"] = mk
    return _PROGRAM_CACHE["nc"], _PROGRAM_CACHE["mk"]


def kernel(**inputs):
    from concourse.bass_utils import run_bass_kernel_spmd
    nc, mk = _get_program()
    zpri = np.zeros((4, NE), np.float32)
    zsel = np.zeros((4, 1), np.float32)
    in_maps = []
    for c in range(8):
        m = host_inputs(inputs, c // 2)
        m["priors"] = zpri
        m["selprev"] = zsel
        m = {k: v for k, v in m.items() if k in mk.ap}
        in_maps.append(m)
    res = run_bass_kernel_spmd(nc, in_maps, core_ids=list(range(8)))
    out = np.empty((4, 4096, 1024), dtype=np.float32)
    for c in range(8):
        h = c % 2
        out[c // 2, h * 2048:(h + 1) * 2048, :] = res.results[c]["xout"][h * 2048:(h + 1) * 2048, :]
    return out


# version 06
# speedup vs baseline: 1.6573x; 1.1660x over previous
import numpy as np
import contextlib
import sys
sys.path.insert(0, '/opt/trn_rl_repo')
import concourse.bass as bass
import concourse.mybir as mybir
from concourse.bass_core_rust import IndirectOffsetOnAxis

F32 = mybir.dt.float32
BF16 = mybir.dt.bfloat16
I32 = mybir.dt.int32
AF = mybir.ActivationFunctionType
ALU = mybir.AluOpType
AX = mybir.AxisListType

ENGINES = ("sync", "tensor", "vector", "scalar", "gpsimd")

D = 1024
T = 4096
NCH = 32
KC = 8
NE = 8
CAP1 = 512
KCAP = CAP1 // 128
CAPREF = 2560
H = 16
DH = 64
NSC = 8
EXPC = 18.0


class P:
    def __init__(self, nc):
        self.nc = nc
        self.q = {e: [] for e in ENGINES}
        self.sems = {}
        self.cnt = {}
        self.virt = {}
        self.lastw = {}
        self._ctx = contextlib.ExitStack()

    def sem(self, name):
        if name not in self.sems:
            self.sems[name] = self._ctx.enter_context(self.nc.semaphore(name))
            self.cnt[name] = 0
        return self.sems[name]

    CHAINED = ("vector", "scalar", "gpsimd")

    def op(self, eng, fn, sem=None, n=1, _dma=False):
        inc = 16 if _dma else 1
        if eng in self.CHAINED:
            chname = "chain_" + eng + ("d" if (_dma and eng == "gpsimd") else "")
            other = None
            if eng == "gpsimd":
                other = "chain_gpsimdd" if not _dma else "chain_gpsimd"
            assert not (_dma and eng in ("vector", "scalar")), "no DMAs on vector/scalar"
            ch = self.sem(chname)
            cv = self.cnt[chname]
            self.cnt[chname] = cv + inc
            if sem is not None:
                reg = self.virt.setdefault(sem, {"ch": chname, "marks": []})
                assert reg["ch"] == chname, f"sem {sem} has mixed producers"
                units = (reg["marks"][-1][0] if reg["marks"] else 0) + n
                reg["marks"].append((units, self.cnt[chname]))
            if other is not None and self.cnt.get(other, 0) > self.lastw.get((eng, other), 0):
                oc = self.sem(other)
                ov = self.cnt[other]
                self.lastw[(eng, other)] = ov
                self.q[eng].append(lambda e, s=oc, v=ov: e.wait_ge(s, v))
            def run(e, fn=fn, ch=ch, cv=cv, inc=inc):
                inst = fn(e)
                if cv > 0:
                    inst = inst._wait_ge(ch, cv)
                return inst.then_inc(ch, inc)
            self.q[eng].append(run)
            return None
        if sem is not None:
            assert sem not in self.virt, f"sem {sem} already virtual"
            s = self.sem(sem)
            self.cnt[sem] += n
            self.q[eng].append(lambda e, fn=fn, s=s, n=n: fn(e).then_inc(s, n))
            return self.cnt[sem]
        self.q[eng].append(lambda e, fn=fn: fn(e))
        return None

    def dma(self, eng, fn, sem=None):
        return self.op(eng, fn, sem, 16, _dma=True)

    def base(self, sem):
        if sem in self.virt:
            marks = self.virt[sem]["marks"]
            return marks[-1][0] if marks else 0
        return self.cnt.get(sem, 0)

    def wait(self, eng, sem, val=None):
        if sem in self.virt:
            reg = self.virt[sem]
            marks = reg["marks"]
            if not marks:
                return
            if val is None:
                tgt = marks[-1][1]
            else:
                tgt = None
                for units, chaincum in marks:
                    if units >= val:
                        tgt = chaincum
                        break
                if tgt is None:
                    tgt = marks[-1][1]
            chname = reg["ch"]
            producer_eng = chname.replace("chain_", "").rstrip("d") if chname.endswith("dd") is False else ""
            producer_eng = chname[6:-1] if chname.endswith("d") and chname != "chain_gpsimd" else chname[6:]
            if chname == "chain_gpsimdd":
                producer_eng = "gpsimd"
            ch = self.sem(chname)
            if eng == producer_eng and chname in ("chain_vector", "chain_scalar", "chain_gpsimd"):
                return
            self.q[eng].append(lambda e, s=ch, v=tgt: e.wait_ge(s, v))
            return
        s = self.sem(sem)
        v = self.cnt[sem] if val is None else val
        if v > 0:
            self.q[eng].append(lambda e, s=s, v=v: e.wait_ge(s, v))

    def set_barrier_tiles(self, tiles):
        self._btiles = tiles

    def barrier(self, engines=ENGINES, name="bar"):
        s = self.sem(name)
        tgt = self.cnt[name] + len(engines)
        self.cnt[name] = tgt
        for eng in engines:
            t = self._btiles.get(eng)
            if eng in ("sync", "tensor") or t is None:
                self.q[eng].append(lambda e, s=s: e.nop().then_inc(s, 1))
            else:
                if eng == "scalar":
                    self.op(eng, lambda e, t=t: e.activation(t, t, AF.Identity))
                else:
                    self.op(eng, lambda e, t=t: e.tensor_copy(t, t))
                chname = "chain_" + eng
                ch = self.sem(chname)
                cv = self.cnt[chname]
                self.q[eng].append(lambda e, ch=ch, cv=cv: e.wait_ge(ch, cv))
                self.q[eng].append(lambda e, s=s: e.sem_inc(s, 1))
        for eng in engines:
            self.q[eng].append(lambda e, s=s, v=tgt: e.wait_ge(s, v))

    def build(self):
        with self.nc.Block() as block:
            for eng in ENGINES:
                qe = self.q[eng]
                def body(handle, qe=qe):
                    for fn in qe:
                        fn(handle)
                getattr(block, eng)(body)


class MK:
    def __init__(self, stages, out_half=False, emit_counts=False):
        self.stages = stages
        self.out_half = out_half
        self.emit_counts = emit_counts
        nc = bass.Bass("TRN2", target_bir_lowering=False, debug=False, dynamic_dma_scratch_size=8192)
        self.nc = nc
        self.p = P(nc)
        ctx = self.p._ctx
        self.ctx = ctx
        sbuf = lambda *a: ctx.enter_context(nc.sbuf_tensor(*a))
        psum = lambda *a: ctx.enter_context(nc.psum_tensor(*a))

        ap = {}
        def din(nm, shape, dt=F32):
            ap[nm] = nc.dram_tensor(nm, shape, dt, kind="ExternalInput").ap()
        self.ap = ap

        st = set(stages)
        self.need_l0ffn = bool(st & {"l0ffn_a", "l0ffn_b"})
        self.need_l0attn = "l0attn" in st
        self.need_l1attn = "l1attn" in st
        self.need_l1ffn = bool(st & {"l1ffn_a", "l1ffn_b"})
        self.need_gate = self.need_l1ffn or emit_counts

        din("x0", [T, D])
        if self.need_l0ffn:
            din("l0r_wT", [D, NE]); din("l0r_b", [1, NE])
            din("l0e_wT", [NE, D, D]); din("l0e_b", [NE, D])
        if self.need_l0attn:
            din("l0a_wq", [D, D]); din("l0a_wk", [D, D])
            din("l0a_wv", [D, D]); din("l0a_wo", [D, D])
        if self.need_l1attn:
            din("l1a_wqkvT", [D, 3 * D]); din("l1a_wprojT", [D, D])
        if self.need_gate:
            din("l1f_wg", [D, NE])
        if self.need_l1ffn:
            din("l1f_w1", [NE, D, 4 * D]); din("l1f_b1T", [NE, 128, 32])
            din("l1f_w2", [NE, 4 * D, D]); din("l1f_b2", [NE, D])
            din("priors", [4, NE]); din("selprev", [4, 1])
        self.outs = {}
        rows = T // 2 if out_half else T
        self.outs["xout"] = nc.dram_tensor("xout", [rows, D], F32, kind="ExternalOutput").ap()
        if emit_counts:
            self.outs["counts"] = nc.dram_tensor("counts", [1, NE], F32, kind="ExternalOutput").ap()
        self.halfsel = None
        if out_half:
            din("halfsel", [1, 1], F32)

        dram = lambda nm, shape, dt: nc.dram_tensor(nm, shape, dt, kind="Internal").ap()
        self.xhat_d = dram("xhat_d", [T, D], BF16)
        if self.need_l0ffn:
            self.l0e_bf = dram("l0e_bf", [NE, D, D], BF16)
        if self.need_l0attn or self.need_l1attn:
            self.qfm_d = dram("qfm_d", [D, T], BF16)
            self.kfm_d = dram("kfm_d", [D, T], BF16)
            self.vtm_d = dram("vtm_d", [T, D], BF16)
        if self.need_l0attn:
            self.ktm_d = dram("ktm_d", [T, D], BF16)
            self.l0a_bf = dram("l0a_bf", [4, D, D], BF16)
        if self.need_l1attn:
            self.l1aq_bf = dram("l1aq_bf", [D, 3 * D], BF16)
            self.l1ap_bf = dram("l1ap_bf", [D, D], BF16)
        if self.need_l1ffn:
            self.w1_bf = dram("w1_bf", [NE, D, 4 * D], BF16)
            self.w2_bf = dram("w2_bf", [NE, 4 * D, D], BF16)
            self.b2_bf = dram("b2_bf", [NE, D], BF16)
            self.idx_d = dram("idx_d", [NE * CAP1, 1], I32)
            self.eo_d = dram("eo_d", [NE * CAP1, D], BF16)
            self.ddisp_d = dram("ddisp_d", [CAP1, D], BF16)

        it = nc.inline_tensor
        self.c_ident = it(np.eye(128, dtype=np.float32), "c_ident")
        self.c_ones_row = it(np.ones((1, 128), dtype=np.float32), "c_ones_row")
        self.c_ones_col = it(np.ones((128, 1), dtype=np.float32), "c_ones_col")
        self.c_ustrict = it(np.triu(np.ones((128, 128), dtype=np.float32), 1), "c_ustrict")
        self.c_iota8 = it(np.tile(np.arange(NE, dtype=np.float32), (128, 1)), "c_iota8")
        self.c_iotam = it(np.tile(np.arange(NE, dtype=np.float32) - 999.0, (128, 1)), "c_iotam")
        self.c_tokid = it(np.arange(T, dtype=np.int32).reshape(NCH, 128).T.copy(), "c_tokid")
        self.c_senti = it(np.full((NE * CAP1, 1), 1 << 29, dtype=np.int32), "c_senti")
        mt = np.zeros((128, 4, 512), dtype=np.float32)
        for _kc in range(4):
            mt[:, _kc, :] = (np.arange(512)[None, :] >=
                             (_kc * 128 + np.arange(128))[:, None])
        import ml_dtypes as _mld
        self.c_mtile = it(mt.reshape(128, 2048).astype(_mld.bfloat16), "c_mtile")

        self.XT = sbuf("XT", [128, NCH, D], F32)
        self.FM = sbuf("FM", [128, KC, T], BF16)
        import os as _os
        _wb = int(_os.environ.get("DEVK_WB", "512"))
        self.wbuf = sbuf("wbuf", [128, KC, _wb], BF16)
        self._xtmb_ded = sbuf("xtmb_d", [128, 2, D], BF16)
        self.lncol = sbuf("lncol", [128, 6, NCH], F32)
        self.gaux = sbuf("gaux", [128, 4, NCH], F32)
        self.rw8 = sbuf("rw8", [128, NCH, NE], F32)
        self.rwT1 = sbuf("rwT1", [8, 128], BF16)
        self.rwb1 = sbuf("rwb1", [128, NE], BF16)
        self.bexp_b = sbuf("bexp_b", [NE, D], BF16)
        self.oh_f = sbuf("oh_f", [128, NE], F32)
        self.scr8 = sbuf("scr8", [128, NE], F32)
        self.gsc = sbuf("gsc", [128, NCH], F32)
        self.slot_f = sbuf("slot_f", [128, NCH], F32)
        self.slot_i = sbuf("slot_i", [128, NCH], I32)
        self.slot2_i = sbuf("slot2_i", [128, NCH], I32)
        self.idxc64 = sbuf("idxc64", [128, NE * KCAP], I32)
        self.idx1 = sbuf("idx1", [128, 1], I32)
        self.poscol = sbuf("poscol", [128, 4], F32)
        self.posi = sbuf("posi", [128, 1], I32)
        self.carry = sbuf("carry", [1, NE], F32)
        self.offrow = sbuf("offrow", [1, NE], F32)
        self.offb = sbuf("offb", [128, NE], F32)
        self.selprev_s = sbuf("selprev_s", [4, 1], F32)
        self.priors_s = sbuf("priors_s", [4, NE], F32)
        self.epscol = sbuf("epscol", [128, 1], F32)
        self.negC = sbuf("negC", [128, 1], F32)
        self.onesr_b = sbuf("onesr_b", [1, 128], BF16)
        self.onesc_b = sbuf("onesc_b", [128, 1], BF16)
        self.onesr_f = sbuf("onesr_f", [1, 128], F32)
        self.onesc_f = sbuf("onesc_f", [128, 1], F32)
        self.ident_b = sbuf("ident_b", [128, 128], BF16)
        self.ustr_b = sbuf("ustr_b", [128, 128], BF16)
        self.iota8_f = sbuf("iota8_f", [128, NE], F32)
        self.big8_f = sbuf("big8_f", [128, NE], F32)
        self.iotam_f = sbuf("iotam_f", [128, NE], F32)
        self.tokid_s = sbuf("tokid_s", [128, NCH], I32)
        self.smallb = sbuf("smallb", [128, KC, NE], BF16)
        self.wgb = sbuf("wgb", [128, KC, NE], BF16)
        self.l0rb_b = sbuf("l0rb_b", [1, NE], BF16)
        self.b1c = sbuf("b1c", [128, 32], F32)
        self.b2sm = sbuf("b2sm", [1, 512], BF16)
        self.idxc = sbuf("idxc", [128, 8], I32)
        self.cnt_s = sbuf("cnt_s", [1, NE], F32)
        self.btile = sbuf("btile", [128, 4], F32)
        self.p.set_barrier_tiles({
            "vector": self.btile[0:1, 0:1],
            "scalar": self.btile[0:1, 1:2],
            "gpsimd": self.btile[0:1, 2:3],
        })

        self.PS = [psum(f"ps{i}", [128, 512], F32) for i in range(4)]
        self.PSB = psum("psb", [128, 1024], F32)
        self.PT = psum("pt", [128, 128], BF16)
        self.PSM = psum("psm", [128, 16], F32)

        if self._xtmb_ded is not None:
            self.xtmb = self._xtmb_ded[:]
        else:
            self.xtmb = self.wbuf[:].rearrange("p k n -> p (k n)")[:, 0:2048].rearrange(
                "p (b n) -> p b n", b=2)
        self.gl = self.PSB[:, 0:256].rearrange("p (n e) -> p n e", e=NE)
        fmf = self.FM[:].rearrange("p k t -> p (k t)")
        self.fmf = fmf
        def fv(off, n, spec=None, **kw):
            v = fmf[:, off:off + n]
            if spec:
                v = v.rearrange(spec, **kw)
            return v
        self.v_qf = fv(0, 4096, "p (k t) -> p k t", t=512)
        self.v_kf = fv(4096, 8192, "p (b k t) -> p b k t", b=2, t=512)
        self.v_vt = fv(12288, 8192, "p (b a o) -> p b a o", b=2, o=D)
        self.v_kt = fv(20480, 4096, "p (a o) -> p a o", o=D)
        self.v_af = fv(24576, 4096, "p (k t) -> p k t", t=512)
        self.v_at = fv(28672, 1024, "p (b t) -> p b t", t=512)
        self.v_ysb = fv(29696, 512)
        self.v_rrb = fv(30208, 512)
        self.statef = fv(30720, 2048).bitcast(F32).rearrange("p (k d) -> p k d", d=DH)
        self.stateb = fv(29696, 512).rearrange("p (k d) -> p k d", d=DH)
        self.fv = fv
        self.v_vt65 = fv(12288, 2080, "p (b a h w) -> p b a h w", b=2, a=4, h=4)
        self.v_dfm = fv(0, 4096, "p (k t) -> p k t", t=512)
        self.wbufB = fv(4096, 4096, "p (k t) -> p k t", t=512)
        self.mtile = fv(16384, 2048, "p (k t) -> p k t", t=512)
        self.v_hfm = fv(8192, 16384, "p (k t) -> p k t", t=512)
        self.v_gath = fv(24576, 2048, "p (b t) -> p b t", t=1024)
        self.v_eob = fv(26624, 1024, "p (b t) -> p b t", t=512)

    def mm(self, out, lhsT, rhs, start=True, stop=True, sem=None):
        return self.p.op("tensor", lambda e: e.matmul(out, lhsT, rhs, start=start, stop=stop), sem)

    def emit_prep(self):
        p = self.p
        def _mkregs(e):
            self.bregs = {"tok": e.to_reg(T - 1), "slot": e.to_reg(NE * CAP1 - 1),
                          "half": e.to_reg(T // 2 - 1)}
        p.q["gpsimd"].append(_mkregs)
        p.dma("sync", lambda e: e.dma_start(out=self.onesr_f[:], in_=self.c_ones_row.ap()), "ld0")
        p.dma("sync", lambda e: e.dma_start(out=self.onesc_f[:], in_=self.c_ones_col.ap()), "ld0")
        p.dma("gpsimd", lambda e: e.dma_start(out=self.ustr_b[:], in_=self.c_ustrict.ap()), "ld0g")
        p.dma("sync", lambda e: e.dma_start(out=self.iota8_f[:], in_=self.c_iota8.ap()), "ld0")
        p.dma("sync", lambda e: e.dma_start(out=self.iotam_f[:], in_=self.c_iotam.ap()), "ld0")
        p.dma("sync", lambda e: e.dma_start(out=self.tokid_s[:], in_=self.c_tokid.ap()), "ld0")
        p.dma("gpsimd", lambda e: e.dma_start(out=self.onesr_b[:], in_=self.c_ones_row.ap()), "ld0g")
        p.dma("gpsimd", lambda e: e.dma_start(out=self.onesc_b[:], in_=self.c_ones_col.ap()), "ld0g")
        p.dma("gpsimd", lambda e: e.dma_start(out=self.ident_b[:], in_=self.c_ident.ap()), "ld0g")
        p.op("vector", lambda e: e.memset(self.btile[:, 0:1], 0.0))
        p.op("scalar", lambda e: e.memzero(self.btile[:, 1:2]))
        p.op("gpsimd", lambda e: e.memset(self.btile[:, 2:3], 0.0))
        p.op("vector", lambda e: e.memset(self.epscol[:], 1e-5))
        p.op("vector", lambda e: e.memset(self.negC[:], -EXPC))
        p.op("vector", lambda e: e.memset(self.big8_f[:], 999.0))
        for c in range(NCH):
            p.dma("sync", lambda e, c=c: e.dma_start(
                out=self.XT[:, c, :], in_=self.ap["x0"][c * 128:(c + 1) * 128, :]), "ldx")
        if self.need_l1ffn:
            p.dma("sync", lambda e: e.dma_start(out=self.priors_s[:], in_=self.ap["priors"]), "ld0")
            p.dma("sync", lambda e: e.dma_start(out=self.selprev_s[:], in_=self.ap["selprev"]), "ld0")
        p.wait("sync", "ldx")
        p.wait("sync", "ld0")
        p.wait("gpsimd", "ld0g")
        p.barrier(name="prep")
        if self.need_l0ffn:
            p.dma("gpsimd", lambda e: e.dma_start(
                out=self.smallb[:],
                in_=self.ap["l0r_wT"].rearrange("(k p) e -> p k e", p=128)), "wsm")
            p.dma("gpsimd", lambda e: e.dma_start(out=self.l0rb_b[:], in_=self.ap["l0r_b"]), "wsm")
            p.dma("gpsimd", lambda e: e.dma_start(out=self.bexp_b[:], in_=self.ap["l0e_b"]), "wsm")
        if self.need_gate:
            p.dma("gpsimd", lambda e: e.dma_start(
                out=self.wgb[:],
                in_=self.ap["l1f_wg"].rearrange("(k p) e -> p k e", p=128)), "wsm2")
        if self.need_l1ffn:
            p.dma("gpsimd", lambda e: e.dma_start(out=self.b2_bf, in_=self.ap["l1f_b2"]), "wsm2")
        if self.need_l0ffn:
            for i in range(NE):
                p.dma("gpsimd", lambda e, i=i: e.dma_start(
                    out=self.l0e_bf[i], in_=self.ap["l0e_wT"][i]), "wl0e")
        if self.need_l0attn:
            for i, nm in enumerate(("l0a_wq", "l0a_wk", "l0a_wv", "l0a_wo")):
                p.dma("gpsimd", lambda e, i=i, nm=nm: e.dma_start(
                    out=self.l0a_bf[i], in_=self.ap[nm]), "wl0a")

    def emit_casts_late(self):
        if getattr(self, "_casts_late_done", False):
            return
        self._casts_late_done = True
        p = self.p
        if self.need_l1attn:
            p.dma("gpsimd", lambda e: e.dma_start(out=self.l1aq_bf, in_=self.ap["l1a_wqkvT"]), "wl1a")
            p.dma("gpsimd", lambda e: e.dma_start(out=self.l1ap_bf, in_=self.ap["l1a_wprojT"]), "wl1a")
        if self.need_l1ffn:
            for i in range(NE):
                p.dma("gpsimd", lambda e, i=i: e.dma_start(
                    out=self.w1_bf[i], in_=self.ap["l1f_w1"][i]), "ww1")
                p.dma("gpsimd", lambda e, i=i: e.dma_start(
                    out=self.w2_bf[i], in_=self.ap["l1f_w2"][i]), "ww2")

    def emit_ln(self, tag):
        p = self.p
        blt = [p.base("lt0"), p.base("lt1")]
        for c in range(NCH):
            p.op("vector", lambda e, c=c: e.reduce_sum(
                self.lncol[:, 0, c:c + 1], self.XT[:, c, :], axis=AX.X), "lv")
            p.op("scalar", lambda e, c=c: e.activation(
                self.PSB[:, :], self.XT[:, c, :], AF.Square,
                accum_out=self.lncol[:, 1, c:c + 1]), "ls")
        p.wait("vector", "ls")
        p.op("vector", lambda e: e.tensor_scalar(
            self.lncol[:, 2, :], self.lncol[:, 0, :], 1.0 / D, None, ALU.mult))
        p.op("vector", lambda e: e.tensor_tensor(
            self.lncol[:, 3, :], self.lncol[:, 2, :], self.lncol[:, 2, :], ALU.mult))
        p.op("vector", lambda e: e.scalar_tensor_tensor(
            self.lncol[:, 4, :], self.lncol[:, 1, :], 1.0 / D, self.lncol[:, 3, :],
            ALU.mult, ALU.subtract), "lv")
        p.wait("scalar", "lv")
        p.op("scalar", lambda e: e.activation(
            self.lncol[:, 5, :], self.lncol[:, 4, :], AF.Sqrt, bias=self.epscol[:]), "ls")
        p.wait("vector", "ls")
        p.op("vector", lambda e: e.reciprocal(self.lncol[:, 4, :], self.lncol[:, 5, :]))
        for c in range(NCH):
            if c >= 2:
                p.wait("vector", f"lt{c % 2}", blt[c % 2] + 16 * (c // 2))
            p.op("vector", lambda e, c=c: e.tensor_scalar(
                self.xtmb[:, c % 2, :], self.XT[:, c, :],
                self.lncol[:, 2, c:c + 1], self.lncol[:, 4, c:c + 1],
                ALU.subtract, ALU.mult), "ln")
            p.wait("sync", "ln")
            p.dma("sync", lambda e, c=c: e.dma_start(
                out=self.xhat_d[c * 128:(c + 1) * 128, :], in_=self.xtmb[:, c % 2, :]),
                f"lt{c % 2}")
        p.wait("sync", "lt0")
        p.wait("sync", "lt1")
        for k in range(KC):
            p.dma("sync", lambda e, k=k: e.dma_start(
                out=self.FM[:, k, :], in_=self.xhat_d[:, k * 128:(k + 1) * 128],
                transpose=True), "lf")
        p.wait("sync", "lf")
        p.barrier(name="b")

    def emit_l0ffn(self, tag):
        p = self.p
        PS, PSM = self.PS, self.PSM
        bge = p.base("ge"); bbbe = p.base("bbe"); bew = p.base("ew")
        p.wait("tensor", "wsm", 16 * 2)
        for c in range(NCH):
            if c >= 1:
                p.wait("tensor", "ge", bge + c)
            for k in range(KC):
                self.mm(PSM[:, 0:8], self.FM[:, k, c * 128:(c + 1) * 128],
                        self.smallb[:, k, :], start=(k == 0), stop=False)
            self.mm(PSM[:, 0:8], self.onesr_b[:], self.l0rb_b[:], start=False, stop=True,
                    sem="gm")
            p.wait("scalar", "gm")
            p.op("scalar", lambda e, c=c: e.activation(
                self.gl[:, c, :], PSM[:, 0:8], AF.Exp), "ge")
        p.wait("vector", "ge")
        p.op("vector", lambda e: e.tensor_reduce(
            self.gaux[:, 0, :].rearrange("p (n o) -> p n o", o=1), self.gl, AX.X, ALU.add))
        p.op("vector", lambda e: e.reciprocal(self.gaux[:, 1, :], self.gaux[:, 0, :]))
        for c in range(NCH):
            p.op("vector", lambda e, c=c: e.max(self.scr8[:], self.gl[:, c, :]))
            p.op("vector", lambda e, c=c: e.tensor_scalar(
                self.oh_f[:], self.gl[:, c, :], self.scr8[:, 1:2], None, ALU.is_ge))
            p.op("vector", lambda e, c=c: e.tensor_scalar(
                self.rw8[:, c, :], self.gl[:, c, :], self.gaux[:, 1, c:c + 1], 0.5,
                ALU.mult, ALU.mult))
            p.op("vector", lambda e, c=c: e.tensor_tensor(
                self.rw8[:, c, :], self.rw8[:, c, :], self.oh_f[:], ALU.mult))
        p.wait("tensor", "wsm")
        for c in range(NCH):
            p.op("vector", lambda e, c=c: e.tensor_copy(
                self.rwb1[:], self.rw8[:, c, :]), "gw")
            p.wait("tensor", "gw")
            p.op("tensor", lambda e, c=c: e.transpose(
                self.PT[0:8, :], self.rwb1[:], self.ident_b[:]), "bt")
            p.wait("vector", "bt")
            p.op("vector", lambda e: e.tensor_copy(self.rwT1[:], self.PT[0:8, :]), "bte")
            p.wait("tensor", "bte")
            for s in range(2):
                slot = c * 2 + s
                bank = PS[slot % 2]
                if slot >= 2:
                    p.wait("tensor", "bbe", bbbe + slot - 1)
                self.mm(bank[:, :], self.rwT1[:], self.bexp_b[:, s * 512:(s + 1) * 512],
                        sem="bbm")
                p.wait("vector", "bbm")
                p.op("vector", lambda e, c=c, s=s, bank=bank: e.tensor_tensor(
                    self.XT[:, c, s * 512:(s + 1) * 512], self.XT[:, c, s * 512:(s + 1) * 512],
                    bank[:, :], ALU.add), "bbe")
        p.wait("sync", "wl0e")
        for i in range(NE):
            for half in range(2):
                slot = i * 2 + half
                if slot >= 1:
                    p.wait("sync", "ew", bew + slot * NCH)
                p.dma("sync", lambda e, i=i, h=half: e.dma_start(
                    out=self.wbuf[:],
                    in_=self.l0e_bf[i, :, h * 512:(h + 1) * 512].rearrange(
                        "(k p) n -> p k n", p=128)), "el")
                p.wait("tensor", "el")
                for c in range(NCH):
                    pb = PS[2 + (c % 2)]
                    if c >= 2:
                        p.wait("tensor", "ew", bew + slot * NCH + c - 1)
                    for k in range(KC):
                        self.mm(pb[:, :], self.FM[:, k, c * 128:(c + 1) * 128],
                                self.wbuf[:, k, :], start=(k == 0), stop=(k == KC - 1),
                                sem=("em" if k == KC - 1 else None))
                    p.wait("vector", "em")
                    p.op("vector", lambda e, c=c, h=half, pb=pb, i=i: e.scalar_tensor_tensor(
                        self.XT[:, c, h * 512:(h + 1) * 512], pb[:, :],
                        self.rw8[:, c, i:i + 1],
                        self.XT[:, c, h * 512:(h + 1) * 512],
                        ALU.mult, ALU.add), "ew")
        p.wait("vector", "ew")
        p.barrier(name="bf")

    def emit_finish(self):
        p = self.p
        if self.out_half:
            raise NotImplementedError
        if not getattr(self, "_stores_done", False):
            for c in range(NCH):
                p.dma("sync", lambda e, c=c: e.dma_start(
                    out=self.outs["xout"][c * 128:(c + 1) * 128, :], in_=self.XT[:, c, :]), "fst")
        p.wait("sync", "fst")

    def emit_dump_dram(self, src_ap):
        p = self.p
        for c in range(NCH):
            p.dma("sync", lambda e, c=c: e.dma_start(
                out=self.xtmb[:, 0, :], in_=src_ap[c * 128:(c + 1) * 128, :]), "dmp1")
            p.wait("vector", "dmp1")
            p.op("vector", lambda e, c=c: e.tensor_copy(self.XT[:, c, :], self.xtmb[:, 0, :]), "dmp2")
            p.wait("sync", "dmp2")
            p.dma("sync", lambda e, c=c: e.dma_start(
                out=self.outs["xout"][c * 128:(c + 1) * 128, :], in_=self.XT[:, c, :]), "fst")
            p.wait("sync", "fst")
        p.wait("sync", "fst")

    def emit_dump_sbuf_f32(self, get_slice, nch=NCH):
        p = self.p
        for c in range(nch):
            p.dma("sync", lambda e, c=c: e.dma_start(
                out=self.outs["xout"][c * 128:(c + 1) * 128, :], in_=get_slice(c)), "fst")
        p.wait("sync", "fst")

    def emit_all(self):
        import os
        dump = os.environ.get("DEVK_DUMP", "")
        self.emit_prep()
        for s in self.stages:
            self.emit_ln(s)
            if dump == "xhat":
                self.emit_dump_dram(self.xhat_d)
                self.p.build()
                return self.nc
            if dump == "fm":
                xv = self.outs["xout"].rearrange("(k a p) d -> (k) (a p d)", k=KC, a=4)
                for k in range(KC):
                    self.p.dma("gpsimd", lambda e, k=k: e.dma_start(
                        out=xv[k:k + 1, :].rearrange("a (p t) -> (a p) t", p=128),
                        in_=self.FM[:, k, :]), "fst")
                self.p.wait("gpsimd", "fst")
                self.p.build()
                return self.nc
            if s in ("l0ffn_a", "l0ffn_b"):
                self.emit_l0ffn(s)
            elif s == "l0attn":
                self.emit_casts_late()
                self.emit_l0attn(s)
            elif s == "l1attn":
                self.emit_casts_late()
                self.emit_l1attn(s)
            elif s in ("l1ffn_a", "l1ffn_b"):
                self.emit_l1ffn(s, final=(s == self.stages[-1]))
            elif s == "l1gate":
                self.emit_l1ffn(s, full=False)
        self.emit_finish()
        self.p.build()
        return self.nc


def host_inputs(d, batch):
    m = {"x0": np.ascontiguousarray(np.asarray(d["L0_attn_X"])[batch])}
    m["l0r_wT"] = np.ascontiguousarray(np.asarray(d["L0_ffn_W_router"]).T)
    m["l0r_b"] = np.ascontiguousarray(np.asarray(d["L0_ffn_b_router"])[None, :])
    m["l0e_wT"] = np.ascontiguousarray(np.asarray(d["L0_ffn_W_experts"]).transpose(0, 2, 1))
    m["l0e_b"] = np.ascontiguousarray(np.asarray(d["L0_ffn_b_experts"]))
    m["l0a_wq"] = np.ascontiguousarray(np.asarray(d["L0_attn_Wq"]))
    m["l0a_wk"] = np.ascontiguousarray(np.asarray(d["L0_attn_Wk"]))
    m["l0a_wv"] = np.ascontiguousarray(np.asarray(d["L0_attn_Wv"]))
    m["l0a_wo"] = np.ascontiguousarray(np.asarray(d["L0_attn_Wo"]))
    m["l1a_wqkvT"] = np.ascontiguousarray(np.asarray(d["L1_attn_W_qkv"]).T)
    m["l1a_wprojT"] = np.ascontiguousarray(np.asarray(d["L1_attn_W_proj"]).T)
    m["l1f_wg"] = np.ascontiguousarray(np.asarray(d["L1_ffn_wg"]))
    m["l1f_w1"] = np.ascontiguousarray(np.asarray(d["L1_ffn_w1"]))
    m["l1f_b1T"] = np.ascontiguousarray(
        np.asarray(d["L1_ffn_b1"]).reshape(8, 32, 128).transpose(0, 2, 1))
    m["l1f_w2"] = np.ascontiguousarray(np.asarray(d["L1_ffn_w2"]))
    m["l1f_b2"] = np.ascontiguousarray(np.asarray(d["L1_ffn_b2"]))
    return m


def _proj_fm_out(self, tag, wsrc_d, col0, dst_d):
    p = self.p
    PS = self.PS
    bpae = p.base("pae"); bpas = [p.base("pas0"), p.base("pas1")]
    for oc in range(8):
        p.wait("sync", "pam")
        p.dma("sync", lambda e, oc=oc: e.dma_start(
            out=self.wbuf[:, :, 0:128],
            in_=wsrc_d[:, col0 + oc * 128:col0 + (oc + 1) * 128].rearrange(
                "(k p) n -> p k n", p=128)), "pal")
        p.wait("tensor", "pal")
        for ts in range(8):
            slot = oc * 8 + ts
            bank = PS[slot % 2]
            if slot >= 2:
                p.wait("tensor", "pae", bpae + slot - 1)
            for k in range(KC):
                self.mm(bank[:, :], self.wbuf[:, k, 0:128],
                        self.FM[:, k, ts * 512:(ts + 1) * 512],
                        start=(k == 0), stop=(k == KC - 1),
                        sem=("pam" if k == KC - 1 else None))
            p.wait("scalar", "pam")
            if slot >= 2:
                p.wait("scalar", f"pas{slot % 2}", bpas[slot % 2] + 16 * (slot // 2))
            p.op("scalar", lambda e, slot=slot, bank=bank: e.activation(
                self.xtmb[:, slot % 2, 0:512], bank[:, :], AF.Identity), "pae")
            p.wait("sync", "pae")
            p.dma("sync", lambda e, oc=oc, ts=ts, slot=slot: e.dma_start(
                out=dst_d[oc * 128:(oc + 1) * 128, ts * 512:(ts + 1) * 512],
                in_=self.xtmb[:, slot % 2, 0:512]), f"pas{slot % 2}")
    p.wait("sync", "pam")
    p.wait("sync", "pas0")
    p.wait("sync", "pas1")


def _proj_tm_out(self, tag, wsrc_d, col0, width, dst_d):
    p = self.p
    PS = self.PS
    bpte = p.base("pte"); bpts = [p.base("pts0"), p.base("pts1")]
    for half in range(width // 512):
        p.wait("sync", "ptm")
        p.dma("sync", lambda e, half=half: e.dma_start(
            out=self.wbuf[:],
            in_=wsrc_d[:, col0 + half * 512:col0 + (half + 1) * 512].rearrange(
                "(k p) n -> p k n", p=128)), "ptl")
        p.wait("tensor", "ptl")
        for c in range(NCH):
            slot = half * NCH + c
            bank = PS[2 + slot % 2]
            if slot >= 2:
                p.wait("tensor", "pte", bpte + slot - 1)
            for k in range(KC):
                self.mm(bank[:, :], self.FM[:, k, c * 128:(c + 1) * 128],
                        self.wbuf[:, k, :], start=(k == 0), stop=(k == KC - 1),
                        sem=("ptm" if k == KC - 1 else None))
            p.wait("scalar", "ptm")
            if slot >= 2:
                p.wait("scalar", f"pts{slot % 2}", bpts[slot % 2] + 16 * (slot // 2))
            p.op("scalar", lambda e, slot=slot, bank=bank: e.activation(
                self.xtmb[:, slot % 2, 512:1024], bank[:, :], AF.Identity), "pte")
            p.wait("sync", "pte")
            p.dma("sync", lambda e, c=c, half=half, slot=slot: e.dma_start(
                out=dst_d[c * 128:(c + 1) * 128, half * 512:(half + 1) * 512],
                in_=self.xtmb[:, slot % 2, 512:1024]), f"pts{slot % 2}")
    p.wait("sync", "ptm")
    p.wait("sync", "pts0")
    p.wait("sync", "pts1")


def emit_l0attn(self, tag):
    p = self.p
    PS, PSB = self.PS, self.PSB
    bse = p.base("se"); bye = p.base("ye"); bsup = p.base("sup")
    bavm = p.base("avm"); bpje = p.base("pje")
    p.wait("sync", "wl0a")
    _proj_fm_out(self, "q" + tag, self.l0a_bf[0], 0, self.qfm_d)
    _proj_fm_out(self, "k" + tag, self.l0a_bf[1], 0, self.kfm_d)
    _proj_tm_out(self, "v" + tag, self.l0a_bf[2], 0, D, self.vtm_d)
    _proj_tm_out(self, "kt" + tag, self.l0a_bf[1], 0, D, self.ktm_d)
    p.barrier(name="ba")
    p.dma("sync", lambda e: e.dma_start(
        out=self.mtile[:].rearrange("p k t -> p (k t)"), in_=self.c_mtile.ap()), "mtl")
    p.wait("vector", "mtl")
    p.op("vector", lambda e: e.memset(self.statef, 0.0))
    p.op("vector", lambda e: e.memset(self.stateb, 0.0), "stz")
    p.wait("tensor", "stz")
    bmsk = p.base("msk")
    g_se = g_ye = g_sup = g_av = g_pje = g_msk = 0
    for sc in range(NSC):
        if sc >= 1:
            p.wait("sync", "pjm")
            p.wait("tensor", "pje")
        p.dma("sync", lambda e, sc=sc: e.dma_start(
            out=self.v_qf, in_=self.qfm_d[:, sc * 512:(sc + 1) * 512].rearrange(
                "(k p) t -> p k t", p=128)), "bl")
        p.dma("sync", lambda e, sc=sc: e.dma_start(
            out=self.v_kf[:, 0], in_=self.kfm_d[:, sc * 512:(sc + 1) * 512].rearrange(
                "(k p) t -> p k t", p=128)), "bl")
        p.dma("sync", lambda e, sc=sc: e.dma_start(
            out=self.v_vt[:, 0], in_=self.vtm_d[sc * 512:(sc + 1) * 512, :].rearrange(
                "(a p) o -> p a o", p=128)), "bl")
        p.dma("sync", lambda e, sc=sc: e.dma_start(
            out=self.v_kt, in_=self.ktm_d[sc * 512:(sc + 1) * 512, :].rearrange(
                "(a p) o -> p a o", p=128)), "bl")
        p.wait("tensor", "bl")
        p.wait("tensor", "sbu")
        for h in range(H):
            hb = (h % 2) * 64
            hk = h // 2
            gh = sc * H + h
            psy = PS[h % 2]
            if gh >= 2:
                p.wait("tensor", "ye", bye + gh - 1)
            self.mm(psy[0:64, :], self.stateb[hb:hb + 64, hk, :],
                    self.v_qf[hb:hb + 64, hk, :], start=True, stop=False)
            pend = None
            for kc in range(4):
                gs = g_se
                g_se += 1
                pss = PS[2 + gs % 2]
                if gs >= 2:
                    p.wait("tensor", "se", bse + gs - 1)
                self.mm(pss[:, :],
                        self.v_kf[hb:hb + 64, 0, hk, kc * 128:(kc + 1) * 128],
                        self.v_qf[hb:hb + 64, hk, :], start=True, stop=True,
                        sem="sm")
                p.wait("scalar", "sm")
                if g_av >= 2:
                    p.wait("scalar", "avm", bavm + g_av - 1)
                p.op("scalar", lambda e, gs=gs, pss=pss: e.activation(
                    self.v_at[:, gs % 2, :], pss[:, :], AF.Identity), "se")
                p.wait("vector", "se")
                p.op("vector", lambda e, kc=kc, gs=gs: e.tensor_tensor(
                    self.v_at[:, gs % 2, :], self.v_at[:, gs % 2, :],
                    self.mtile[:, kc, :], ALU.mult), "msk")
                g_msk += 1
                mi = g_msk
                g_av += 1
                if pend is not None:
                    pend()
                def _pend(gs=gs, mi=mi, psy=psy, h=h, kc=kc):
                    p.wait("tensor", "msk", bmsk + mi)
                    self.mm(psy[0:64, :], self.v_vt[:, 0, kc, h * 64:(h + 1) * 64],
                            self.v_at[:, gs % 2, :], start=False, stop=(kc == 3),
                            sem="avm")
                pend = _pend
            if gh >= 1:
                p.wait("tensor", "sup", bsup + gh)
            for kc in range(4):
                self.mm(PSB[0:64, 0:64], self.v_kt[:, kc, h * 64:(h + 1) * 64],
                        self.v_vt[:, 0, kc, h * 64:(h + 1) * 64],
                        start=(kc == 0), stop=(kc == 3),
                        sem=("sum" if kc == 3 else None))
            pend()
            p.wait("scalar", "avm")
            p.op("scalar", lambda e, hb=hb, hk=hk, psy=psy: e.activation(
                self.v_af[hb:hb + 64, hk, :], psy[0:64, :], AF.Identity), "ye")
            p.wait("vector", "sum")
            p.op("vector", lambda e, hb=hb, hk=hk: e.tensor_tensor(
                self.statef[hb:hb + 64, hk, :], self.statef[hb:hb + 64, hk, :],
                PSB[0:64, 0:64], ALU.add), "sup")
            p.op("vector", lambda e, hb=hb, hk=hk: e.tensor_copy(
                self.stateb[hb:hb + 64, hk, :], self.statef[hb:hb + 64, hk, :]),
                "sbu")
        p.wait("tensor", "ye")
        p.wait("tensor", "se")
        for s in range(2):
            p.wait("sync", "pjm")
            p.dma("sync", lambda e, s=s: e.dma_start(
                out=self.wbuf[:],
                in_=self.l0a_bf[3][:, s * 512:(s + 1) * 512].rearrange(
                    "(k p) n -> p k n", p=128)), "pjl")
            p.wait("tensor", "pjl")
            for tc in range(4):
                gp = g_pje
                g_pje += 1
                bank = PS[2 + gp % 2]
                if gp >= 2:
                    p.wait("tensor", "pje", bpje + gp - 1)
                for k in range(KC):
                    self.mm(bank[:, :], self.v_af[:, k, tc * 128:(tc + 1) * 128],
                            self.wbuf[:, k, :], start=(k == 0), stop=(k == KC - 1),
                            sem=("pjm" if k == KC - 1 else None))
                p.wait("vector", "pjm")
                p.op("vector", lambda e, sc=sc, tc=tc, s=s, bank=bank: e.tensor_tensor(
                    self.XT[:, sc * 4 + tc, s * 512:(s + 1) * 512],
                    self.XT[:, sc * 4 + tc, s * 512:(s + 1) * 512],
                    bank[:, :], ALU.add), "pje")
    p.wait("vector", "pje")
    p.barrier(name="bb")


MK.emit_l0attn = emit_l0attn


def emit_l1attn(self, tag):
    p = self.p
    PS, PSB = self.PS, self.PSB
    bse = p.base("se"); bavs = p.base("avs"); bpje = p.base("pje")
    p.wait("sync", "wl1a")
    _proj_fm_out(self, "q" + tag, self.l1aq_bf, 0, self.qfm_d)
    _proj_fm_out(self, "k" + tag, self.l1aq_bf, D, self.kfm_d)
    _proj_tm_out(self, "v" + tag, self.l1aq_bf, 2 * D, D, self.vtm_d)
    p.barrier(name="ba")
    p.dma("sync", lambda e: e.dma_start(
        out=self.mtile[:].rearrange("p k t -> p (k t)"), in_=self.c_mtile.ap()), "mtl")
    p.wait("vector", "mtl")
    p.op("vector", lambda e: e.memset(self.v_vt65[:, :, :, :, 64:65], 1.0), "vo")
    p.wait("tensor", "vo")
    rr32 = self.statef[:].rearrange("p a b -> p (a b)")[0:1, 0:512]
    bmsk = p.base("msk")
    v_qfB = self.fv(20480, 4096, "p (k t) -> p k t", t=512)
    qbufs = (self.v_qf, v_qfB)
    g_se = g_pje = g_kl = g_msk = 0
    for sc in range(NSC):
        qf = qbufs[sc % 2]
        if sc >= 1:
            p.wait("sync", "pjm")
            p.wait("tensor", "pje")
        else:
            p.dma("sync", lambda e, qf=qf: e.dma_start(
                out=qf, in_=self.qfm_d[:, 0:512].rearrange(
                    "(k p) t -> p k t", p=128)), "bl")
        p.wait("tensor", "bl")
        for g in range(4):
            p.wait("tensor", "yse")
            for ksc in range(sc + 1):
                par = g_kl % 2
                g_kl += 1
                if g_kl >= 3:
                    p.wait("sync", "sm", None)
                p.dma("sync", lambda e, g=g, ksc=ksc, par=par: e.dma_start(
                    out=self.v_kf[:, par, 0:2, :],
                    in_=self.kfm_d[256 * g:256 * (g + 1),
                                    ksc * 512:(ksc + 1) * 512].rearrange(
                        "(k p) t -> p k t", p=128)), f"kl{par}")
            
                for kc4 in range(4):
                    p.dma("sync", lambda e, g=g, ksc=ksc, par=par, kc4=kc4: e.dma_start(
                        out=self.v_vt65[:, par, kc4, :, 0:64],
                        in_=self.vtm_d[ksc * 512 + kc4 * 128:ksc * 512 + (kc4 + 1) * 128,
                                        256 * g:256 * (g + 1)].rearrange(
                            "p (h w) -> p h w", w=64)), f"vl{par}")
                p.wait("tensor", f"kl{par}")
                p.wait("tensor", f"vl{par}")
                pend = None
                for hi in range(4):
                    h = 4 * g + hi
                    hb = (h % 2) * 64
                    hkk = hi // 2
                    psy = PS[hi]
                    for kc in range(4):
                        gs = g_se
                        g_se += 1
                        pss = PSB[:, (gs % 2) * 512:(gs % 2 + 1) * 512]
                        if gs >= 2:
                            p.wait("tensor", "se", bse + gs - 1)
                        self.mm(pss, self.v_kf[hb:hb + 64, par, hkk,
                                                kc * 128:(kc + 1) * 128],
                                qf[hb:hb + 64, h // 2, :],
                                start=True, stop=True, sem="sm")
                        p.wait("scalar", "sm")
                        if g_se >= 3:
                            p.wait("scalar", "avs", bavs + g_se - 2)
                        p.op("scalar", lambda e, gs=gs, pss=pss: e.activation(
                            self.v_at[:, gs % 2, :], pss, AF.Exp,
                            bias=self.negC[:], scale=0.125), "se")
                        mi = None
                        if ksc == sc:
                            p.wait("vector", "se")
                            p.op("vector", lambda e, kc=kc, gs=gs: e.tensor_tensor(
                                self.v_at[:, gs % 2, :], self.v_at[:, gs % 2, :],
                                self.mtile[:, kc, :], ALU.mult), "msk")
                            g_msk += 1
                            mi = g_msk
                        if pend is not None:
                            pend()
                        def _pend(gs=gs, mi=mi, psy=psy, hi=hi, kc=kc, par=par,
                                  first=(ksc == 0 and kc == 0),
                                  last=(ksc == sc and kc == 3)):
                            if mi is not None:
                                p.wait("tensor", "msk", bmsk + mi)
                            else:
                                p.wait("tensor", "se", bse + gs + 1)
                            self.mm(psy[0:65, :],
                                    self.v_vt65[:, par, kc, hi, :],
                                    self.v_at[:, gs % 2, :], start=first,
                                    stop=last, sem="avs")
                        pend = _pend
                pend()
            for hi in range(4):
                h = 4 * g + hi
                hb = (h % 2) * 64
                psy = PS[hi]
                p.wait("vector", "avs")
                p.op("vector", lambda e, psy=psy, rr32=rr32: e.reciprocal(
                    rr32, psy[64:65, :]))
                p.op("vector", lambda e, rr32=rr32: e.tensor_copy(
                    self.v_rrb[0:1, :], rr32), "rrb")
                p.wait("tensor", "rrb")
                p.wait("tensor", "se")
                self.mm(PSB[:, 0:512], self.onesr_b[:], self.v_rrb[0:1, 0:512],
                        start=True, stop=True, sem="bcm")
                p.wait("scalar", "bcm")
                p.op("scalar", lambda e, psy=psy: e.activation(
                    self.v_ysb[0:64, :], psy[0:64, :], AF.Identity), "yse")
                p.wait("vector", "yse")
                p.op("vector", lambda e, hb=hb, h=h: e.tensor_tensor(
                    self.v_af[hb:hb + 64, h // 2, :], self.v_ysb[0:64, :],
                    PSB[0:64, 0:512], ALU.mult), "afw")
                p.wait("tensor", "afw")
        p.wait("tensor", "afw")
        if sc + 1 < NSC:
            p.wait("sync", "sm", None)
            p.dma("sync", lambda e, sc=sc: e.dma_start(
                out=qbufs[(sc + 1) % 2],
                in_=self.qfm_d[:, (sc + 1) * 512:(sc + 2) * 512].rearrange(
                    "(k p) t -> p k t", p=128)), "bl")
        for s in range(2):
            p.wait("sync", "pjm")
            p.dma("sync", lambda e, s=s: e.dma_start(
                out=self.wbuf[:],
                in_=self.l1ap_bf[:, s * 512:(s + 1) * 512].rearrange(
                    "(k p) n -> p k n", p=128)), "pjl")
            p.wait("tensor", "pjl")
            for tc in range(4):
                gp = g_pje
                g_pje += 1
                bank = PS[2 + gp % 2]
                if gp >= 2:
                    p.wait("tensor", "pje", bpje + gp - 1)
                for k in range(KC):
                    self.mm(bank[:, :], self.v_af[:, k, tc * 128:(tc + 1) * 128],
                            self.wbuf[:, k, :], start=(k == 0), stop=(k == KC - 1),
                            sem=("pjm" if k == KC - 1 else None))
                p.wait("vector", "pjm")
                p.op("vector", lambda e, sc=sc, tc=tc, s=s, bank=bank: e.tensor_tensor(
                    self.XT[:, sc * 4 + tc, s * 512:(s + 1) * 512],
                    self.XT[:, sc * 4 + tc, s * 512:(s + 1) * 512],
                    bank[:, :], ALU.add), "pje")
    p.wait("vector", "pje")
    p.barrier(name="bb")


MK.emit_l1attn = emit_l1attn


def emit_l1ffn(self, tag, full=True, final=False):
    p = self.p
    PS, PSB, PSM = self.PS, self.PSB, self.PSM
    gl = self.gl
    bge = p.base("ge"); bhe = p.base("he"); bcb = p.base("cb")
    p.wait("tensor", "wsm2")
    for c in range(NCH):
        if c >= 1:
            p.wait("tensor", "ge", bge + c)
        for k in range(KC):
            self.mm(PSM[:, 0:8], self.FM[:, k, c * 128:(c + 1) * 128],
                    self.wgb[:, k, :], start=(k == 0), stop=(k == KC - 1),
                    sem=("gm" if k == KC - 1 else None))
        p.wait("scalar", "gm")
        p.op("scalar", lambda e, c=c: e.activation(
            gl[:, c, :], PSM[:, 0:8], AF.Exp), "ge")
    p.wait("vector", "ge")
    p.op("vector", lambda e: e.tensor_reduce(
        self.gaux[:, 0, :].rearrange("p (n o) -> p n o", o=1), gl, AX.X, ALU.add))
    p.op("vector", lambda e: e.reciprocal(self.gaux[:, 1, :], self.gaux[:, 0, :]))
    idxf = self.rw8[:, :, 0]
    m1f = self.rw8[:, :, 1]
    for c in range(NCH):
        p.op("vector", lambda e, c=c: e.max(self.scr8[:], gl[:, c, :]))
        p.op("vector", lambda e, c=c: e.tensor_copy(m1f[:, c:c + 1], self.scr8[:, 0:1]))
        p.op("vector", lambda e, c=c: e.tensor_scalar(
            self.oh_f[:], gl[:, c, :], self.scr8[:, 0:1], None, ALU.is_ge))
        p.op("vector", lambda e: e.tensor_tensor(
            self.scr8[:], self.oh_f[:], self.iotam_f[:], ALU.mult))
        p.op("vector", lambda e: e.tensor_scalar(
            self.scr8[:], self.scr8[:], 999.0, None, ALU.add))
        p.op("vector", lambda e, c=c: e.tensor_reduce(
            idxf[:, c:c + 1].rearrange("p (n o) -> p n o", o=1),
            self.scr8[:].rearrange("p (n e) -> p n e", n=1), AX.X, ALU.min))
    p.op("vector", lambda e: e.tensor_tensor(
        self.gaux[:, 3, :], m1f[:], self.gaux[:, 1, :], ALU.mult), "gg")
    p.wait("tensor", "gg")
    if full:
        self.mm(PSM[0:1, 0:8], self.selprev_s[:], self.priors_s[:], sem="of")
        p.wait("vector", "of")
        p.op("vector", lambda e: e.tensor_copy(self.offrow[:], PSM[0:1, 0:8]))
        p.op("vector", lambda e: e.tensor_copy(self.carry[:], PSM[0:1, 0:8]), "cw")
        p.wait("tensor", "cw")
        self.mm(PSM[:, 8:16], self.onesr_f[:], self.offrow[:], sem="ob")
        p.wait("vector", "ob")
        p.op("vector", lambda e: e.tensor_copy(self.offb[:], PSM[:, 8:16]), "obs")
    else:
        p.op("vector", lambda e: e.memset(self.offrow[:], 0.0))
        p.op("vector", lambda e: e.memset(self.offb[:], 0.0))
        p.op("vector", lambda e: e.memset(self.carry[:], 0.0), "cw")
        p.wait("tensor", "cw")
    for c in range(NCH):
        p.op("vector", lambda e, c=c: e.tensor_scalar(
            self.oh_f[:], self.iota8_f[:], idxf[:, c:c + 1], None, ALU.is_equal))
        p.op("vector", lambda e: e.tensor_copy(self.scr8b_oh(), self.oh_f[:]), "ohb")
        p.wait("tensor", "ohb")
        p.wait("tensor", "pe", None)
        p.wait("tensor", "cw", None)
        self.mm(PSM[:, 0:8], self.ustr_b[:], self.scr8b_oh(), start=True, stop=False)
        self.mm(PSM[:, 0:8], self.onesr_f[:], self.carry[:], start=False, stop=True,
                sem="pm")
        self.mm(PS[0][0:1, 0:8], self.onesc_f[:], self.oh_f[:], start=True, stop=True,
                sem="cm")
        p.wait("vector", "pm")
        p.op("vector", lambda e: e.tensor_tensor(
            self.scr8[:], PSM[:, 0:8], self.oh_f[:], ALU.mult))
        p.op("vector", lambda e: e.tensor_reduce(
            self.poscol[:, 0:1].rearrange("p (n o) -> p n o", o=1),
            self.scr8[:].rearrange("p (n e) -> p n e", n=1), AX.X, ALU.add), "pe")
        p.op("vector", lambda e, c=c: e.tensor_scalar(
            self.poscol[:, 1:2], self.poscol[:, 0:1], float(CAPREF), None, ALU.is_lt))
        p.op("vector", lambda e, c=c: e.scalar_tensor_tensor(
            self.gsc[:, c:c + 1], self.gaux[:, 3, c:c + 1], 0.5, self.poscol[:, 1:2],
            ALU.mult, ALU.mult))
        p.op("vector", lambda e: e.tensor_tensor(
            self.scr8[:], self.offb[:], self.oh_f[:], ALU.mult))
        p.op("vector", lambda e: e.tensor_reduce(
            self.poscol[:, 2:3].rearrange("p (n o) -> p n o", o=1),
            self.scr8[:].rearrange("p (n e) -> p n e", n=1), AX.X, ALU.add))
        p.op("vector", lambda e: e.tensor_sub(
            self.poscol[:, 3:4], self.poscol[:, 0:1], self.poscol[:, 2:3]))
        p.op("vector", lambda e, c=c: e.scalar_tensor_tensor(
            self.slot_f[:, c:c + 1], idxf[:, c:c + 1], float(CAP1), self.poscol[:, 3:4],
            ALU.mult, ALU.add))
        p.op("vector", lambda e: e.tensor_scalar(
            self.poscol[:, 2:3], self.poscol[:, 3:4], float(CAP1), None, ALU.is_ge))
        p.op("vector", lambda e, c=c: e.scalar_tensor_tensor(
            self.slot_f[:, c:c + 1], self.poscol[:, 2:3], 1e7, self.slot_f[:, c:c + 1],
            ALU.mult, ALU.add))
        p.op("vector", lambda e, c=c: e.tensor_scalar(
            self.poscol[:, 2:3], self.poscol[:, 1:2], -1e7, 1e7, ALU.mult, ALU.add))
        p.op("vector", lambda e, c=c: e.tensor_tensor(
            self.slot_f[:, c:c + 1], self.slot_f[:, c:c + 1], self.poscol[:, 2:3],
            ALU.add))
        p.op("vector", lambda e, c=c: e.tensor_copy(
            self.slot_i[:, c:c + 1], self.slot_f[:, c:c + 1]), "sl")
        p.op("vector", lambda e, c=c: e.tensor_copy(
            self.posi[:, 0:1], self.poscol[:, 3:4]))
        p.op("vector", lambda e: e.tensor_scalar(
            self.posi[:, 0:1], self.posi[:, 0:1], 127, None, ALU.bitwise_and))
        p.op("vector", lambda e, c=c: e.tensor_copy(
            self.slot_f[:, c:c + 1], self.posi[:, 0:1]))
        p.op("vector", lambda e, c=c: e.tensor_sub(
            self.poscol[:, 2:3], self.poscol[:, 3:4], self.slot_f[:, c:c + 1]))
        p.op("vector", lambda e: e.tensor_scalar(
            self.poscol[:, 2:3], self.poscol[:, 2:3], 1.0 / 128.0, None, ALU.mult))
        p.op("vector", lambda e, c=c: e.scalar_tensor_tensor(
            self.poscol[:, 2:3], idxf[:, c:c + 1], float(KCAP), self.poscol[:, 2:3],
            ALU.mult, ALU.add))
        p.op("vector", lambda e, c=c: e.scalar_tensor_tensor(
            self.slot_f[:, c:c + 1], self.slot_f[:, c:c + 1], float(NE * KCAP),
            self.poscol[:, 2:3], ALU.mult, ALU.add))
        p.op("vector", lambda e, c=c: e.tensor_scalar(
            self.poscol[:, 2:3], self.poscol[:, 1:2], -1e7, 1e7, ALU.mult, ALU.add))
        p.op("vector", lambda e, c=c: e.tensor_tensor(
            self.slot_f[:, c:c + 1], self.slot_f[:, c:c + 1], self.poscol[:, 2:3],
            ALU.add))
        p.op("vector", lambda e, c=c: e.tensor_scalar(
            self.poscol[:, 2:3], self.poscol[:, 3:4], float(CAP1), None, ALU.is_ge))
        p.op("vector", lambda e, c=c: e.scalar_tensor_tensor(
            self.slot_f[:, c:c + 1], self.poscol[:, 2:3], 1e7, self.slot_f[:, c:c + 1],
            ALU.mult, ALU.add))
        p.op("vector", lambda e, c=c: e.tensor_copy(
            self.slot2_i[:, c:c + 1], self.slot_f[:, c:c + 1]), "sl")
        p.wait("vector", "cm")
        p.op("vector", lambda e: e.tensor_tensor(
            self.carry[:], self.carry[:], PS[0][0:1, 0:8], ALU.add), "cw")
    if self.emit_counts and not full:
        p.op("vector", lambda e: e.tensor_sub(self.cnt_s[:], self.carry[:], self.offrow[:]),
             "cnt")
        p.wait("sync", "cnt")
        p.dma("sync", lambda e: e.dma_start(out=self.outs["counts"], in_=self.cnt_s[:]), "cst")
        p.wait("sync", "cst")
        return
    p.dma("sync", lambda e: e.dma_start(out=self.idx_d, in_=self.c_senti.ap()), "sn")
    p.wait("gpsimd", "sn")
    p.wait("gpsimd", "sl")
    for c in range(NCH):
        p.dma("gpsimd", lambda e, c=c: e.indirect_dma_start(
            out=self.idx_d, out_offset=IndirectOffsetOnAxis(self.slot2_i[:, c:c + 1], 0),
            in_=self.tokid_s[:, c:c + 1], in_offset=None,
            bounds_check=self.bregs['slot'], oob_is_err=False), "sc")
    p.wait("sync", "sc")
    p.dma("sync", lambda e: e.dma_start(
        out=self.idxc64[:].rearrange("p (a n) -> p a n", n=1),
        in_=self.idx_d.rearrange("(p a) n -> p a n", p=128)), "ix")
    p.wait("gpsimd", "ix")
    p.wait("tensor", "cw")
    p.op("vector", lambda e: e.memset(self.xtmb[:], 0.0), "xz")
    p.wait("gpsimd", "xz")
    wl_marks = []

    def wload(eng, tag, mk):
        L = len(wl_marks)
        wb = (self.wbuf[:], self.wbufB)[L % 2]
        if L >= 2:
            s_, v_ = wl_marks[L - 2][1]
            p.wait(eng, s_, v_)
        p.dma(eng, mk(wb), f"{tag}{L % 2}")
        wl_marks.append([wb, None])
        return wb, f"{tag}{L % 2}"

    for ei in range(NE):
        for j in range(KCAP):
            p.wait("gpsimd", f"gst{j % 2}", None)
            p.op("gpsimd", lambda e, ei=ei, j=j: e.tensor_copy(
                self.idx1[:], self.idxc64[:, ei * KCAP + j:ei * KCAP + j + 1]), "ixc")
            p.dma("gpsimd", lambda e, ei=ei, j=j: e.indirect_dma_start(
                out=self.xtmb[:, j % 2, :], out_offset=None,
                in_=self.xhat_d,
                in_offset=IndirectOffsetOnAxis(self.idx1[:], 0),
                bounds_check=self.bregs['tok'], oob_is_err=False), "gth")
            p.wait("sync", "gth")
            p.dma("sync", lambda e, j=j: e.dma_start(
                out=self.ddisp_d[j * 128:(j + 1) * 128, :],
                in_=self.xtmb[:, j % 2, :]), f"gst{j % 2}")
        p.wait("sync", "gst0")
        p.wait("sync", "gst1")
        p.wait("sync", "w1m")
        for k in range(KC):
            p.dma("sync", lambda e, k=k: e.dma_start(
                out=self.v_dfm[:, k, :],
                in_=self.ddisp_d[:, k * 128:(k + 1) * 128], transpose=True), "dt")
        p.wait("sync", "he")
        p.dma("sync", lambda e, ei=ei: e.dma_start(
            out=self.b1c[:], in_=self.ap["l1f_b1T"][ei]), "dt")
        p.wait("tensor", "dt")
        if ei >= 1:
            p.wait("tensor", "ee", None)
        for fs in range(8):
            wb, wsem = wload("sync", "w1l", lambda wb, ei=ei, fs=fs:
                             (lambda e: e.dma_start(
                                 out=wb,
                                 in_=self.w1_bf[ei, :, fs * 512:(fs + 1) * 512].rearrange(
                                     "(k p) n -> p k n", p=128))))
            p.wait("tensor", wsem)
            for fc in range(4):
                slot = fs * 4 + fc
                bank = PS[slot % 2]
                if ei * 32 + slot >= 2:
                    p.wait("tensor", "he", bhe + ei * 32 + slot - 1)
                for k in range(KC):
                    self.mm(bank[:, :], wb[:, k, fc * 128:(fc + 1) * 128],
                            self.v_dfm[:, k, :],
                            start=(k == 0), stop=(k == KC - 1),
                            sem=("w1m" if k == KC - 1 else None))
                p.wait("scalar", "w1m")
                p.op("scalar", lambda e, slot=slot, bank=bank: e.activation(
                    self.v_hfm[:, slot, :], bank[:, :], AF.Relu,
                    bias=self.b1c[:, slot:slot + 1]), "he")
            wl_marks[-1][1] = ("w1m", p.base("w1m"))
        p.wait("tensor", "he")
        for mh in range(2):
            p.wait("sync", "em")
            p.dma("sync", lambda e, ei=ei, mh=mh: e.dma_start(
                out=self.b2sm[:], in_=self.b2_bf[ei:ei + 1, mh * 512:(mh + 1) * 512]),
                "b2l")
            for kg in range(4):
                wb, wsem = wload("gpsimd", "w2l", lambda wb, ei=ei, mh=mh, kg=kg:
                                 (lambda e: e.dma_start(
                                     out=wb,
                                     in_=self.w2_bf[ei, kg * 1024:(kg + 1) * 1024,
                                                    mh * 512:(mh + 1) * 512].rearrange(
                                         "(k p) n -> p k n", p=128))))
                p.wait("tensor", wsem)
                if kg == 0:
                    p.wait("tensor", "ee", None)
                    p.wait("tensor", "b2l")
                for tc in range(4):
                    for k in range(KC):
                        self.mm(PS[tc][:, :],
                                self.v_hfm[:, kg * 8 + k, tc * 128:(tc + 1) * 128],
                                wb[:, k, :],
                                start=(kg == 0 and k == 0), stop=False,
                                sem=("w2m" if k == KC - 1 else None))
                if kg == 3:
                    for tc in range(4):
                        self.mm(PS[tc][:, :], self.onesr_b[:], self.b2sm[:, 0:512],
                                start=False, stop=True, sem="em")
                wl_marks[-1][1] = ("w2m", p.base("w2m"))
            p.wait("scalar", "em")
            for tc in range(4):
                p.op("scalar", lambda e, tc=tc: e.activation(
                    self.v_eob[:, tc % 2, :], PS[tc][:, :], AF.Identity), "ee")
                p.wait("sync", "ee")
                p.dma("sync", lambda e, ei=ei, mh=mh, tc=tc: e.dma_start(
                    out=self.eo_d[ei * CAP1 + tc * 128:ei * CAP1 + (tc + 1) * 128,
                                  mh * 512:(mh + 1) * 512],
                    in_=self.v_eob[:, tc % 2, :]), f"eos{tc % 2}")
                p.wait("scalar", f"eos{tc % 2}")
    p.wait("gpsimd", "eos0")
    p.wait("gpsimd", "eos1")
    bcb2 = p.base("cb")
    for c in range(NCH):
        p.op("vector", lambda e, c=c: e.memset(self.v_gath[:, c % 2, :], 0.0), "gz")
        p.wait("gpsimd", "gz")
        if c >= 2:
            p.wait("gpsimd", "cb", bcb + c - 1)
        p.dma("gpsimd", lambda e, c=c: e.indirect_dma_start(
            out=self.v_gath[:, c % 2, :], out_offset=None,
            in_=self.eo_d, in_offset=IndirectOffsetOnAxis(self.slot_i[:, c:c + 1], 0),
            bounds_check=self.bregs['slot'], oob_is_err=False), "gb")
        p.wait("vector", "gb")
        for s in range(2):
            p.op("vector", lambda e, c=c, s=s: e.scalar_tensor_tensor(
                self.XT[:, c, s * 512:(s + 1) * 512],
                self.v_gath[:, c % 2, s * 512:(s + 1) * 512],
                self.gsc[:, c:c + 1],
                self.XT[:, c, s * 512:(s + 1) * 512],
                ALU.mult, ALU.add), "cb" if s == 1 else None)
        if final:
            p.wait("sync", "cb", bcb2 + c + 1)
            p.dma("sync", lambda e, c=c: e.dma_start(
                out=self.outs["xout"][c * 128:(c + 1) * 128, :],
                in_=self.XT[:, c, :]), "fst")
    if final:
        self._stores_done = True
    p.wait("vector", "cb")
    p.barrier(name="bf")


def scr8b_oh(self):
    if not hasattr(self, "_oh_b"):
        self._oh_b = self.ctx.enter_context(self.nc.sbuf_tensor("oh_b", [128, NE], BF16))
    return self._oh_b[:]


MK.emit_l1ffn = emit_l1ffn
MK.scr8b_oh = scr8b_oh


_PROGRAM_CACHE = {}


def _get_program():
    if "nc" not in _PROGRAM_CACHE:
        mk = MK(["l0ffn_a", "l0attn", "l0ffn_b", "l1ffn_a", "l1attn", "l1ffn_b"])
        nc = mk.emit_all()
        _PROGRAM_CACHE["nc"] = nc
        _PROGRAM_CACHE["mk"] = mk
    return _PROGRAM_CACHE["nc"], _PROGRAM_CACHE["mk"]


def kernel(**inputs):
    from concourse.bass_utils import run_bass_kernel_spmd
    nc, mk = _get_program()
    zpri = np.zeros((4, NE), np.float32)
    zsel = np.zeros((4, 1), np.float32)
    in_maps = []
    for c in range(8):
        m = host_inputs(inputs, c // 2)
        m["priors"] = zpri
        m["selprev"] = zsel
        m = {k: v for k, v in m.items() if k in mk.ap}
        in_maps.append(m)
    res = run_bass_kernel_spmd(nc, in_maps, core_ids=list(range(8)))
    out = np.empty((4, 4096, 1024), dtype=np.float32)
    for c in range(8):
        h = c % 2
        out[c // 2, h * 2048:(h + 1) * 2048, :] = res.results[c]["xout"][h * 2048:(h + 1) * 2048, :]
    return out
